# version 14
# baseline (speedup 1.0000x reference)
import math
from contextlib import ExitStack

import numpy as np
import ml_dtypes
import concourse.bass as bass
import concourse.mybir as mybir
from concourse.bass_utils import run_bass_kernel_spmd

F32 = mybir.dt.float32
BF16 = mybir.dt.bfloat16
AF = mybir.ActivationFunctionType
ALU = mybir.AluOpType
AX = mybir.AxisListType

S = 2048
D = 2048
H = 16
DH = 128
NT = S // 128
ND = D // 128
HFF = 8192
NHC = 2 * HFF // 128
SCALE = 1.0 / math.sqrt(DH)
EPS_LN = 1e-5
EPS_RMS = float(np.finfo(np.float32).eps)


class _Pool:
    def __init__(self, h):
        self.h = h
        self.n = 0


class Sem:

    def __init__(self, pool):
        self.p = pool
        self.base = pool.n

    @property
    def h(self):
        return self.p.h

    @property
    def n(self):
        return self.p.n

    def inc(self, k=1):
        self.p.n += k
        return self.p.n

    def at(self, v):
        return self.base + v


def build_program(phases="0qabc"):
    nc = bass.Bass("TRN2", target_bir_lowering=False)

    x_in = nc.dram_tensor("x", [S, D], F32, kind="ExternalInput").ap()
    wqp = nc.dram_tensor("wqp", [D, D], F32, kind="ExternalInput").ap()
    wkp = nc.dram_tensor("wkp", [D, D], F32, kind="ExternalInput").ap()
    wvp = nc.dram_tensor("wvp", [D, D], F32, kind="ExternalInput").ap()
    wo_in = nc.dram_tensor("wo", [D, D], F32, kind="ExternalInput").ap()
    wgu_in = nc.dram_tensor("wgu", [2 * HFF, D], F32, kind="ExternalInput").ap()
    wpr_in = nc.dram_tensor("wpr", [D, HFF], F32, kind="ExternalInput").ap()
    bq_in = nc.dram_tensor("bq", [128, D], F32, kind="ExternalInput").ap()
    bk_in = nc.dram_tensor("bk", [128, D], F32, kind="ExternalInput").ap()
    bvt_in = nc.dram_tensor("bvt", [128, D], F32, kind="ExternalInput").ap()
    bot_in = nc.dram_tensor("bot", [128, D], F32, kind="ExternalInput").ap()
    tab1_in = nc.dram_tensor("tab1", [128, NT * 64], F32, kind="ExternalInput").ap()
    tab2_in = nc.dram_tensor("tab2", [128, NT * 64], F32, kind="ExternalInput").ap()
    idf_in = nc.dram_tensor("identf", [128, 128], F32, kind="ExternalInput").ap()
    idb_in = nc.dram_tensor("identb", [128, 128], BF16, kind="ExternalInput").ap()
    dmask_in = nc.dram_tensor("dmask", [128, 640], F32, kind="ExternalInput").ap()
    ones_in = nc.dram_tensor("onesb", [128, 1], BF16, kind="ExternalInput").ap()

    out = nc.dram_tensor("out", [S, D], F32, kind="ExternalOutput").ap()

    ln1T = nc.dram_tensor("ln1T", [NT, D, 128], F32, kind="Internal")
    zT = nc.dram_tensor("zT", [D, S], F32, kind="Internal")
    x1S = nc.dram_tensor("x1S", [S, D], F32, kind="Internal")
    ln2Th = nc.dram_tensor("ln2Th", [NT, D, 128], BF16, kind="Internal")
    ln2Tl = nc.dram_tensor("ln2Tl", [NT, D, 128], BF16, kind="Internal")
    wgT = nc.dram_tensor("wgT", [NHC, 128, ND, 128], BF16, kind="Internal")
    wpT = nc.dram_tensor("wpT", [ND, 128, 64, 128], BF16, kind="Internal")

    with ExitStack() as top:
        tsb = lambda name, shp, dt: top.enter_context(nc.sbuf_tensor(name, shp, dt))

        idf = tsb("idf", [128, 128], F32)
        idb = tsb("idb", [128, 128], BF16)
        onesb = tsb("onesc", [128, 1], BF16)
        stat = tsb("stat", [128, 24], F32)

        pb = [top.enter_context(nc.psum_tensor(f"pb{i}", [128, 512], F32)) for i in range(6)]

        pools = {}

        def sem(name, pool=None):
            key = pool or name
            if key not in pools:
                pools[key] = _Pool(top.enter_context(nc.semaphore(key)))
            return Sem(pools[key])

        BAR = sem("bar")
        block = top.enter_context(nc.Block())

        def barrier():
            tgt = BAR.n + 5
            for name in ("sync", "vector", "scalar", "tensor", "gpsimd"):
                @getattr(block, name)
                def _(eng, name=name):
                    if name in ("vector", "scalar", "tensor"):
                        eng.drain().then_inc(BAR.h, 1)
                    else:
                        eng.nop().then_inc(BAR.h, 1)
                    eng.wait_ge(BAR.h, tgt)
            BAR.inc(5)

        ldc = sem("ldc")

        @block.sync
        def _(sync):
            for dst, src in ((idf, idf_in), (idb, idb_in), (onesb, ones_in)):
                sync.dma_start(dst[:], src).then_inc(ldc.h, 16)
                ldc.inc(16)
            sync.wait_ge(ldc.h, ldc.n)

        @block.vector
        def _(vector):
            vector.memset(stat[:, 8:9], EPS_LN)
        barrier()

        with ExitStack() as ph:
            psb = lambda name, shp, dt: ph.enter_context(nc.sbuf_tensor(name, shp, dt))
            xt = [psb(f"p0x{i}", [128, D], F32) for i in range(2)]
            lnst = [psb(f"p0st{i}", [128, ND, 128], F32) for i in range(2)]
            tmp1 = psb("p0t1", [128, D], F32)
            tmp2 = psb("p0t2", [128, D], F32)
            sqb = psb("p0sq", [128, D], F32)

            ldx = [sem("p0ldx0"), sem("p0ldx1")]
            v1 = sem("p0v1")
            s1 = sem("p0s1")
            v2 = sem("p0v2")
            t1 = sem("p0t1s")
            se = sem("p0se")
            stl = [sem("p0stl0"), sem("p0stl1")]

            @block.sync
            def _(sync):
                for tt in range(NT):
                    par = tt % 2
                    if tt >= 2:
                        sync.wait_ge(v1.h, tt - 1)
                    sync.dma_start(xt[par][:], x_in[tt * 128:(tt + 1) * 128, :]).then_inc(ldx[par].h, 16)
                    ldx[par].inc(16)

            @block.vector
            def _(vector):
                for tt in range(NT):
                    par = tt % 2
                    vector.wait_ge(ldx[par].h, 16 * (tt // 2 + 1))
                    vector.tensor_reduce(stat[:, 0:1], xt[par][:], axis=AX.X, op=ALU.add)
                    vector.drain()
                    vector.tensor_scalar(stat[:, 1:2], stat[:, 0:1], 1.0 / D, None, op0=ALU.mult)
                    vector.drain()
                    vector.tensor_scalar(tmp1[:], xt[par][:], stat[:, 1:2], None, op0=ALU.subtract)
                    vector.tensor_tensor(sqb[:], tmp1[:], tmp1[:], op=ALU.mult)
                    vector.tensor_reduce(stat[:, 2:3], sqb[:], axis=AX.X, op=ALU.add)
                    vector.drain()
                    vector.drain().then_inc(v1.h, 1)
                    v1.inc()
                    vector.wait_ge(s1.h, tt + 1)
                    vector.reciprocal(stat[:, 4:5], stat[:, 3:4])
                    vector.drain()
                    if tt >= 1:
                        vector.wait_ge(t1.h, 4 * tt)
                    vector.tensor_scalar(tmp2[:], tmp1[:], stat[:, 4:5], None, op0=ALU.mult).then_inc(v2.h, 1)
                    v2.inc()

            @block.scalar
            def _(scalar):
                for tt in range(NT):
                    scalar.wait_ge(v1.h, tt + 1)
                    scalar.activation(stat[:, 3:4], stat[:, 2:3], AF.Sqrt,
                                      bias=stat[:, 8:9], scale=1.0 / D).then_inc(s1.h, 1)
                    s1.inc()
                    par = tt % 2
                    for g in range(4):
                        scalar.wait_ge(t1.h, 4 * tt + g + 1)
                        if g == 0:
                            scalar.wait_ge(stl[par].h, 16 * (tt // 2))
                        scalar.copy(lnst[par][:, 4 * g:4 * (g + 1), :], pb[g % 2][:]).then_inc(se.h, 1)
                        se.inc()

            @block.tensor
            def _(tensor):
                for tt in range(NT):
                    tensor.wait_ge(v2.h, tt + 1)
                    for g in range(4):
                        if 4 * tt + g >= 2:
                            tensor.wait_ge(se.h, 4 * tt + g - 1)
                        for j in range(4):
                            dc = 4 * g + j
                            mm = tensor.transpose(pb[g % 2][:, j * 128:(j + 1) * 128],
                                                  tmp2[:, dc * 128:(dc + 1) * 128], idf[:])
                        mm.then_inc(t1.h, 1)
                        t1.inc()

            @block.gpsimd
            def _(gpsimd):
                for tt in range(NT):
                    par = tt % 2
                    gpsimd.wait_ge(se.h, 4 * (tt + 1))
                    dst = bass.AP(ln1T, tt * D * 128, [[128, 128], [128 * 128, ND], [1, 128]])
                    gpsimd.dma_start(dst, lnst[par][:]).then_inc(stl[par].h, 16)
                    stl[par].inc(16)
                gpsimd.wait_ge(stl[0].h, stl[0].n)
                gpsimd.wait_ge(stl[1].h, stl[1].n)
            barrier()

        with ExitStack() as pa:
            asb = lambda name, shp, dt: pa.enter_context(nc.sbuf_tensor(name, shp, dt))
            Ksb = asb("Ksb", [128, 4, S], F32)
            Qsb = asb("Qsb", [128, 4, S], F32)
            Vsb = asb("Vsb", [128, NT, 512], F32)
            costab = asb("costab", [128, NT, 64], F32)
            sintab = asb("sintab", [128, NT, 64], F32)
            dmaskp = asb("dmaskp", [128, 640], F32)

            qwb4 = [asb(f"qwb{i}", [128, D], BF16) for i in range(3)]
            qt_ = asb("qt", [128, D], BF16)
            qg1_2 = [asb(f"qg1{i}", [128, D], BF16) for i in range(4)]
            qg2 = asb("qg2", [128, D], BF16)
            qsc = asb("qsc", [128, 32], BF16)
            qrec = asb("qrec", [128, 32], F32)
            qstg = asb("qstg", [128, ND, 128], BF16)
            QL = [sem(f"ql{i}", f"QL{i}") for i in range(3)]
            Qv = sem("qv", "Qv")
            Qt = sem("qt_s", "Qt")
            Qs = sem("qs", "Qs")
            Qst = sem("qst", "Qst")

            def unit_aps(u):
                if u < NHC:
                    srca = bass.AP(wgu_in.tensor, u * 128 * D, [[D, 128], [1, D]])
                    dsta = bass.AP(wgT, u * 128 * ND * 128, [[ND * 128, 128], [128, ND], [1, 128]])
                else:
                    v = u - NHC
                    dd, h4 = divmod(v, 4)
                    srca = bass.AP(wpr_in.tensor, dd * 128 * HFF + h4 * 2048, [[HFF, 128], [1, 2048]])
                    dsta = bass.AP(wpT, dd * 128 * 64 * 128 + h4 * 16 * 128,
                                   [[64 * 128, 128], [128, 16], [1, 128]])
                return srca, dsta

            def q_load(gpsimd, v):
                if v >= 192:
                    return
                if v >= 3:
                    gpsimd.wait_ge(Qv.h, Qv.at(v - 2))
                srca, _ = unit_aps(v)
                gpsimd.dma_start(qwb4[v % 3][:], srca).then_inc(QL[v % 3].h, 16)
                QL[v % 3].inc(16)

            def q_vector(vector, u):
                par = u % 3
                qwb = qwb4[par]
                qg1 = qg1_2[u % 4]
                vector.wait_ge(QL[par].h, QL[par].at(16 * (u // 3 + 1)))
                if u >= 4:
                    vector.wait_ge(Qt.h, Qt.at(2 * (u - 3)))
                wb3 = qwb[:].rearrange("p (g k) -> p g k", k=64)
                vector.tensor_reduce(qrec[:, 0:32], wb3, axis=AX.X, op=ALU.add,
                                     apply_absolute_value=True)
                vector.drain()
                vector.tensor_scalar(qsc[:, 0:32], qrec[:, 0:32], 1.0 / 64, 1e-8,
                                     op0=ALU.mult, op1=ALU.max)
                vector.drain()
                vector.reciprocal(qrec[:, 0:32], qsc[:, 0:32])
                vector.drain()
                rec3 = bass.AP(qrec, 0, [[qrec[:].ap[0][0], 128], [1, 32], [0, 64]])
                sc3 = bass.AP(qsc, 0, [[qsc[:].ap[0][0], 128], [1, 32], [0, 64]])
                t3 = qt_[:].rearrange("p (g k) -> p g k", k=64)
                vector.tensor_tensor(t3, wb3, rec3, op=ALU.mult)
                vector.tensor_scalar(qg2[:], qt_[:], -0.5, None, op0=ALU.is_lt)
                vector.scalar_tensor_tensor(qg1[:], qt_[:], 0.5, qg2[:],
                                            op0=ALU.is_gt, op1=ALU.subtract)
                g13 = qg1[:].rearrange("p (g k) -> p g k", k=64)
                qt3 = qt_[:].rearrange("p (g k) -> p g k", k=64)
                vector.tensor_tensor(qt3, g13, sc3, op=ALU.mult)
                vector.tensor_tensor(qg2[:], qt_[:], qwb[:], op=ALU.subtract)
                vector.tensor_tensor(qg1[:], qwb[:], qg2[:], op=ALU.add).then_inc(Qv.h, 1)
                Qv.inc()

            def q_tensor(tensor, u, qpb):
                qg1 = qg1_2[u % 4]
                tensor.wait_ge(Qv.h, Qv.at(u + 1))
                for g in range(2):
                    if u >= 1:
                        tensor.wait_ge(Qs.h, Qs.at(2 * (u - 1) + g + 1))
                    for j in range(8):
                        blk = 8 * g + j
                        mm = tensor.transpose(qpb[g][:, j * 128:(j + 1) * 128],
                                              qg1[:, blk * 128:(blk + 1) * 128], idb[:])
                    mm.then_inc(Qt.h, 1)
                    Qt.inc()

            def q_vecevac(vector, u, qpb):
                for g in range(2):
                    vector.wait_ge(Qt.h, Qt.at(2 * u + g + 1))
                    if g == 0:
                        vector.wait_ge(Qst.h, Qst.at(16 * u))
                    vector.tensor_copy(qstg[:, 8 * g:8 * (g + 1), :],
                                       qpb[g][:]).then_inc(Qs.h, 1)
                    Qs.inc()

            def q_store(gpsimd, u):
                gpsimd.wait_ge(Qs.h, Qs.at(2 * (u + 1)))
                _, dsta = unit_aps(u)
                gpsimd.dma_start(dsta, qstg[:]).then_inc(Qst.h, 16)
                Qst.inc(16)

            qctr = {"vslot": 0, "t": 0, "s": 0, "ld": 3, "st": 0}

            QLAG = 6

            def q_vector_slot(vector, qpb):
                u = qctr["vslot"] - QLAG
                qctr["vslot"] += 1
                if 0 <= u < 192:
                    q_vector(vector, u)
                q_vecevac_upto(vector, qctr["vslot"] - 7, qpb)

            def q_tensor_upto(tensor, target, qpb):
                while qctr["t"] < min(target, 192):
                    q_tensor(tensor, qctr["t"], qpb)
                    qctr["t"] += 1

            def q_vecevac_upto(vector, target, qpb):
                while qctr["s"] < min(target, 192):
                    q_vecevac(vector, qctr["s"], qpb)
                    qctr["s"] += 1

            def q_gp_pass(gpsimd, ld_target, st_target):
                ld_target = min(ld_target, 192)
                st_target = min(st_target, 192)
                while qctr["ld"] < ld_target or qctr["st"] < st_target:
                    if qctr["ld"] < ld_target:
                        q_load(gpsimd, qctr["ld"])
                        qctr["ld"] += 1
                    if qctr["st"] < st_target:
                        q_store(gpsimd, qctr["st"])
                        qctr["st"] += 1

            lda1 = sem("palda1")

            @block.sync
            def _(sync):
                for dst, srcx in ((costab, tab1_in), (sintab, tab2_in), (dmaskp, dmask_in)):
                    sync.dma_start(dst[:], srcx).then_inc(lda1.h, 16)
                    lda1.inc(16)
                sync.wait_ge(lda1.h, lda1.n)

            @block.gpsimd
            def _(gpsimd):
                for v in range(3):
                    q_load(gpsimd, v)
            barrier()

            aparts = 3 if "a" in phases else (2 if "Y" in phases else (1 if "X" in phases else 0))
            for q4 in (range(4) if aparts else []):
                tagq = f"a{q4}"
                def emit_rope_proj(tag, w_src, b_src, dst_sb, with_scale, pi):
                    with ExitStack() as pp:
                        psb = lambda name, shp, dt: pp.enter_context(nc.sbuf_tensor(name, shp, dt))
                        qpb = [pp.enter_context(nc.psum_tensor(f"{tagq}{tag}qpb{i}", [128, 1024], BF16))
                               for i in range(2)]
                        wa = psb(tagq + tag + "wa", [128, ND, 512], F32)
                        bt = psb(tagq + tag + "bt", [128, 512], F32)
                        ttg = psb(tagq + tag + "ttg", [128, 8, 512], F32)
                        rA = Vsb[:, 0, 0:256]
                        rB = Vsb[:, 0, 256:512]
                        rC = Vsb[:, 1, 0:256]
                        rD = Vsb[:, 1, 256:512]
                        vpitch = Vsb[:].ap[0][0]
                        lhsb2 = [bass.AP(Vsb, (2 + 4 * i) * 512, [[vpitch, 128], [128, ND], [1, 128]])
                                 for i in range(2)]


                        ldw = sem(tagq + tag + "ldw", "Pldw")
                        ldr = [sem(tagq + tag + "ldr0", "Pldr0"), sem(tagq + tag + "ldr1", "Pldr1")]
                        ta = sem(tagq + tag + "ta", "Pta")
                        veA = sem(tagq + tag + "veA", "PveA")
                        tf = sem(tagq + tag + "tf", "Ptf")
                        se = sem(tagq + tag + "se", "Pse")

                        @block.sync
                        def _(sync, q4=q4):
                            src_w = bass.AP(w_src.tensor, q4 * 512, [[D, 128], [128 * D, ND], [1, 512]])
                            sync.dma_start(wa[:], src_w).then_inc(ldw.h, 16)
                            ldw.inc(16)
                            sync.dma_start(bt[:], b_src[:, q4 * 512:(q4 + 1) * 512]).then_inc(ldw.h, 16)
                            ldw.inc(16)
                            for tt in range(NT):
                                par = tt % 2
                                if tt >= 2:
                                    sync.wait_ge(ta.h, ta.at(tt - 1))
                                srcl = bass.AP(ln1T, tt * D * 128, [[128, 128], [128 * 128, ND], [1, 128]])
                                sync.dma_start(lhsb2[par], srcl).then_inc(ldr[par].h, 16)
                                ldr[par].inc(16)

                        @block.vector
                        def _(vector, q4=q4):
                            vector.wait_ge(ldw.h, ldw.at(32))
                            if with_scale:
                                vector.tensor_scalar(bt[:], bt[:], SCALE, None, op0=ALU.mult)
                            for tt in range(NT):
                                g, ti = divmod(tt, 4)
                                slot = (g % 2) * 4
                                vector.wait_ge(ta.h, ta.at(tt + 1))
                                if tt % 4 == 0 and tt >= 8:
                                    vector.wait_ge(tf.h, tf.at(4 * (g - 1)))
                                if with_scale:
                                    vector.scalar_tensor_tensor(ttg[:, slot + ti, :], pb[tt % 2][:], SCALE,
                                                                bt[:], op0=ALU.mult, op1=ALU.add)
                                else:
                                    vector.tensor_tensor(ttg[:, slot + ti, :], pb[tt % 2][:], bt[:], op=ALU.add)
                                if ti == 3:
                                    for ti2 in range(4):
                                        tt2 = 4 * g + ti2
                                        base = ttg[:, slot + ti2, :]
                                        pitch = ttg[:].ap[0][0]
                                        off = base.offset
                                        x1v = bass.AP(ttg, off, [[pitch, 128], [128, 4], [1, 64]])
                                        x2v = bass.AP(ttg, off + 64, [[pitch, 128], [128, 4], [1, 64]])
                                        cpitch = costab[:].ap[0][0]
                                        c3 = bass.AP(costab, tt2 * 64, [[cpitch, 128], [0, 4], [1, 64]])
                                        s3 = bass.AP(sintab, tt2 * 64, [[cpitch, 128], [0, 4], [1, 64]])
                                        a4 = rA.rearrange("p (h k) -> p h k", k=64)
                                        b4 = rB.rearrange("p (h k) -> p h k", k=64)
                                        c4 = rC.rearrange("p (h k) -> p h k", k=64)
                                        d4 = rD.rearrange("p (h k) -> p h k", k=64)
                                        vector.tensor_tensor(a4, x1v, c3, op=ALU.mult)
                                        vector.tensor_tensor(b4, x2v, s3, op=ALU.mult)
                                        vector.tensor_tensor(c4, x1v, s3, op=ALU.mult)
                                        vector.tensor_tensor(d4, x2v, c3, op=ALU.mult)
                                        vector.tensor_tensor(x1v, a4, b4, op=ALU.subtract)
                                        vector.tensor_tensor(x2v, c4, d4, op=ALU.add)
                                vector.drain().then_inc(veA.h, 1)
                                veA.inc()
                                q_vector_slot(vector, qpb)

                        @block.tensor
                        def _(tensor, q4=q4):
                            tensor.wait_ge(ldw.h, ldw.at(32))

                            def emit_proj_g(tensor, g):
                                for ti in range(4):
                                    tt = 4 * g + ti
                                    par = tt % 2
                                    tensor.wait_ge(ldr[par].h, ldr[par].at(16 * (tt // 2 + 1)))
                                    if tt >= 2:
                                        tensor.wait_ge(veA.h, veA.at(tt - 1))
                                    lb = lhsb2[par]
                                    for dc in range(ND):
                                        mm = tensor.matmul(pb[tt % 2][:],
                                                           bass.AP(Vsb, lb.offset + dc * 128, [[vpitch, 128], [1, 128]]),
                                                           wa[:, dc, :],
                                                           start=(dc == 0), stop=(dc == ND - 1))
                                    mm.then_inc(ta.h, 1)
                                    ta.inc()
                                    q_tensor_upto(tensor, 16 * pi + tt - 6, qpb)

                            def emit_tr_g(tensor, g):
                                slot = (g % 2) * 4
                                tensor.wait_ge(veA.h, veA.at(4 * (g + 1)))
                                for hh in range(4):
                                    fill = 4 * g + hh
                                    if fill >= 2:
                                        tensor.wait_ge(se.h, se.at(fill - 1))
                                    for ti in range(4):
                                        mm = tensor.transpose(pb[4 + hh % 2][:, ti * 128:(ti + 1) * 128],
                                                              ttg[:, slot + ti, hh * 128:(hh + 1) * 128], idf[:])
                                    mm.then_inc(tf.h, 1)
                                    tf.inc()

                            emit_proj_g(tensor, 0)
                            for g in range(4):
                                if g + 1 < 4:
                                    emit_proj_g(tensor, g + 1)
                                emit_tr_g(tensor, g)
                            q_tensor_upto(tensor, 16 * pi + 9, qpb)

                        @block.scalar
                        def _(scalar, q4=q4):
                            for g in range(4):
                                for hh in range(4):
                                    fill = 4 * g + hh
                                    scalar.wait_ge(tf.h, tf.at(fill + 1))
                                    scalar.copy(dst_sb[:, hh, 4 * g * 128:4 * g * 128 + 512],
                                                pb[4 + hh % 2][:]).then_inc(se.h, 1)
                                    se.inc()

                        @block.gpsimd
                        def _(gpsimd, q4=q4):
                            q_gp_pass(gpsimd, 16 * pi + 12, 16 * pi + 9)
                        barrier()

                emit_rope_proj("K", wkp, bk_in, Ksb, False, 3 * q4)
                emit_rope_proj("Q", wqp, bq_in, Qsb, True, 3 * q4 + 1)
                if aparts < 2:
                    continue

                with ExitStack() as pp:
                    psb = lambda name, shp, dt: pp.enter_context(nc.sbuf_tensor(name, shp, dt))
                    vpi = 3 * q4 + 2
                    vlast = (q4 == 3)
                    qpbv = [pp.enter_context(nc.psum_tensor(f"{tagq}vqpb{i}", [128, 1024], BF16))
                            for i in range(2)]
                    wva = psb(tagq + "wva", [128, ND, 512], F32)
                    bvq = psb(tagq + "bvq", [128, 512], F32)
                    lhs = [psb(f"{tagq}lhs{i}", [128, ND, 128], F32) for i in range(2)]

                    ldw2 = sem(tagq + "ldw2", "Aldw2")
                    ldl = [sem(tagq + "ldl0", "Aldl0"), sem(tagq + "ldl1", "Aldl1")]
                    t2 = sem(tagq + "t2", "At2")
                    ve2 = sem(tagq + "ve2", "Ave2")

                    @block.sync
                    def _(sync, q4=q4):
                        src_v = bass.AP(wvp.tensor, q4 * 512, [[D, 128], [128 * D, ND], [1, 512]])
                        sync.dma_start(wva[:], src_v).then_inc(ldw2.h, 16)
                        ldw2.inc(16)
                        sync.dma_start(bvq[:], bvt_in[:, q4 * 512:(q4 + 1) * 512]).then_inc(ldw2.h, 16)
                        ldw2.inc(16)
                        for tt in range(NT):
                            par = tt % 2
                            if tt >= 2:
                                sync.wait_ge(t2.h, t2.at(tt - 1))
                            src = bass.AP(ln1T, tt * D * 128, [[128, 128], [128 * 128, ND], [1, 128]])
                            sync.dma_start(lhs[par][:], src).then_inc(ldl[par].h, 16)
                            ldl[par].inc(16)

                    @block.tensor
                    def _(tensor, q4=q4):
                        tensor.wait_ge(ldw2.h, ldw2.at(32))
                        for tt in range(NT):
                            par = tt % 2
                            tensor.wait_ge(ldl[par].h, ldl[par].at(16 * (tt // 2 + 1)))
                            if tt >= 4:
                                tensor.wait_ge(ve2.h, ve2.at(tt - 3))
                            for dc in range(ND):
                                mm = tensor.matmul(pb[tt % 4][:], lhs[par][:, dc, :], wva[:, dc, :],
                                                   start=(dc == 0), stop=(dc == ND - 1))
                            mm.then_inc(t2.h, 1)
                            t2.inc()
                            q_tensor_upto(tensor, 16 * vpi + tt - 6, qpbv)

                    @block.vector
                    def _(vector, q4=q4):
                        for tt in range(NT):
                            vector.wait_ge(t2.h, t2.at(tt + 1))
                            vector.tensor_tensor(Vsb[:, tt, :], pb[tt % 4][:], bvq[:],
                                                 op=ALU.add).then_inc(ve2.h, 1)
                            ve2.inc()
                            q_vector_slot(vector, qpbv)
                        if vlast:
                            for _ in range(QLAG):
                                q_vector_slot(vector, qpbv)
                            q_vecevac_upto(vector, 192, qpbv)

                    @block.tensor
                    def _(tensor, q4=q4):
                        q_tensor_upto(tensor, 192 if vlast else 16 * vpi + 9, qpbv)

                    @block.gpsimd
                    def _(gpsimd, q4=q4):
                        q_gp_pass(gpsimd, 192 if vlast else 16 * vpi + 12,
                                  192 if vlast else 16 * vpi + 9)
                    barrier()

                with ExitStack() as pp:
                    psb = lambda name, shp, dt: pp.enter_context(nc.sbuf_tensor(name, shp, dt))
                    scb2 = [psb(tagq + f"scb{i}", [128, S], F32) for i in range(2)]
                    prb2 = [psb(tagq + f"prb{i}", [128, S], F32) for i in range(2)]
                    pTb2 = [psb(tagq + f"pTb{i}", [128, NT, 128], F32) for i in range(2)]
                    zqb = psb(tagq + "zqb", [128, NT, 128], F32)
                    px = [pp.enter_context(nc.psum_tensor(f"{tagq}px{i}", [128, 512], F32)) for i in range(2)]

                    ts1 = sem(tagq + "ts1", "Ats1")
                    se1 = sem(tagq + "se1", "Ase1")
                    vm = sem(tagq + "vm", "Avm")
                    sx = sem(tagq + "sx", "Asx")
                    tt1 = sem(tagq + "tt1", "Att1")
                    sp = sem(tagq + "sp", "Asp")
                    tz = sem(tagq + "tz", "Atz")
                    sz = sem(tagq + "sz", "Asz")
                    stz = sem(tagq + "stz", "Astz")

                    def nk(qt):
                        return (qt * 128 + 128 + 511) // 512

                    def ngr(qt):
                        return (qt + 1 + 7) // 8

                    def cum(hh, qt, per):
                        full = sum(per(q) for q in range(NT))
                        return hh * full + sum(per(q) for q in range(qt))

                    def prev_cs(hh, qt):
                        cq = hh * NT + qt
                        pq = cq - 1
                        return cum(pq // NT, pq % NT, nk)

                    def prev_cg(hh, qt):
                        cq = hh * NT + qt
                        pq = cq - 1
                        return cum(pq // NT, pq % NT, ngr)

                    SC = [(4, 5, 6, 7), (10, 11, 12, 13)]

                    @block.tensor
                    def _(tensor, q4=q4):
                        def emit_scores(tensor, hh, qt):
                            kv = (qt + 1) * 128
                            cs = cum(hh, qt, nk)
                            cq = hh * NT + qt
                            p = cq % 2
                            if cq >= 2:
                                tensor.wait_ge(se1.h, se1.at(prev_cs(hh, qt)))
                                pq2 = cq - 2
                                tensor.wait_ge(sp.h, sp.at(cum(pq2 // NT, pq2 % NT, ngr) + ngr(pq2 % NT)))
                            for kc in range(nk(qt)):
                                n = min(512, kv - kc * 512)
                                if kc >= 2:
                                    tensor.wait_ge(se1.h, se1.at(cs + kc - 1))
                                tensor.matmul(pb[2 * p + kc % 2][:, 0:n],
                                              Qsb[:, hh, qt * 128:(qt + 1) * 128],
                                              Ksb[:, hh, kc * 512:kc * 512 + n],
                                              start=True, stop=True).then_inc(ts1.h, 1)
                                ts1.inc()

                        def emit_tp_pv(tensor, hh, qt):
                            cg = cum(hh, qt, ngr)
                            cq = hh * NT + qt
                            p = cq % 2
                            scb = scb2[p]
                            pTb = pTb2[p]
                            tensor.wait_ge(vm.h, vm.at(3 * cq + 3))
                            if cq >= 2:
                                tensor.wait_ge(sp.h, sp.at(prev_cg(hh, qt)))
                            done, g = 0, 0
                            while done < qt + 1:
                                take = min(8, qt + 1 - done)
                                if g >= 1:
                                    tensor.wait_ge(sp.h, sp.at(cg + g))
                                for j in range(take):
                                    kb = done + j
                                    bank = px[p] if j < 4 else pb[2 * p + 1]
                                    mm = tensor.transpose(bank[:, (j % 4) * 128:(j % 4 + 1) * 128],
                                                          scb[:, kb * 128:(kb + 1) * 128], idf[:])
                                mm.then_inc(tt1.h, 1)
                                tt1.inc()
                                done += take
                                g += 1
                            tensor.wait_ge(sp.h, sp.at(cg + ngr(qt)))
                            if cq >= 2:
                                tensor.wait_ge(sz.h, sz.at(cq - 1))
                            for kb in range(qt + 1):
                                mm = tensor.matmul(pb[4 + p][:, 0:128],
                                                   Vsb[:, kb, hh * 128:(hh + 1) * 128],
                                                   pTb[:, kb, :], start=(kb == 0), stop=(kb == qt))
                            mm.then_inc(tz.h, 1)
                            tz.inc()

                        iters = [(hh, qt) for hh in range(4) for qt in range(NT)]
                        emit_scores(tensor, *iters[0])
                        for idx, (hh, qt) in enumerate(iters):
                            if idx + 1 < len(iters):
                                emit_scores(tensor, *iters[idx + 1])
                            emit_tp_pv(tensor, hh, qt)

                    @block.scalar
                    def _(scalar, q4=q4):
                        def emit_evacs(scalar, hh, qt):
                            kv = (qt + 1) * 128
                            cs = cum(hh, qt, nk)
                            cq = hh * NT + qt
                            p = cq % 2
                            scb = scb2[p]
                            if cq >= 2:
                                scalar.wait_ge(vm.h, vm.at(3 * (cq - 1)))
                            if cq >= 2:
                                pq2 = cq - 2
                                scalar.wait_ge(tt1.h, tt1.at(cum(pq2 // NT, pq2 % NT, ngr) + ngr(pq2 % NT)))
                            for kc in range(nk(qt) - 1):
                                n = 512
                                scalar.wait_ge(ts1.h, ts1.at(cs + kc + 1))
                                scalar.copy(scb[:, kc * 512:kc * 512 + n],
                                            pb[2 * p + kc % 2][:, 0:n]).then_inc(se1.h, 1)
                                se1.inc()

                        def emit_rest(scalar, hh, qt):
                            kv = (qt + 1) * 128
                            cg = cum(hh, qt, ngr)
                            cq = hh * NT + qt
                            p = cq % 2
                            scb = scb2[p]
                            prb = prb2[p]
                            pTb = pTb2[p]
                            mx, nmx, sm, rc = SC[p]
                            scalar.wait_ge(vm.h, vm.at(3 * cq + 1))
                            scalar.activation(prb[:, 0:kv], scb[:, 0:kv], AF.Exp,
                                              bias=stat[:, nmx:nmx + 1],
                                              accum_out=stat[:, sm:sm + 1]).then_inc(sx.h, 1)
                            sx.inc()
                            done, g = 0, 0
                            while done < qt + 1:
                                take = min(8, qt + 1 - done)
                                scalar.wait_ge(tt1.h, tt1.at(cg + g + 1))
                                if done == 0 and cq >= 2:
                                    scalar.wait_ge(tz.h, tz.at(cq - 1))
                                t1_ = min(take, 4)
                                scalar.copy(pTb[:, done:done + t1_, :],
                                            px[p][:, 0:t1_ * 128])
                                if take > 4:
                                    scalar.copy(pTb[:, done + 4:done + take, :],
                                                pb[2 * p + 1][:, 0:(take - 4) * 128])
                                scalar.drain().then_inc(sp.h, 1)
                                sp.inc()
                                done += take
                                g += 1
                            scalar.wait_ge(tz.h, tz.at(cq + 1))
                            if qt == 0:
                                scalar.wait_ge(stz.h, stz.at(16 * hh))
                            scalar.copy(zqb[:, qt, :], pb[4 + p][:, 0:128]).then_inc(sz.h, 1)
                            sz.inc()

                        iters = [(hh, qt) for hh in range(4) for qt in range(NT)]
                        emit_evacs(scalar, *iters[0])
                        for idx, (hh, qt) in enumerate(iters):
                            if idx + 1 < len(iters):
                                emit_evacs(scalar, *iters[idx + 1])
                            emit_rest(scalar, hh, qt)

                    @block.vector
                    def _(vector, q4=q4):
                        for hh in range(4):
                            for qt in range(NT):
                                kv = (qt + 1) * 128
                                cs = cum(hh, qt, nk)
                                cq = hh * NT + qt
                                p = cq % 2
                                scb = scb2[p]
                                prb = prb2[p]
                                mx, nmx, sm, rc = SC[p]
                                nkq = nk(qt)
                                lastn = kv - (nkq - 1) * 512
                                mxb = 16 + 4 * p
                                for kc in range(nkq - 1):
                                    vector.wait_ge(se1.h, se1.at(cs + kc + 1))
                                    vector.tensor_reduce(stat[:, mxb + kc:mxb + kc + 1],
                                                         scb[:, kc * 512:(kc + 1) * 512],
                                                         axis=AX.X, op=ALU.max)
                                vector.wait_ge(ts1.h, ts1.at(cs + nkq))
                                if cq >= 2:
                                    pq2 = cq - 2
                                    vector.wait_ge(tt1.h, tt1.at(cum(pq2 // NT, pq2 % NT, ngr) + ngr(pq2 % NT)))
                                vector.tensor_tensor(scb[:, (nkq - 1) * 512:kv],
                                                     pb[2 * p + (nkq - 1) % 2][:, 0:lastn],
                                                     dmaskp[:, 640 - lastn:640], op=ALU.add).then_inc(se1.h, 1)
                                se1.inc()
                                if nkq == 1:
                                    vector.tensor_reduce(stat[:, nmx:nmx + 1], scb[:, 0:kv], axis=AX.X,
                                                         op=ALU.max, negate=True)
                                else:
                                    vector.tensor_reduce(stat[:, mxb + nkq - 1:mxb + nkq],
                                                         scb[:, (nkq - 1) * 512:kv],
                                                         axis=AX.X, op=ALU.max)
                                    vector.drain()
                                    vector.tensor_reduce(stat[:, nmx:nmx + 1], stat[:, mxb:mxb + nkq],
                                                         axis=AX.X, op=ALU.max, negate=True)
                                vector.drain()
                                vector.drain().then_inc(vm.h, 1)
                                vm.inc()
                                vector.wait_ge(sx.h, sx.at(cq + 1))
                                vector.reciprocal(stat[:, rc:rc + 1], stat[:, sm:sm + 1])
                                vector.drain()
                                vector.drain().then_inc(vm.h, 1)
                                vm.inc()
                                vector.tensor_scalar(scb[:, 0:kv], prb[:, 0:kv], stat[:, rc:rc + 1],
                                                     None, op0=ALU.mult).then_inc(vm.h, 1)
                                vm.inc()

                    @block.gpsimd
                    def _(gpsimd, q4=q4):
                        for hh in range(4):
                            gpsimd.wait_ge(sz.h, sz.at(NT * (hh + 1)))
                            h = q4 * 4 + hh
                            dst = bass.AP(zT, h * 128 * S, [[S, 128], [1, S]])
                            gpsimd.dma_start(dst, zqb[:].rearrange("p a b -> p (a b)")).then_inc(stz.h, 16)
                            stz.inc(16)
                        gpsimd.wait_ge(stz.h, stz.n)
                    barrier()

        if "b" in phases:
            with ExitStack() as ph:
                psb = lambda name, shp, dt: ph.enter_context(nc.sbuf_tensor(name, shp, dt))
                wos = psb("wos", [128, ND, D], F32)
                botc = psb("botc", [128, D], F32)
                zc = [psb(f"zc{i}", [128, ND, 128], F32) for i in range(2)]
                xtb = psb("pbx0", [128, D], F32)
                att = psb("att", [128, D], F32)
                x1b = psb("x1b", [128, D], F32)
                tmpb = psb("pbt1", [128, D], F32)
                sqb2 = psb("pbsq", [128, 1024], F32)
                lnh = psb("lnh", [128, ND, 128], BF16)
                lnl = psb("lnl", [128, ND, 128], BF16)

                ldo = sem("pbldo")
                ldz = [sem("pbldz0"), sem("pbldz1")]
                ldx2 = sem("pbldx0")
                tw = sem("pbtw")
                vev = sem("pbvev")
                vs1 = sem("pbvs1")
                ss1 = sem("pbss1")
                vx1 = sem("pbvx1")
                vln = sem("pbvln")
                tt2 = sem("pbtt2")
                sh = sem("pbsh")
                vl = sem("pbvl")
                stx = sem("pbstx")
                stn = sem("pbstn")

                @block.sync
                def _(sync):
                    src_o = bass.AP(wo_in.tensor, 0, [[D, 128], [128 * D, ND], [1, D]])
                    sync.dma_start(wos[:], src_o).then_inc(ldo.h, 16)
                    ldo.inc(16)
                    sync.dma_start(botc[:], bot_in).then_inc(ldo.h, 16)
                    ldo.inc(16)
                    for qt in range(NT):
                        par = qt % 2
                        if qt >= 2:
                            sync.wait_ge(tw.h, 4 * (qt - 1))
                        if qt >= 1:
                            sync.wait_ge(vx1.h, qt)
                        src = bass.AP(zT, qt * 128, [[S, 128], [128 * S, ND], [1, 128]])
                        sync.dma_start(zc[par][:], src).then_inc(ldz[par].h, 16)
                        ldz[par].inc(16)
                        sync.dma_start(xtb[:], x_in[qt * 128:(qt + 1) * 128, :]).then_inc(ldx2.h, 16)
                        ldx2.inc(16)

                @block.tensor
                def _(tensor):
                    tensor.wait_ge(ldo.h, 32)

                    def emit_wo(tensor, qt):
                        par = qt % 2
                        tensor.wait_ge(ldz[par].h, 16 * (qt // 2 + 1))
                        for dcol in range(4):
                            if qt >= 1:
                                tensor.wait_ge(vev.h, 4 * (qt - 1) + dcol + 1)
                            for dc in range(ND):
                                mm = tensor.matmul(pb[dcol][:], zc[par][:, dc, :],
                                                   wos[:, dc, dcol * 512:(dcol + 1) * 512],
                                                   start=(dc == 0), stop=(dc == ND - 1))
                            mm.then_inc(tw.h, 1)
                            tw.inc()

                    def emit_ln2t(tensor, qt):
                        tensor.wait_ge(vln.h, qt + 1)
                        for g in range(4):
                            if 4 * qt + g >= 2:
                                tensor.wait_ge(vl.h, 4 * qt + g - 1)
                            for j in range(4):
                                dc = 4 * g + j
                                mm = tensor.transpose(pb[4 + g % 2][:, j * 128:(j + 1) * 128],
                                                      tmpb[:, dc * 128:(dc + 1) * 128], idf[:])
                            mm.then_inc(tt2.h, 1)
                            tt2.inc()

                    emit_wo(tensor, 0)
                    for qt in range(NT):
                        if qt + 1 < NT:
                            emit_wo(tensor, qt + 1)
                        emit_ln2t(tensor, qt)

                @block.vector
                def _(vector):
                    for qt in range(NT):
                        par = qt % 2
                        for dcol in range(4):
                            vector.wait_ge(tw.h, 4 * qt + dcol + 1)
                            vector.tensor_tensor(att[:, dcol * 512:(dcol + 1) * 512], pb[dcol][:],
                                                 botc[:, dcol * 512:(dcol + 1) * 512],
                                                 op=ALU.add).then_inc(vev.h, 1)
                            vev.inc()
                        if qt >= 1:
                            vector.wait_ge(tt2.h, 4 * qt)
                        vector.tensor_reduce(stat[:, 0:1], att[:], axis=AX.X, op=ALU.add)
                        vector.drain()
                        vector.tensor_scalar(stat[:, 1:2], stat[:, 0:1], 1.0 / D, None, op0=ALU.mult)
                        vector.drain()
                        vector.tensor_scalar(tmpb[:], att[:], stat[:, 1:2], None, op0=ALU.subtract)
                        vector.tensor_tensor(sqb2[:], tmpb[:, 0:1024], tmpb[:, 0:1024], op=ALU.mult)
                        vector.tensor_reduce(stat[:, 2:3], sqb2[:], axis=AX.X, op=ALU.add)
                        vector.tensor_tensor(sqb2[:], tmpb[:, 1024:2048], tmpb[:, 1024:2048], op=ALU.mult)
                        vector.tensor_reduce(stat[:, 10:11], sqb2[:], axis=AX.X, op=ALU.add)
                        vector.drain()
                        vector.tensor_tensor(stat[:, 2:3], stat[:, 2:3], stat[:, 10:11], op=ALU.add)
                        vector.drain()
                        vector.drain().then_inc(vs1.h, 1)
                        vs1.inc()
                        vector.wait_ge(ss1.h, 2 * qt + 1)
                        vector.reciprocal(stat[:, 4:5], stat[:, 3:4])
                        vector.drain()
                        vector.tensor_scalar(att[:], tmpb[:], stat[:, 4:5], None, op0=ALU.mult)
                        vector.wait_ge(ldx2.h, 16 * (qt + 1))
                        if qt >= 1:
                            vector.wait_ge(stx.h, 16 * qt)
                        vector.tensor_tensor(x1b[:], xtb[:], att[:], op=ALU.add).then_inc(vx1.h, 1)
                        vx1.inc()
                        vector.tensor_reduce(stat[:, 0:1], x1b[:], axis=AX.X, op=ALU.add)
                        vector.drain()
                        vector.tensor_scalar(stat[:, 1:2], stat[:, 0:1], 1.0 / D, None, op0=ALU.mult)
                        vector.drain()
                        vector.tensor_scalar(tmpb[:], x1b[:], stat[:, 1:2], None, op0=ALU.subtract)
                        vector.tensor_tensor(sqb2[:], tmpb[:, 0:1024], tmpb[:, 0:1024], op=ALU.mult)
                        vector.tensor_reduce(stat[:, 2:3], sqb2[:], axis=AX.X, op=ALU.add)
                        vector.tensor_tensor(sqb2[:], tmpb[:, 1024:2048], tmpb[:, 1024:2048], op=ALU.mult)
                        vector.tensor_reduce(stat[:, 10:11], sqb2[:], axis=AX.X, op=ALU.add)
                        vector.drain()
                        vector.tensor_tensor(stat[:, 2:3], stat[:, 2:3], stat[:, 10:11], op=ALU.add)
                        vector.drain()
                        vector.drain().then_inc(vs1.h, 1)
                        vs1.inc()
                        vector.wait_ge(ss1.h, 2 * qt + 2)
                        vector.reciprocal(stat[:, 4:5], stat[:, 3:4])
                        vector.drain()
                        vector.tensor_scalar(tmpb[:], tmpb[:], stat[:, 4:5], None,
                                             op0=ALU.mult).then_inc(vln.h, 1)
                        vln.inc()
                        for g in range(4):
                            vector.wait_ge(sh.h, 4 * qt + g + 1)
                            hi = lnh[:, 4 * g:4 * (g + 1), :].rearrange("p a b -> p (a b)")
                            lo = lnl[:, 4 * g:4 * (g + 1), :].rearrange("p a b -> p (a b)")
                            vector.scalar_tensor_tensor(lo, pb[4 + g % 2][:], 1.0, hi,
                                                        op0=ALU.mult, op1=ALU.subtract).then_inc(vl.h, 1)
                            vl.inc()

                @block.scalar
                def _(scalar):
                    for qt in range(NT):
                        for k in range(2):
                            scalar.wait_ge(vs1.h, 2 * qt + k + 1)
                            scalar.activation(stat[:, 3:4], stat[:, 2:3], AF.Sqrt,
                                              bias=stat[:, 8:9], scale=1.0 / D).then_inc(ss1.h, 1)
                            ss1.inc()
                        for g in range(4):
                            scalar.wait_ge(tt2.h, 4 * qt + g + 1)
                            if g == 0:
                                scalar.wait_ge(stn.h, 32 * qt)
                            scalar.copy(lnh[:, 4 * g:4 * (g + 1), :], pb[4 + g % 2][:]).then_inc(sh.h, 1)
                            sh.inc()

                @block.gpsimd
                def _(gpsimd):
                    for qt in range(NT):
                        gpsimd.wait_ge(vx1.h, qt + 1)
                        gpsimd.dma_start(bass.AP(x1S, qt * 128 * D, [[D, 128], [1, D]]),
                                         x1b[:]).then_inc(stx.h, 16)
                        stx.inc(16)
                        gpsimd.wait_ge(vl.h, 4 * (qt + 1))
                        dsth = bass.AP(ln2Th, qt * D * 128, [[128, 128], [128 * 128, ND], [1, 128]])
                        dstl = bass.AP(ln2Tl, qt * D * 128, [[128, 128], [128 * 128, ND], [1, 128]])
                        gpsimd.dma_start(dsth, lnh[:]).then_inc(stn.h, 16)
                        stn.inc(16)
                        gpsimd.dma_start(dstl, lnl[:]).then_inc(stn.h, 16)
                        stn.inc(16)
                    gpsimd.wait_ge(stx.h, stx.n)
                    gpsimd.wait_ge(stn.h, stn.n)
                barrier()

        if "c" in phases:
            with ExitStack() as ph:
                psb = lambda name, shp, dt: ph.enter_context(nc.sbuf_tensor(name, shp, dt))
                l2h = psb("l2h", [128, ND, 512], BF16)
                l2l = psb("l2l", [128, ND, 512], BF16)
                hTh = psb("hTh", [128, 16, 512], BF16)
                hTl = psb("hTl", [128, 16, 512], BF16)
                yT = psb("yT", [128, ND, 512], F32)
                wgb = [psb(f"wgb{i}", [128, ND, 128], BF16) for i in range(2)]
                wub = [psb(f"wub{i}", [128, ND, 128], BF16) for i in range(2)]
                wpb = [psb(f"wpb{i}", [128, 16, 128], BF16) for i in range(2)]
                sgb = psb("sgb", [128, 512], F32)
                hfb = psb("hfb", [128, 512], F32)
                hfb2 = psb("hfb2", [128, 512], F32)
                h2bb2 = [psb(f"h2bb{i}", [128, 512], BF16) for i in range(2)]
                rr = psb("rr", [1, 1024], F32)
                invc = psb("invc", [128, 4], F32)
                ysb = psb("ysb", [128, D], F32)
                x1t = [psb(f"x1t{i}", [128, D], F32) for i in range(2)]
                osb = [psb(f"osb{i}", [128, D], F32) for i in range(2)]
                tmpc = psb("pct1", [128, D], F32)
                sqc = psb("pcsq", [128, D], F32)
                pcy = [ph.enter_context(nc.psum_tensor(f"pcy{i}", [128, 512], F32)) for i in range(2)]

                for tc in range(4):
                    tg = f"c{tc}_"
                    ldln = sem(tg + "ldln", "Cldln")
                    ldg = [sem(tg + "ldg0", "Cldg0"), sem(tg + "ldg1", "Cldg1")]
                    ldp = [sem(tg + "ldp0", "Cldp0"), sem(tg + "ldp1", "Cldp1")]
                    ldx1 = [sem(tg + "ldx10", "Cldx10"), sem(tg + "ldx11", "Cldx11")]
                    tgu = sem(tg + "tgu", "Ctgu")
                    ssg = sem(tg + "ssg", "Cssg")
                    vh = sem(tg + "vh", "Cvh")
                    sh2 = sem(tg + "sh2", "Csh2")
                    vlo = sem(tg + "vlo", "Cvlo")
                    tp1 = sem(tg + "tp1", "Ctp1")
                    vy = sem(tg + "vy", "Cvy")
                    trm = sem(tg + "trm", "Ctrm")
                    vrm = sem(tg + "vrm", "Cvrm")
                    srm = sem(tg + "srm", "Csrm")
                    vri = sem(tg + "vri", "Cvri")
                    tiv = sem(tg + "tiv", "Ctiv")
                    siv = sem(tg + "siv", "Csiv")
                    tyt = sem(tg + "tyt", "Ctyt")
                    vye = sem(tg + "vye", "Cvye")
                    vo = sem(tg + "vo", "Cvo")
                    sto = sem(tg + "sto", "Csto")

                    @block.sync
                    def _(sync, tc=tc):
                        for s in range(4):
                            srch = bass.AP(ln2Th, (tc * 4 + s) * D * 128,
                                           [[128, 128], [128 * 128, ND], [1, 128]])
                            srcl = bass.AP(ln2Tl, (tc * 4 + s) * D * 128,
                                           [[128, 128], [128 * 128, ND], [1, 128]])
                            sync.dma_start(l2h[:, :, s * 128:(s + 1) * 128], srch).then_inc(ldln.h, 16)
                            ldln.inc(16)
                            sync.dma_start(l2l[:, :, s * 128:(s + 1) * 128], srcl).then_inc(ldln.h, 16)
                            ldln.inc(16)
                        for hg in range(4):
                            for hc in range(16):
                                j = hg * 16 + hc
                                par = j % 2
                                if j >= 2:
                                    sync.wait_ge(tgu.h, tgu.at(j - 1))
                                hcg = hg * 16 + hc
                                srcg = bass.AP(wgT, hcg * 128 * ND * 128,
                                               [[ND * 128, 128], [128, ND], [1, 128]])
                                srcu = bass.AP(wgT, (64 + hcg) * 128 * ND * 128,
                                               [[ND * 128, 128], [128, ND], [1, 128]])
                                sync.dma_start(wgb[par][:], srcg).then_inc(ldg[par].h, 16)
                                sync.dma_start(wub[par][:], srcu).then_inc(ldg[par].h, 16)
                                ldg[par].inc(32)
                            for dd in range(16):
                                k = hg * 16 + dd
                                par = k % 2
                                if k >= 2:
                                    sync.wait_ge(tp1.h, tp1.at(k - 1))
                                srcp = bass.AP(wpT, dd * 128 * 64 * 128 + hg * 16 * 128,
                                               [[64 * 128, 128], [128, 16], [1, 128]])
                                sync.dma_start(wpb[par][:], srcp).then_inc(ldp[par].h, 16)
                                ldp[par].inc(16)
                        for tt in range(4):
                            par = tt % 2
                            if tt >= 2:
                                sync.wait_ge(vo.h, vo.at(tt - 1))
                            t_glob = tc * 4 + tt
                            sync.dma_start(x1t[par][:],
                                           bass.AP(x1S, t_glob * 128 * D, [[D, 128], [1, D]])).then_inc(ldx1[par].h, 16)
                            ldx1[par].inc(16)

                    def vye_expected(gi_done):
                        tt_full, g_in = divmod(gi_done, 4)
                        return tt_full * 5 + g_in

                    @block.tensor
                    def _(tensor, tc=tc):
                        tensor.wait_ge(ldln.h, ldln.at(128))
                        for hg in range(4):
                            for hc in range(16):
                                j = hg * 16 + hc
                                par = j % 2
                                gb = pb[0] if par == 0 else pcy[0]
                                ub = pb[1] if par == 0 else pcy[1]
                                tensor.wait_ge(ldg[par].h, ldg[par].at(32 * (j // 2 + 1)))
                                if j >= 2:
                                    tensor.wait_ge(vh.h, vh.at(j - 1))
                                if j >= 2:
                                    tensor.wait_ge(sh2.h, sh2.at(j - 1))
                                    tensor.matmul(pb[2][0:1, :], onesb[:], h2bb2[(j - 2) % 2][:],
                                                  start=(j == 2), stop=False, skip_group_check=True)
                                for dc in range(ND):
                                    tensor.matmul(gb[:], wgb[par][:, dc, :], l2h[:, dc, :],
                                                  start=(dc == 0), stop=False, skip_group_check=True)
                                    tensor.matmul(gb[:], wgb[par][:, dc, :], l2l[:, dc, :],
                                                  start=False, stop=(dc == ND - 1), skip_group_check=True)
                                for dc in range(ND):
                                    tensor.matmul(ub[:], wub[par][:, dc, :], l2h[:, dc, :],
                                                  start=(dc == 0), stop=False, skip_group_check=True)
                                    mm = tensor.matmul(ub[:], wub[par][:, dc, :], l2l[:, dc, :],
                                                       start=False, stop=(dc == ND - 1), skip_group_check=True)
                                mm.then_inc(tgu.h, 1)
                                tgu.inc()
                            for dd in range(ND):
                                k = hg * 16 + dd
                                par = k % 2
                                tensor.wait_ge(ldp[par].h, ldp[par].at(16 * (k // 2 + 1)))
                                tensor.wait_ge(vlo.h, vlo.at(hg * 16 + 16))
                                if k >= 2:
                                    tensor.wait_ge(vy.h, vy.at(k - 1))
                                for hc in range(16):
                                    tensor.matmul(pb[4 + k % 2][:], wpb[par][:, hc, :], hTh[:, hc, :],
                                                  start=(hc == 0), stop=False, skip_group_check=True)
                                    mm = tensor.matmul(pb[4 + k % 2][:], wpb[par][:, hc, :], hTl[:, hc, :],
                                                       start=False, stop=(hc == 15), skip_group_check=True)
                                mm.then_inc(tp1.h, 1)
                                tp1.inc()
                        tensor.wait_ge(sh2.h, sh2.at(63))
                        tensor.matmul(pb[2][0:1, :], onesb[:], h2bb2[0][:],
                                      start=False, stop=False, skip_group_check=True)
                        tensor.wait_ge(sh2.h, sh2.at(64))
                        tensor.matmul(pb[2][0:1, :], onesb[:], h2bb2[1][:],
                                      start=False, stop=True, skip_group_check=True).then_inc(trm.h, 1)
                        trm.inc()
                        tensor.wait_ge(vri.h, vri.at(1))
                        for ttj in range(4):
                            mm = tensor.transpose(pb[3][:, ttj:ttj + 1], rr[0:1, ttj * 128:(ttj + 1) * 128],
                                                  idf[0:1, 0:1])
                        mm.then_inc(tiv.h, 1)
                        tiv.inc()
                        tensor.wait_ge(vy.h, vy.at(64))
                        for tt in range(4):
                            for g in range(4):
                                gi = 4 * tt + g
                                if gi >= 2:
                                    tensor.wait_ge(vye.h, vye.at(vye_expected(gi - 1)))
                                for j in range(4):
                                    dd = 4 * g + j
                                    mm = tensor.transpose(pcy[g % 2][:, j * 128:(j + 1) * 128],
                                                          yT[:, dd, tt * 128:(tt + 1) * 128], idf[:])
                                mm.then_inc(tyt.h, 1)
                                tyt.inc()

                    @block.scalar
                    def _(scalar, tc=tc):
                        for j in range(64):
                            hg, hc = divmod(j, 16)
                            scalar.wait_ge(tgu.h, tgu.at(j + 1))
                            gbs = pb[0] if j % 2 == 0 else pcy[0]
                            scalar.activation(sgb[:], gbs[:], AF.Sigmoid).then_inc(ssg.h, 1)
                            ssg.inc()
                            scalar.wait_ge(vh.h, vh.at(j + 1))
                            if hg >= 1:
                                scalar.wait_ge(tp1.h, tp1.at(hg * 16))
                            scalar.copy(hTh[:, hc, :], hfb2[:])
                            scalar.activation(h2bb2[j % 2][:], hfb2[:], AF.Square).then_inc(sh2.h, 1)
                            sh2.inc()
                        scalar.wait_ge(vrm.h, vrm.at(1))
                        scalar.activation(rr[0:1, 512:1024], rr[0:1, 0:512], AF.Sqrt).then_inc(srm.h, 1)
                        srm.inc()
                        scalar.wait_ge(tiv.h, tiv.at(1))
                        scalar.copy(invc[:, 0:4], pb[3][:, 0:4]).then_inc(siv.h, 1)
                        siv.inc()
                        for tt in range(4):
                            scalar.wait_ge(vo.h, vo.at(tt))
                            scalar.wait_ge(vye.h, vye.at(5 * tt + 5))
                            scalar.activation(stat[:, 3:4], stat[:, 2:3], AF.Sqrt,
                                              bias=stat[:, 8:9], scale=1.0 / D).then_inc(srm.h, 1)
                            srm.inc()

                    @block.vector
                    def _(vector, tc=tc):
                        for hg in range(4):
                            for hc in range(16):
                                j = hg * 16 + hc
                                vector.wait_ge(ssg.h, ssg.at(j + 1))
                                gbv = pb[0] if j % 2 == 0 else pcy[0]
                                ubv = pb[1] if j % 2 == 0 else pcy[1]
                                vector.tensor_tensor(hfb[:], sgb[:], gbv[:], op=ALU.mult)
                                vector.tensor_tensor(hfb2[:], hfb[:], ubv[:], op=ALU.mult).then_inc(vh.h, 1)
                                vh.inc()
                                vector.wait_ge(sh2.h, sh2.at(j + 1))
                                if hg >= 1:
                                    vector.wait_ge(tp1.h, tp1.at(hg * 16))
                                vector.scalar_tensor_tensor(hTl[:, hc, :], hfb2[:], 1.0, hTh[:, hc, :],
                                                            op0=ALU.mult, op1=ALU.subtract).then_inc(vlo.h, 1)
                                vlo.inc()
                            for dd in range(ND):
                                k = hg * 16 + dd
                                vector.wait_ge(tp1.h, tp1.at(k + 1))
                                if hg == 0:
                                    vector.tensor_copy(yT[:, dd, :], pb[4 + k % 2][:]).then_inc(vy.h, 1)
                                else:
                                    vector.tensor_tensor(yT[:, dd, :], yT[:, dd, :],
                                                         pb[4 + k % 2][:], op=ALU.add).then_inc(vy.h, 1)
                                vy.inc()
                        vector.wait_ge(trm.h, trm.at(1))
                        vector.tensor_scalar(rr[0:1, 0:512], pb[2][0:1, :], 1.0 / HFF, EPS_RMS,
                                             op0=ALU.mult, op1=ALU.add).then_inc(vrm.h, 1)
                        vrm.inc()
                        vector.wait_ge(srm.h, srm.at(1))
                        vector.reciprocal(rr[0:1, 0:512], rr[0:1, 512:1024])
                        vector.drain()
                        vector.drain().then_inc(vri.h, 1)
                        vri.inc()
                        vector.wait_ge(siv.h, siv.at(1))
                        for tt in range(4):
                            par = tt % 2
                            for g in range(4):
                                gi = 4 * tt + g
                                vector.wait_ge(tyt.h, tyt.at(gi + 1))
                                vector.tensor_scalar(ysb[:, g * 512:(g + 1) * 512], pcy[g % 2][:],
                                                     invc[:, tt:tt + 1], None,
                                                     op0=ALU.mult).then_inc(vye.h, 1)
                                vye.inc()
                            vector.tensor_reduce(stat[:, 0:1], ysb[:], axis=AX.X, op=ALU.add)
                            vector.drain()
                            vector.tensor_scalar(stat[:, 1:2], stat[:, 0:1], 1.0 / D, None, op0=ALU.mult)
                            vector.drain()
                            vector.tensor_scalar(tmpc[:], ysb[:], stat[:, 1:2], None, op0=ALU.subtract)
                            vector.tensor_tensor(sqc[:], tmpc[:], tmpc[:], op=ALU.mult)
                            vector.tensor_reduce(stat[:, 2:3], sqc[:], axis=AX.X, op=ALU.add)
                            vector.drain()
                            vector.drain().then_inc(vye.h, 1)
                            vye.inc()
                            vector.wait_ge(srm.h, srm.at(tt + 2))
                            vector.reciprocal(stat[:, 4:5], stat[:, 3:4])
                            vector.drain()
                            vector.tensor_scalar(tmpc[:], tmpc[:], stat[:, 4:5], None, op0=ALU.mult)
                            vector.wait_ge(ldx1[par].h, ldx1[par].at(16 * (tt // 2 + 1)))
                            if tt >= 2:
                                vector.wait_ge(sto.h, sto.at(16 * (tt - 1)))
                            vector.tensor_tensor(osb[par][:], x1t[par][:], tmpc[:],
                                                 op=ALU.add).then_inc(vo.h, 1)
                            vo.inc()

                    @block.gpsimd
                    def _(gpsimd, tc=tc):
                        for tt in range(4):
                            par = tt % 2
                            gpsimd.wait_ge(vo.h, vo.at(tt + 1))
                            t_glob = tc * 4 + tt
                            gpsimd.dma_start(out[t_glob * 128:(t_glob + 1) * 128, :],
                                             osb[par][:]).then_inc(sto.h, 16)
                            sto.inc(16)
                        gpsimd.wait_ge(sto.h, sto.n)
                    barrier()

    return nc


_ROPE_THETA = 10000.0


def _host_layouts(inputs):
    x = np.asarray(inputs["L0_attn_normalized_resid_pre"], dtype=np.float32)
    WQ = np.asarray(inputs["L0_attn_W_Q"], dtype=np.float32)
    WK = np.asarray(inputs["L0_attn_W_K"], dtype=np.float32)
    WV = np.asarray(inputs["L0_attn_W_V"], dtype=np.float32)
    WO = np.asarray(inputs["L0_attn_W_O"], dtype=np.float32)
    bQ = np.asarray(inputs["L0_attn_b_Q"], dtype=np.float32)
    bK = np.asarray(inputs["L0_attn_b_K"], dtype=np.float32)
    bV = np.asarray(inputs["L0_attn_b_V"], dtype=np.float32)
    bO = np.asarray(inputs["L0_attn_b_O"], dtype=np.float32)
    wgu = np.ascontiguousarray(np.asarray(inputs["L0_ffn_w_gate_up"], dtype=np.float32))
    wpr = np.ascontiguousarray(np.asarray(inputs["L0_ffn_w_proj"], dtype=np.float32))

    perm = np.concatenate([np.arange(0, DH, 2), np.arange(1, DH, 2)])
    wqp = np.ascontiguousarray(WQ[:, :, perm].transpose(1, 0, 2).reshape(D, D))
    wkp = np.ascontiguousarray(WK[:, :, perm].transpose(1, 0, 2).reshape(D, D))
    wvp = np.ascontiguousarray(WV.transpose(1, 0, 2).reshape(D, D))
    wo = np.ascontiguousarray(WO.reshape(D, D))
    bq = np.ascontiguousarray(np.broadcast_to(bQ[:, perm].reshape(1, D), (128, D)))
    bk = np.ascontiguousarray(np.broadcast_to(bK[:, perm].reshape(1, D), (128, D)))
    bvt = np.ascontiguousarray(np.broadcast_to(bV.reshape(1, D), (128, D)))
    bot = np.ascontiguousarray(np.broadcast_to(bO.reshape(1, D), (128, D)))

    inv_freq = (1.0 / _ROPE_THETA ** (np.arange(0, DH, 2, dtype=np.float32) / DH)).astype(np.float32)
    m = np.arange(S, dtype=np.float32)
    freqs = np.outer(m, inv_freq).astype(np.float32)
    cosf = np.cos(freqs).astype(np.float32)
    sinf = np.sin(freqs).astype(np.float32)
    tab1 = np.ascontiguousarray(cosf.reshape(NT, 128, 64).transpose(1, 0, 2).reshape(128, NT * 64))
    tab2 = np.ascontiguousarray(sinf.reshape(NT, 128, 64).transpose(1, 0, 2).reshape(128, NT * 64))

    identf = np.eye(128, dtype=np.float32)
    identb = np.eye(128, dtype=ml_dtypes.bfloat16)
    dmask = np.zeros((128, 640), dtype=np.float32)
    dmask[:, 512:640] = np.triu(np.full((128, 128), -1.0e30, dtype=np.float32), 1)
    onesb = np.ones((128, 1), dtype=ml_dtypes.bfloat16)

    const = dict(wqp=wqp, wkp=wkp, wvp=wvp, wo=wo, wgu=wgu, wpr=wpr,
                 bq=bq, bk=bk, bvt=bvt, bot=bot, tab1=tab1, tab2=tab2,
                 identf=identf, identb=identb, dmask=dmask, onesb=onesb)
    return x, const


def kernel(**inputs):
    x, const = _host_layouts(inputs)
    nc = build_program()
    in_maps = [dict(const, x=np.ascontiguousarray(x[b])) for b in range(8)]
    res = run_bass_kernel_spmd(nc, in_maps, core_ids=list(range(8)))
    return np.stack([np.asarray(r["out"]) for r in res.results], axis=0)


# version 15
# speedup vs baseline: 1.0210x; 1.0210x over previous
import math
from contextlib import ExitStack

import numpy as np
import ml_dtypes
import concourse.bass as bass
import concourse.mybir as mybir
from concourse.bass_utils import run_bass_kernel_spmd

F32 = mybir.dt.float32
BF16 = mybir.dt.bfloat16
AF = mybir.ActivationFunctionType
ALU = mybir.AluOpType
AX = mybir.AxisListType

S = 2048
D = 2048
H = 16
DH = 128
NT = S // 128
ND = D // 128
HFF = 8192
NHC = 2 * HFF // 128
SCALE = 1.0 / math.sqrt(DH)
EPS_LN = 1e-5
EPS_RMS = float(np.finfo(np.float32).eps)


class _Pool:
    def __init__(self, h):
        self.h = h
        self.n = 0


class Sem:

    def __init__(self, pool):
        self.p = pool
        self.base = pool.n

    @property
    def h(self):
        return self.p.h

    @property
    def n(self):
        return self.p.n

    def inc(self, k=1):
        self.p.n += k
        return self.p.n

    def at(self, v):
        return self.base + v


def build_program(phases="0qabc"):
    nc = bass.Bass("TRN2", target_bir_lowering=False)

    x_in = nc.dram_tensor("x", [S, D], F32, kind="ExternalInput").ap()
    wqp = nc.dram_tensor("wqp", [D, D], F32, kind="ExternalInput").ap()
    wkp = nc.dram_tensor("wkp", [D, D], F32, kind="ExternalInput").ap()
    wvp = nc.dram_tensor("wvp", [D, D], F32, kind="ExternalInput").ap()
    wo_in = nc.dram_tensor("wo", [D, D], F32, kind="ExternalInput").ap()
    wgu_in = nc.dram_tensor("wgu", [2 * HFF, D], F32, kind="ExternalInput").ap()
    wpr_in = nc.dram_tensor("wpr", [D, HFF], F32, kind="ExternalInput").ap()
    bq_in = nc.dram_tensor("bq", [128, D], F32, kind="ExternalInput").ap()
    bk_in = nc.dram_tensor("bk", [128, D], F32, kind="ExternalInput").ap()
    bvt_in = nc.dram_tensor("bvt", [128, D], F32, kind="ExternalInput").ap()
    bot_in = nc.dram_tensor("bot", [128, D], F32, kind="ExternalInput").ap()
    tab1_in = nc.dram_tensor("tab1", [128, NT * 64], F32, kind="ExternalInput").ap()
    tab2_in = nc.dram_tensor("tab2", [128, NT * 64], F32, kind="ExternalInput").ap()
    idf_in = nc.dram_tensor("identf", [128, 128], F32, kind="ExternalInput").ap()
    idb_in = nc.dram_tensor("identb", [128, 128], BF16, kind="ExternalInput").ap()
    dmask_in = nc.dram_tensor("dmask", [128, 640], F32, kind="ExternalInput").ap()
    ones_in = nc.dram_tensor("onesb", [128, 1], BF16, kind="ExternalInput").ap()

    out = nc.dram_tensor("out", [S, D], F32, kind="ExternalOutput").ap()

    ln1T = nc.dram_tensor("ln1T", [NT, D, 128], F32, kind="Internal")
    zT = nc.dram_tensor("zT", [D, S], F32, kind="Internal")
    x1S = nc.dram_tensor("x1S", [S, D], F32, kind="Internal")
    ln2Th = nc.dram_tensor("ln2Th", [NT, D, 128], BF16, kind="Internal")
    ln2Tl = nc.dram_tensor("ln2Tl", [NT, D, 128], BF16, kind="Internal")
    wgT = nc.dram_tensor("wgT", [NHC, 128, ND, 128], BF16, kind="Internal")
    wpT = nc.dram_tensor("wpT", [ND, 128, 64, 128], BF16, kind="Internal")

    with ExitStack() as top:
        tsb = lambda name, shp, dt: top.enter_context(nc.sbuf_tensor(name, shp, dt))

        idf = tsb("idf", [128, 128], F32)
        idb = tsb("idb", [128, 128], BF16)
        onesb = tsb("onesc", [128, 1], BF16)
        stat = tsb("stat", [128, 24], F32)

        pb = [top.enter_context(nc.psum_tensor(f"pb{i}", [128, 512], F32)) for i in range(6)]

        pools = {}

        def sem(name, pool=None):
            key = pool or name
            if key not in pools:
                pools[key] = _Pool(top.enter_context(nc.semaphore(key)))
            return Sem(pools[key])

        BAR = sem("bar")
        block = top.enter_context(nc.Block())

        def barrier():
            tgt = BAR.n + 5
            for name in ("sync", "vector", "scalar", "tensor", "gpsimd"):
                @getattr(block, name)
                def _(eng, name=name):
                    if name in ("vector", "scalar", "tensor"):
                        eng.drain().then_inc(BAR.h, 1)
                    else:
                        eng.nop().then_inc(BAR.h, 1)
                    eng.wait_ge(BAR.h, tgt)
            BAR.inc(5)

        ldc = sem("ldc")

        @block.sync
        def _(sync):
            for dst, src in ((idf, idf_in), (idb, idb_in), (onesb, ones_in)):
                sync.dma_start(dst[:], src).then_inc(ldc.h, 16)
                ldc.inc(16)
            sync.wait_ge(ldc.h, ldc.n)

        @block.vector
        def _(vector):
            vector.memset(stat[:, 8:9], EPS_LN)
        barrier()

        with ExitStack() as ph:
            psb = lambda name, shp, dt: ph.enter_context(nc.sbuf_tensor(name, shp, dt))
            xt = [psb(f"p0x{i}", [128, D], F32) for i in range(2)]
            lnst = [psb(f"p0st{i}", [128, ND, 128], F32) for i in range(2)]
            tmp1 = psb("p0t1", [128, D], F32)
            tmp2 = psb("p0t2", [128, D], F32)
            sqb = psb("p0sq", [128, D], F32)

            ldx = [sem("p0ldx0"), sem("p0ldx1")]
            v1 = sem("p0v1")
            s1 = sem("p0s1")
            v2 = sem("p0v2")
            t1 = sem("p0t1s")
            se = sem("p0se")
            stl = [sem("p0stl0"), sem("p0stl1")]

            @block.sync
            def _(sync):
                for tt in range(NT):
                    par = tt % 2
                    if tt >= 2:
                        sync.wait_ge(v1.h, tt - 1)
                    sync.dma_start(xt[par][:], x_in[tt * 128:(tt + 1) * 128, :]).then_inc(ldx[par].h, 16)
                    ldx[par].inc(16)

            @block.vector
            def _(vector):
                for tt in range(NT):
                    par = tt % 2
                    vector.wait_ge(ldx[par].h, 16 * (tt // 2 + 1))
                    vector.tensor_reduce(stat[:, 0:1], xt[par][:], axis=AX.X, op=ALU.add)
                    vector.drain()
                    vector.tensor_scalar(stat[:, 1:2], stat[:, 0:1], 1.0 / D, None, op0=ALU.mult)
                    vector.drain()
                    vector.tensor_scalar(tmp1[:], xt[par][:], stat[:, 1:2], None, op0=ALU.subtract)
                    vector.tensor_tensor(sqb[:], tmp1[:], tmp1[:], op=ALU.mult)
                    vector.tensor_reduce(stat[:, 2:3], sqb[:], axis=AX.X, op=ALU.add)
                    vector.drain()
                    vector.drain().then_inc(v1.h, 1)
                    v1.inc()
                    vector.wait_ge(s1.h, tt + 1)
                    vector.reciprocal(stat[:, 4:5], stat[:, 3:4])
                    vector.drain()
                    if tt >= 1:
                        vector.wait_ge(t1.h, 4 * tt)
                    vector.tensor_scalar(tmp2[:], tmp1[:], stat[:, 4:5], None, op0=ALU.mult).then_inc(v2.h, 1)
                    v2.inc()

            @block.scalar
            def _(scalar):
                for tt in range(NT):
                    scalar.wait_ge(v1.h, tt + 1)
                    scalar.activation(stat[:, 3:4], stat[:, 2:3], AF.Sqrt,
                                      bias=stat[:, 8:9], scale=1.0 / D).then_inc(s1.h, 1)
                    s1.inc()
                    par = tt % 2
                    for g in range(4):
                        scalar.wait_ge(t1.h, 4 * tt + g + 1)
                        if g == 0:
                            scalar.wait_ge(stl[par].h, 16 * (tt // 2))
                        scalar.copy(lnst[par][:, 4 * g:4 * (g + 1), :], pb[g % 2][:]).then_inc(se.h, 1)
                        se.inc()

            @block.tensor
            def _(tensor):
                for tt in range(NT):
                    tensor.wait_ge(v2.h, tt + 1)
                    for g in range(4):
                        if 4 * tt + g >= 2:
                            tensor.wait_ge(se.h, 4 * tt + g - 1)
                        for j in range(4):
                            dc = 4 * g + j
                            mm = tensor.transpose(pb[g % 2][:, j * 128:(j + 1) * 128],
                                                  tmp2[:, dc * 128:(dc + 1) * 128], idf[:])
                        mm.then_inc(t1.h, 1)
                        t1.inc()

            @block.gpsimd
            def _(gpsimd):
                for tt in range(NT):
                    par = tt % 2
                    gpsimd.wait_ge(se.h, 4 * (tt + 1))
                    dst = bass.AP(ln1T, tt * D * 128, [[128, 128], [128 * 128, ND], [1, 128]])
                    gpsimd.dma_start(dst, lnst[par][:]).then_inc(stl[par].h, 16)
                    stl[par].inc(16)
                gpsimd.wait_ge(stl[0].h, stl[0].n)
                gpsimd.wait_ge(stl[1].h, stl[1].n)
            barrier()

        with ExitStack() as pa:
            asb = lambda name, shp, dt: pa.enter_context(nc.sbuf_tensor(name, shp, dt))
            Ksb = asb("Ksb", [128, 4, S], F32)
            Qsb = asb("Qsb", [128, 4, S], F32)
            Vsb = asb("Vsb", [128, NT, 512], F32)
            costab = asb("costab", [128, NT, 64], F32)
            sintab = asb("sintab", [128, NT, 64], F32)
            dmaskp = asb("dmaskp", [128, 640], F32)

            qwb4 = [asb(f"qwb{i}", [128, D], BF16) for i in range(3)]
            qt_ = asb("qt", [128, D], BF16)
            qg1_2 = [asb(f"qg1{i}", [128, D], BF16) for i in range(4)]
            qg2 = asb("qg2", [128, D], BF16)
            qsc = asb("qsc", [128, 32], BF16)
            qrec = asb("qrec", [128, 32], F32)
            qstg = asb("qstg", [128, ND, 128], BF16)
            QL = [sem(f"ql{i}", f"QL{i}") for i in range(3)]
            Qv = sem("qv", "Qv")
            Qt = sem("qt_s", "Qt")
            Qs = sem("qs", "Qs")
            Qst = sem("qst", "Qst")

            def unit_aps(u):
                if u < NHC:
                    srca = bass.AP(wgu_in.tensor, u * 128 * D, [[D, 128], [1, D]])
                    dsta = bass.AP(wgT, u * 128 * ND * 128, [[ND * 128, 128], [128, ND], [1, 128]])
                else:
                    v = u - NHC
                    dd, h4 = divmod(v, 4)
                    srca = bass.AP(wpr_in.tensor, dd * 128 * HFF + h4 * 2048, [[HFF, 128], [1, 2048]])
                    dsta = bass.AP(wpT, dd * 128 * 64 * 128 + h4 * 16 * 128,
                                   [[64 * 128, 128], [128, 16], [1, 128]])
                return srca, dsta

            def q_load(gpsimd, v):
                if v >= 192:
                    return
                if v >= 3:
                    gpsimd.wait_ge(Qv.h, Qv.at(v - 2))
                srca, _ = unit_aps(v)
                gpsimd.dma_start(qwb4[v % 3][:], srca).then_inc(QL[v % 3].h, 16)
                QL[v % 3].inc(16)

            def q_vector(vector, u):
                par = u % 3
                qwb = qwb4[par]
                qg1 = qg1_2[u % 4]
                vector.wait_ge(QL[par].h, QL[par].at(16 * (u // 3 + 1)))
                if u >= 4:
                    vector.wait_ge(Qt.h, Qt.at(2 * (u - 3)))
                wb3 = qwb[:].rearrange("p (g k) -> p g k", k=64)
                vector.tensor_reduce(qrec[:, 0:32], wb3, axis=AX.X, op=ALU.add,
                                     apply_absolute_value=True)
                vector.drain()
                vector.tensor_scalar(qsc[:, 0:32], qrec[:, 0:32], 1.0 / 64, 1e-8,
                                     op0=ALU.mult, op1=ALU.max)
                vector.drain()
                vector.reciprocal(qrec[:, 0:32], qsc[:, 0:32])
                vector.drain()
                rec3 = bass.AP(qrec, 0, [[qrec[:].ap[0][0], 128], [1, 32], [0, 64]])
                sc3 = bass.AP(qsc, 0, [[qsc[:].ap[0][0], 128], [1, 32], [0, 64]])
                t3 = qt_[:].rearrange("p (g k) -> p g k", k=64)
                vector.tensor_tensor(t3, wb3, rec3, op=ALU.mult)
                vector.tensor_scalar(qg2[:], qt_[:], -0.5, None, op0=ALU.is_lt)
                vector.scalar_tensor_tensor(qg1[:], qt_[:], 0.5, qg2[:],
                                            op0=ALU.is_gt, op1=ALU.subtract)
                g13 = qg1[:].rearrange("p (g k) -> p g k", k=64)
                qt3 = qt_[:].rearrange("p (g k) -> p g k", k=64)
                vector.tensor_tensor(qt3, g13, sc3, op=ALU.mult)
                vector.tensor_tensor(qg2[:], qt_[:], qwb[:], op=ALU.subtract)
                vector.tensor_tensor(qg1[:], qwb[:], qg2[:], op=ALU.add).then_inc(Qv.h, 1)
                Qv.inc()

            def q_tensor(tensor, u, qpb):
                qg1 = qg1_2[u % 4]
                tensor.wait_ge(Qv.h, Qv.at(u + 1))
                for g in range(2):
                    if u >= 1:
                        tensor.wait_ge(Qs.h, Qs.at(2 * (u - 1) + g + 1))
                    for j in range(8):
                        blk = 8 * g + j
                        mm = tensor.transpose(qpb[g][:, j * 128:(j + 1) * 128],
                                              qg1[:, blk * 128:(blk + 1) * 128], idb[:])
                    mm.then_inc(Qt.h, 1)
                    Qt.inc()

            def q_vecevac(vector, u, qpb):
                for g in range(2):
                    vector.wait_ge(Qt.h, Qt.at(2 * u + g + 1))
                    if g == 0:
                        vector.wait_ge(Qst.h, Qst.at(16 * u))
                    vector.tensor_copy(qstg[:, 8 * g:8 * (g + 1), :],
                                       qpb[g][:]).then_inc(Qs.h, 1)
                    Qs.inc()

            def q_store(gpsimd, u):
                gpsimd.wait_ge(Qs.h, Qs.at(2 * (u + 1)))
                _, dsta = unit_aps(u)
                gpsimd.dma_start(dsta, qstg[:]).then_inc(Qst.h, 16)
                Qst.inc(16)

            qctr = {"vslot": 0, "t": 0, "s": 0, "ld": 3, "st": 0}

            QLAG = 6

            def q_vector_slot(vector, qpb):
                u = qctr["vslot"] - QLAG
                qctr["vslot"] += 1
                if 0 <= u < 192:
                    q_vector(vector, u)
                q_vecevac_upto(vector, qctr["vslot"] - 7, qpb)

            def q_tensor_upto(tensor, target, qpb):
                while qctr["t"] < min(target, 192):
                    q_tensor(tensor, qctr["t"], qpb)
                    qctr["t"] += 1

            def q_vecevac_upto(vector, target, qpb):
                while qctr["s"] < min(target, 192):
                    q_vecevac(vector, qctr["s"], qpb)
                    qctr["s"] += 1

            def q_gp_pass(gpsimd, ld_target, st_target):
                ld_target = min(ld_target, 192)
                st_target = min(st_target, 192)
                while qctr["ld"] < ld_target or qctr["st"] < st_target:
                    if qctr["ld"] < ld_target:
                        q_load(gpsimd, qctr["ld"])
                        qctr["ld"] += 1
                    if qctr["st"] < st_target:
                        q_store(gpsimd, qctr["st"])
                        qctr["st"] += 1

            lda1 = sem("palda1")

            @block.sync
            def _(sync):
                for dst, srcx in ((costab, tab1_in), (sintab, tab2_in), (dmaskp, dmask_in)):
                    sync.dma_start(dst[:], srcx).then_inc(lda1.h, 16)
                    lda1.inc(16)
                sync.wait_ge(lda1.h, lda1.n)

            @block.gpsimd
            def _(gpsimd):
                for v in range(3):
                    q_load(gpsimd, v)
            barrier()

            aparts = 3 if "a" in phases else (2 if "Y" in phases else (1 if "X" in phases else 0))
            for q4 in (range(4) if aparts else []):
                tagq = f"a{q4}"
                def emit_rope_proj(tag, w_src, b_src, dst_sb, with_scale, pi):
                    with ExitStack() as pp:
                        psb = lambda name, shp, dt: pp.enter_context(nc.sbuf_tensor(name, shp, dt))
                        qpb = [pp.enter_context(nc.psum_tensor(f"{tagq}{tag}qpb{i}", [128, 1024], BF16))
                               for i in range(2)]
                        wa = psb(tagq + tag + "wa", [128, ND, 512], F32)
                        bt = psb(tagq + tag + "bt", [128, 512], F32)
                        ttg = psb(tagq + tag + "ttg", [128, 8, 512], F32)
                        rA = Vsb[:, 0, 0:256]
                        rB = Vsb[:, 0, 256:512]
                        rC = Vsb[:, 1, 0:256]
                        rD = Vsb[:, 1, 256:512]
                        vpitch = Vsb[:].ap[0][0]
                        lhsb2 = [bass.AP(Vsb, (2 + 4 * i) * 512, [[vpitch, 128], [128, ND], [1, 128]])
                                 for i in range(2)]


                        ldw = sem(tagq + tag + "ldw", "Pldw")
                        ldr = [sem(tagq + tag + "ldr0", "Pldr0"), sem(tagq + tag + "ldr1", "Pldr1")]
                        ta = sem(tagq + tag + "ta", "Pta")
                        veA = sem(tagq + tag + "veA", "PveA")
                        tf = sem(tagq + tag + "tf", "Ptf")
                        se = sem(tagq + tag + "se", "Pse")

                        @block.sync
                        def _(sync, q4=q4):
                            src_w = bass.AP(w_src.tensor, q4 * 512, [[D, 128], [128 * D, ND], [1, 512]])
                            sync.dma_start(wa[:], src_w).then_inc(ldw.h, 16)
                            ldw.inc(16)
                            sync.dma_start(bt[:], b_src[:, q4 * 512:(q4 + 1) * 512]).then_inc(ldw.h, 16)
                            ldw.inc(16)
                            for tt in range(NT):
                                par = tt % 2
                                if tt >= 2:
                                    sync.wait_ge(ta.h, ta.at(tt - 1))
                                srcl = bass.AP(ln1T, tt * D * 128, [[128, 128], [128 * 128, ND], [1, 128]])
                                sync.dma_start(lhsb2[par], srcl).then_inc(ldr[par].h, 16)
                                ldr[par].inc(16)

                        @block.vector
                        def _(vector, q4=q4):
                            vector.wait_ge(ldw.h, ldw.at(32))
                            if with_scale:
                                vector.tensor_scalar(bt[:], bt[:], SCALE, None, op0=ALU.mult)
                            for tt in range(NT):
                                g, ti = divmod(tt, 4)
                                slot = (g % 2) * 4
                                vector.wait_ge(ta.h, ta.at(tt + 1))
                                if tt % 4 == 0 and tt >= 8:
                                    vector.wait_ge(tf.h, tf.at(4 * (g - 1)))
                                if with_scale:
                                    vector.scalar_tensor_tensor(ttg[:, slot + ti, :], pb[tt % 2][:], SCALE,
                                                                bt[:], op0=ALU.mult, op1=ALU.add)
                                else:
                                    vector.tensor_tensor(ttg[:, slot + ti, :], pb[tt % 2][:], bt[:], op=ALU.add)
                                if ti == 3:
                                    for ti2 in range(4):
                                        tt2 = 4 * g + ti2
                                        base = ttg[:, slot + ti2, :]
                                        pitch = ttg[:].ap[0][0]
                                        off = base.offset
                                        x1v = bass.AP(ttg, off, [[pitch, 128], [128, 4], [1, 64]])
                                        x2v = bass.AP(ttg, off + 64, [[pitch, 128], [128, 4], [1, 64]])
                                        cpitch = costab[:].ap[0][0]
                                        c3 = bass.AP(costab, tt2 * 64, [[cpitch, 128], [0, 4], [1, 64]])
                                        s3 = bass.AP(sintab, tt2 * 64, [[cpitch, 128], [0, 4], [1, 64]])
                                        a4 = rA.rearrange("p (h k) -> p h k", k=64)
                                        b4 = rB.rearrange("p (h k) -> p h k", k=64)
                                        c4 = rC.rearrange("p (h k) -> p h k", k=64)
                                        d4 = rD.rearrange("p (h k) -> p h k", k=64)
                                        vector.tensor_tensor(a4, x1v, c3, op=ALU.mult)
                                        vector.tensor_tensor(b4, x2v, s3, op=ALU.mult)
                                        vector.tensor_tensor(c4, x1v, s3, op=ALU.mult)
                                        vector.tensor_tensor(d4, x2v, c3, op=ALU.mult)
                                        vector.tensor_tensor(x1v, a4, b4, op=ALU.subtract)
                                        vector.tensor_tensor(x2v, c4, d4, op=ALU.add)
                                vector.drain().then_inc(veA.h, 1)
                                veA.inc()
                                q_vector_slot(vector, qpb)

                        @block.tensor
                        def _(tensor, q4=q4):
                            tensor.wait_ge(ldw.h, ldw.at(32))

                            def emit_proj_g(tensor, g):
                                for ti in range(4):
                                    tt = 4 * g + ti
                                    par = tt % 2
                                    tensor.wait_ge(ldr[par].h, ldr[par].at(16 * (tt // 2 + 1)))
                                    if tt >= 2:
                                        tensor.wait_ge(veA.h, veA.at(tt - 1))
                                    lb = lhsb2[par]
                                    for dc in range(ND):
                                        mm = tensor.matmul(pb[tt % 2][:],
                                                           bass.AP(Vsb, lb.offset + dc * 128, [[vpitch, 128], [1, 128]]),
                                                           wa[:, dc, :],
                                                           start=(dc == 0), stop=(dc == ND - 1))
                                    mm.then_inc(ta.h, 1)
                                    ta.inc()
                                    q_tensor_upto(tensor, 16 * pi + tt - 6, qpb)

                            def emit_tr_g(tensor, g):
                                slot = (g % 2) * 4
                                tensor.wait_ge(veA.h, veA.at(4 * (g + 1)))
                                for hh in range(4):
                                    fill = 4 * g + hh
                                    if fill >= 2:
                                        tensor.wait_ge(se.h, se.at(fill - 1))
                                    for ti in range(4):
                                        mm = tensor.transpose(pb[4 + hh % 2][:, ti * 128:(ti + 1) * 128],
                                                              ttg[:, slot + ti, hh * 128:(hh + 1) * 128], idf[:])
                                    mm.then_inc(tf.h, 1)
                                    tf.inc()

                            emit_proj_g(tensor, 0)
                            for g in range(4):
                                if g + 1 < 4:
                                    emit_proj_g(tensor, g + 1)
                                emit_tr_g(tensor, g)
                            q_tensor_upto(tensor, 16 * pi + 9, qpb)

                        @block.scalar
                        def _(scalar, q4=q4):
                            for g in range(4):
                                for hh in range(4):
                                    fill = 4 * g + hh
                                    scalar.wait_ge(tf.h, tf.at(fill + 1))
                                    scalar.copy(dst_sb[:, hh, 4 * g * 128:4 * g * 128 + 512],
                                                pb[4 + hh % 2][:]).then_inc(se.h, 1)
                                    se.inc()

                        @block.gpsimd
                        def _(gpsimd, q4=q4):
                            q_gp_pass(gpsimd, 16 * pi + 12, 16 * pi + 9)
                        barrier()

                emit_rope_proj("K", wkp, bk_in, Ksb, False, 3 * q4)
                emit_rope_proj("Q", wqp, bq_in, Qsb, True, 3 * q4 + 1)
                if aparts < 2:
                    continue

                with ExitStack() as pp:
                    psb = lambda name, shp, dt: pp.enter_context(nc.sbuf_tensor(name, shp, dt))
                    vpi = 3 * q4 + 2
                    vlast = (q4 == 3)
                    qpbv = [pp.enter_context(nc.psum_tensor(f"{tagq}vqpb{i}", [128, 1024], BF16))
                            for i in range(2)]
                    wva = psb(tagq + "wva", [128, ND, 512], F32)
                    bvq = psb(tagq + "bvq", [128, 512], F32)
                    lhs = [psb(f"{tagq}lhs{i}", [128, ND, 128], F32) for i in range(2)]

                    ldw2 = sem(tagq + "ldw2", "Aldw2")
                    ldl = [sem(tagq + "ldl0", "Aldl0"), sem(tagq + "ldl1", "Aldl1")]
                    t2 = sem(tagq + "t2", "At2")
                    ve2 = sem(tagq + "ve2", "Ave2")

                    @block.sync
                    def _(sync, q4=q4):
                        src_v = bass.AP(wvp.tensor, q4 * 512, [[D, 128], [128 * D, ND], [1, 512]])
                        sync.dma_start(wva[:], src_v).then_inc(ldw2.h, 16)
                        ldw2.inc(16)
                        sync.dma_start(bvq[:], bvt_in[:, q4 * 512:(q4 + 1) * 512]).then_inc(ldw2.h, 16)
                        ldw2.inc(16)
                        for tt in range(NT):
                            par = tt % 2
                            if tt >= 2:
                                sync.wait_ge(t2.h, t2.at(tt - 1))
                            src = bass.AP(ln1T, tt * D * 128, [[128, 128], [128 * 128, ND], [1, 128]])
                            sync.dma_start(lhs[par][:], src).then_inc(ldl[par].h, 16)
                            ldl[par].inc(16)

                    @block.tensor
                    def _(tensor, q4=q4):
                        tensor.wait_ge(ldw2.h, ldw2.at(32))
                        for tt in range(NT):
                            par = tt % 2
                            tensor.wait_ge(ldl[par].h, ldl[par].at(16 * (tt // 2 + 1)))
                            if tt >= 4:
                                tensor.wait_ge(ve2.h, ve2.at(tt - 3))
                            for dc in range(ND):
                                mm = tensor.matmul(pb[tt % 4][:], lhs[par][:, dc, :], wva[:, dc, :],
                                                   start=(dc == 0), stop=(dc == ND - 1))
                            mm.then_inc(t2.h, 1)
                            t2.inc()
                            q_tensor_upto(tensor, 16 * vpi + tt - 6, qpbv)

                    @block.vector
                    def _(vector, q4=q4):
                        for tt in range(NT):
                            vector.wait_ge(t2.h, t2.at(tt + 1))
                            vector.tensor_tensor(Vsb[:, tt, :], pb[tt % 4][:], bvq[:],
                                                 op=ALU.add).then_inc(ve2.h, 1)
                            ve2.inc()
                            q_vector_slot(vector, qpbv)
                        if vlast:
                            for _ in range(QLAG):
                                q_vector_slot(vector, qpbv)
                            q_vecevac_upto(vector, 192, qpbv)

                    @block.tensor
                    def _(tensor, q4=q4):
                        q_tensor_upto(tensor, 192 if vlast else 16 * vpi + 9, qpbv)

                    @block.gpsimd
                    def _(gpsimd, q4=q4):
                        q_gp_pass(gpsimd, 192 if vlast else 16 * vpi + 12,
                                  192 if vlast else 16 * vpi + 9)
                    barrier()

                with ExitStack() as pp:
                    psb = lambda name, shp, dt: pp.enter_context(nc.sbuf_tensor(name, shp, dt))
                    scb2 = [psb(tagq + f"scb{i}", [128, S], F32) for i in range(2)]
                    prb2 = [psb(tagq + f"prb{i}", [128, S], F32) for i in range(2)]
                    pTb2 = [psb(tagq + f"pTb{i}", [128, NT, 128], F32) for i in range(2)]
                    zqb = psb(tagq + "zqb", [128, NT, 128], F32)
                    px = [pp.enter_context(nc.psum_tensor(f"{tagq}px{i}", [128, 512], F32)) for i in range(2)]

                    ts1 = sem(tagq + "ts1", "Ats1")
                    se1 = sem(tagq + "se1", "Ase1")
                    vm = sem(tagq + "vm", "Avm")
                    sx = sem(tagq + "sx", "Asx")
                    tt1 = sem(tagq + "tt1", "Att1")
                    sp = sem(tagq + "sp", "Asp")
                    tz = sem(tagq + "tz", "Atz")
                    sz = sem(tagq + "sz", "Asz")
                    stz = sem(tagq + "stz", "Astz")

                    def nk(qt):
                        return (qt * 128 + 128 + 511) // 512

                    def ngr(qt):
                        return (qt + 1 + 7) // 8

                    def cum(hh, qt, per):
                        full = sum(per(q) for q in range(NT))
                        return hh * full + sum(per(q) for q in range(qt))

                    def prev_cs(hh, qt):
                        cq = hh * NT + qt
                        pq = cq - 1
                        return cum(pq // NT, pq % NT, nk)

                    def prev_cg(hh, qt):
                        cq = hh * NT + qt
                        pq = cq - 1
                        return cum(pq // NT, pq % NT, ngr)

                    SC = [(4, 5, 6, 7), (10, 11, 12, 13)]

                    @block.tensor
                    def _(tensor, q4=q4):
                        def emit_scores(tensor, hh, qt):
                            kv = (qt + 1) * 128
                            cs = cum(hh, qt, nk)
                            cq = hh * NT + qt
                            p = cq % 2
                            if cq >= 2:
                                tensor.wait_ge(se1.h, se1.at(prev_cs(hh, qt)))
                                pq2 = cq - 2
                                tensor.wait_ge(sp.h, sp.at(cum(pq2 // NT, pq2 % NT, ngr) + ngr(pq2 % NT)))
                            for kc in range(nk(qt)):
                                n = min(512, kv - kc * 512)
                                if kc >= 2:
                                    tensor.wait_ge(se1.h, se1.at(cs + kc - 1))
                                tensor.matmul(pb[2 * p + kc % 2][:, 0:n],
                                              Qsb[:, hh, qt * 128:(qt + 1) * 128],
                                              Ksb[:, hh, kc * 512:kc * 512 + n],
                                              start=True, stop=True).then_inc(ts1.h, 1)
                                ts1.inc()

                        def emit_tp_pv(tensor, hh, qt):
                            cg = cum(hh, qt, ngr)
                            cq = hh * NT + qt
                            p = cq % 2
                            scb = scb2[p]
                            pTb = pTb2[p]
                            tensor.wait_ge(vm.h, vm.at(3 * cq + 3))
                            if cq >= 2:
                                tensor.wait_ge(sp.h, sp.at(prev_cg(hh, qt)))
                            done, g = 0, 0
                            while done < qt + 1:
                                take = min(8, qt + 1 - done)
                                if g >= 1:
                                    tensor.wait_ge(sp.h, sp.at(cg + g))
                                for j in range(take):
                                    kb = done + j
                                    bank = px[p] if j < 4 else pb[2 * p + 1]
                                    mm = tensor.transpose(bank[:, (j % 4) * 128:(j % 4 + 1) * 128],
                                                          scb[:, kb * 128:(kb + 1) * 128], idf[:])
                                mm.then_inc(tt1.h, 1)
                                tt1.inc()
                                done += take
                                g += 1
                            tensor.wait_ge(sp.h, sp.at(cg + ngr(qt)))
                            if cq >= 2:
                                tensor.wait_ge(sz.h, sz.at(cq - 1))
                            for kb in range(qt + 1):
                                mm = tensor.matmul(pb[4 + p][:, 0:128],
                                                   Vsb[:, kb, hh * 128:(hh + 1) * 128],
                                                   pTb[:, kb, :], start=(kb == 0), stop=(kb == qt))
                            mm.then_inc(tz.h, 1)
                            tz.inc()

                        iters = [(hh, qt) for hh in range(4) for qt in range(NT)]
                        emit_scores(tensor, *iters[0])
                        for idx, (hh, qt) in enumerate(iters):
                            if idx + 1 < len(iters):
                                emit_scores(tensor, *iters[idx + 1])
                            emit_tp_pv(tensor, hh, qt)

                    @block.scalar
                    def _(scalar, q4=q4):
                        def emit_evacs(scalar, hh, qt):
                            kv = (qt + 1) * 128
                            cs = cum(hh, qt, nk)
                            cq = hh * NT + qt
                            p = cq % 2
                            scb = scb2[p]
                            if cq >= 2:
                                scalar.wait_ge(vm.h, vm.at(3 * (cq - 1)))
                            if cq >= 2:
                                pq2 = cq - 2
                                scalar.wait_ge(tt1.h, tt1.at(cum(pq2 // NT, pq2 % NT, ngr) + ngr(pq2 % NT)))
                            for kc in range(nk(qt) - 1):
                                n = 512
                                scalar.wait_ge(ts1.h, ts1.at(cs + kc + 1))
                                scalar.copy(scb[:, kc * 512:kc * 512 + n],
                                            pb[2 * p + kc % 2][:, 0:n]).then_inc(se1.h, 1)
                                se1.inc()

                        def emit_rest(scalar, hh, qt):
                            kv = (qt + 1) * 128
                            cg = cum(hh, qt, ngr)
                            cq = hh * NT + qt
                            p = cq % 2
                            scb = scb2[p]
                            prb = prb2[p]
                            pTb = pTb2[p]
                            mx, nmx, sm, rc = SC[p]
                            scalar.wait_ge(vm.h, vm.at(3 * cq + 1))
                            scalar.activation(prb[:, 0:kv], scb[:, 0:kv], AF.Exp,
                                              bias=stat[:, nmx:nmx + 1],
                                              accum_out=stat[:, sm:sm + 1]).then_inc(sx.h, 1)
                            sx.inc()
                            done, g = 0, 0
                            while done < qt + 1:
                                take = min(8, qt + 1 - done)
                                scalar.wait_ge(tt1.h, tt1.at(cg + g + 1))
                                if done == 0 and cq >= 2:
                                    scalar.wait_ge(tz.h, tz.at(cq - 1))
                                t1_ = min(take, 4)
                                scalar.copy(pTb[:, done:done + t1_, :],
                                            px[p][:, 0:t1_ * 128])
                                if take > 4:
                                    scalar.copy(pTb[:, done + 4:done + take, :],
                                                pb[2 * p + 1][:, 0:(take - 4) * 128])
                                scalar.drain().then_inc(sp.h, 1)
                                sp.inc()
                                done += take
                                g += 1
                            scalar.wait_ge(tz.h, tz.at(cq + 1))
                            if qt == 0:
                                scalar.wait_ge(stz.h, stz.at(16 * hh))
                            scalar.copy(zqb[:, qt, :], pb[4 + p][:, 0:128]).then_inc(sz.h, 1)
                            sz.inc()

                        iters = [(hh, qt) for hh in range(4) for qt in range(NT)]
                        emit_evacs(scalar, *iters[0])
                        for idx, (hh, qt) in enumerate(iters):
                            if idx + 1 < len(iters):
                                emit_evacs(scalar, *iters[idx + 1])
                            emit_rest(scalar, hh, qt)

                    @block.vector
                    def _(vector, q4=q4):
                        for hh in range(4):
                            for qt in range(NT):
                                kv = (qt + 1) * 128
                                cs = cum(hh, qt, nk)
                                cq = hh * NT + qt
                                p = cq % 2
                                scb = scb2[p]
                                prb = prb2[p]
                                mx, nmx, sm, rc = SC[p]
                                nkq = nk(qt)
                                lastn = kv - (nkq - 1) * 512
                                mxb = 16 + 4 * p
                                for kc in range(nkq - 1):
                                    vector.wait_ge(se1.h, se1.at(cs + kc + 1))
                                    vector.tensor_reduce(stat[:, mxb + kc:mxb + kc + 1],
                                                         scb[:, kc * 512:(kc + 1) * 512],
                                                         axis=AX.X, op=ALU.max)
                                vector.wait_ge(ts1.h, ts1.at(cs + nkq))
                                if cq >= 2:
                                    pq2 = cq - 2
                                    vector.wait_ge(tt1.h, tt1.at(cum(pq2 // NT, pq2 % NT, ngr) + ngr(pq2 % NT)))
                                vector.tensor_tensor(scb[:, (nkq - 1) * 512:kv],
                                                     pb[2 * p + (nkq - 1) % 2][:, 0:lastn],
                                                     dmaskp[:, 640 - lastn:640], op=ALU.add).then_inc(se1.h, 1)
                                se1.inc()
                                if nkq == 1:
                                    vector.tensor_reduce(stat[:, nmx:nmx + 1], scb[:, 0:kv], axis=AX.X,
                                                         op=ALU.max, negate=True)
                                else:
                                    vector.tensor_reduce(stat[:, mxb + nkq - 1:mxb + nkq],
                                                         scb[:, (nkq - 1) * 512:kv],
                                                         axis=AX.X, op=ALU.max)
                                    vector.drain()
                                    vector.tensor_reduce(stat[:, nmx:nmx + 1], stat[:, mxb:mxb + nkq],
                                                         axis=AX.X, op=ALU.max, negate=True)
                                vector.drain()
                                vector.drain().then_inc(vm.h, 1)
                                vm.inc()
                                vector.wait_ge(sx.h, sx.at(cq + 1))
                                vector.reciprocal(stat[:, rc:rc + 1], stat[:, sm:sm + 1])
                                vector.drain()
                                vector.drain().then_inc(vm.h, 1)
                                vm.inc()
                                vector.tensor_scalar(scb[:, 0:kv], prb[:, 0:kv], stat[:, rc:rc + 1],
                                                     None, op0=ALU.mult).then_inc(vm.h, 1)
                                vm.inc()

                    @block.gpsimd
                    def _(gpsimd, q4=q4):
                        for hh in range(4):
                            gpsimd.wait_ge(sz.h, sz.at(NT * (hh + 1)))
                            h = q4 * 4 + hh
                            dst = bass.AP(zT, h * 128 * S, [[S, 128], [1, S]])
                            gpsimd.dma_start(dst, zqb[:].rearrange("p a b -> p (a b)")).then_inc(stz.h, 16)
                            stz.inc(16)
                        gpsimd.wait_ge(stz.h, stz.n)
                    barrier()

        if "b" in phases:
            with ExitStack() as ph:
                psb = lambda name, shp, dt: ph.enter_context(nc.sbuf_tensor(name, shp, dt))
                wos = psb("wos", [128, ND, D], F32)
                botc = psb("botc", [128, D], F32)
                zc = [psb(f"zc{i}", [128, ND, 128], F32) for i in range(2)]
                xtb = psb("pbx0", [128, D], F32)
                att = psb("att", [128, D], F32)
                x1b = psb("x1b", [128, D], F32)
                tmpb = psb("pbt1", [128, D], F32)
                sqb2 = psb("pbsq", [128, 1024], F32)
                lnh = psb("lnh", [128, ND, 128], BF16)
                lnl = psb("lnl", [128, ND, 128], BF16)

                ldo = sem("pbldo")
                ldz = [sem("pbldz0"), sem("pbldz1")]
                ldx2 = sem("pbldx0")
                tw = sem("pbtw")
                vev = sem("pbvev")
                vs1 = sem("pbvs1")
                ss1 = sem("pbss1")
                vmn = sem("pbvmn")
                s2q = sem("pbs2q")
                vx1 = sem("pbvx1")
                vln = sem("pbvln")
                tt2 = sem("pbtt2")
                sh = sem("pbsh")
                vl = sem("pbvl")
                stx = sem("pbstx")
                stn = sem("pbstn")

                @block.sync
                def _(sync):
                    src_o = bass.AP(wo_in.tensor, 0, [[D, 128], [128 * D, ND], [1, D]])
                    sync.dma_start(wos[:], src_o).then_inc(ldo.h, 16)
                    ldo.inc(16)
                    sync.dma_start(botc[:], bot_in).then_inc(ldo.h, 16)
                    ldo.inc(16)
                    for qt in range(NT):
                        par = qt % 2
                        if qt >= 2:
                            sync.wait_ge(tw.h, 4 * (qt - 1))
                        if qt >= 1:
                            sync.wait_ge(vx1.h, qt)
                        src = bass.AP(zT, qt * 128, [[S, 128], [128 * S, ND], [1, 128]])
                        sync.dma_start(zc[par][:], src).then_inc(ldz[par].h, 16)
                        ldz[par].inc(16)
                        sync.dma_start(xtb[:], x_in[qt * 128:(qt + 1) * 128, :]).then_inc(ldx2.h, 16)
                        ldx2.inc(16)

                @block.tensor
                def _(tensor):
                    tensor.wait_ge(ldo.h, 32)

                    def emit_wo(tensor, qt):
                        par = qt % 2
                        tensor.wait_ge(ldz[par].h, 16 * (qt // 2 + 1))
                        for dcol in range(4):
                            if qt >= 1:
                                tensor.wait_ge(vev.h, 4 * (qt - 1) + dcol + 1)
                            for dc in range(ND):
                                mm = tensor.matmul(pb[dcol][:], zc[par][:, dc, :],
                                                   wos[:, dc, dcol * 512:(dcol + 1) * 512],
                                                   start=(dc == 0), stop=(dc == ND - 1))
                            mm.then_inc(tw.h, 1)
                            tw.inc()

                    def emit_ln2t(tensor, qt):
                        tensor.wait_ge(vln.h, qt + 1)
                        for g in range(4):
                            if 4 * qt + g >= 2:
                                tensor.wait_ge(vl.h, 4 * qt + g - 1)
                            for j in range(4):
                                dc = 4 * g + j
                                mm = tensor.transpose(pb[4 + g % 2][:, j * 128:(j + 1) * 128],
                                                      tmpb[:, dc * 128:(dc + 1) * 128], idf[:])
                            mm.then_inc(tt2.h, 1)
                            tt2.inc()

                    emit_wo(tensor, 0)
                    for qt in range(NT):
                        if qt + 1 < NT:
                            emit_wo(tensor, qt + 1)
                        emit_ln2t(tensor, qt)

                @block.vector
                def _(vector):
                    for qt in range(NT):
                        par = qt % 2
                        for dcol in range(4):
                            vector.wait_ge(tw.h, 4 * qt + dcol + 1)
                            vector.tensor_tensor(att[:, dcol * 512:(dcol + 1) * 512], pb[dcol][:],
                                                 botc[:, dcol * 512:(dcol + 1) * 512],
                                                 op=ALU.add).then_inc(vev.h, 1)
                            vev.inc()
                        if qt >= 1:
                            vector.wait_ge(tt2.h, 4 * qt)
                        vector.tensor_reduce(stat[:, 0:1], att[:], axis=AX.X, op=ALU.add)
                        vector.drain()
                        vector.tensor_scalar(stat[:, 1:2], stat[:, 0:1], -1.0 / D, None, op0=ALU.mult)
                        vector.drain()
                        vector.drain().then_inc(vmn.h, 1)
                        vmn.inc()
                        vector.tensor_scalar(tmpb[:], att[:], stat[:, 1:2], None, op0=ALU.add)
                        vector.wait_ge(s2q.h, 2 * qt + 1)
                        vector.tensor_tensor(stat[:, 2:3], stat[:, 2:3], stat[:, 10:11], op=ALU.add)
                        vector.drain()
                        vector.drain().then_inc(vs1.h, 1)
                        vs1.inc()
                        vector.wait_ge(ss1.h, 2 * qt + 1)
                        vector.reciprocal(stat[:, 4:5], stat[:, 3:4])
                        vector.drain()
                        vector.tensor_scalar(att[:], tmpb[:], stat[:, 4:5], None, op0=ALU.mult)
                        vector.wait_ge(ldx2.h, 16 * (qt + 1))
                        if qt >= 1:
                            vector.wait_ge(stx.h, 16 * qt)
                        vector.tensor_tensor(x1b[:], xtb[:], att[:], op=ALU.add).then_inc(vx1.h, 1)
                        vx1.inc()
                        vector.tensor_reduce(stat[:, 0:1], x1b[:], axis=AX.X, op=ALU.add)
                        vector.drain()
                        vector.tensor_scalar(stat[:, 1:2], stat[:, 0:1], -1.0 / D, None, op0=ALU.mult)
                        vector.drain()
                        vector.drain().then_inc(vmn.h, 1)
                        vmn.inc()
                        vector.tensor_scalar(tmpb[:], x1b[:], stat[:, 1:2], None, op0=ALU.add)
                        vector.wait_ge(s2q.h, 2 * qt + 2)
                        vector.tensor_tensor(stat[:, 2:3], stat[:, 2:3], stat[:, 10:11], op=ALU.add)
                        vector.drain()
                        vector.drain().then_inc(vs1.h, 1)
                        vs1.inc()
                        vector.wait_ge(ss1.h, 2 * qt + 2)
                        vector.reciprocal(stat[:, 4:5], stat[:, 3:4])
                        vector.drain()
                        vector.tensor_scalar(tmpb[:], tmpb[:], stat[:, 4:5], None,
                                             op0=ALU.mult).then_inc(vln.h, 1)
                        vln.inc()
                        for g in range(4):
                            vector.wait_ge(sh.h, 4 * qt + g + 1)
                            hi = lnh[:, 4 * g:4 * (g + 1), :].rearrange("p a b -> p (a b)")
                            lo = lnl[:, 4 * g:4 * (g + 1), :].rearrange("p a b -> p (a b)")
                            vector.scalar_tensor_tensor(lo, pb[4 + g % 2][:], 1.0, hi,
                                                        op0=ALU.mult, op1=ALU.subtract).then_inc(vl.h, 1)
                            vl.inc()

                @block.scalar
                def _(scalar):
                    for qt in range(NT):
                        for k in range(2):
                            buf = att if k == 0 else x1b
                            scalar.wait_ge(vmn.h, 2 * qt + k + 1)
                            scalar.activation(sqb2[:], buf[:, 0:1024], AF.Square,
                                              bias=stat[:, 1:2], accum_out=stat[:, 2:3])
                            scalar.activation(sqb2[:], buf[:, 1024:2048], AF.Square,
                                              bias=stat[:, 1:2],
                                              accum_out=stat[:, 10:11]).then_inc(s2q.h, 1)
                            s2q.inc()
                            scalar.wait_ge(vs1.h, 2 * qt + k + 1)
                            scalar.activation(stat[:, 3:4], stat[:, 2:3], AF.Sqrt,
                                              bias=stat[:, 8:9], scale=1.0 / D).then_inc(ss1.h, 1)
                            ss1.inc()
                        for g in range(4):
                            scalar.wait_ge(tt2.h, 4 * qt + g + 1)
                            if g == 0:
                                scalar.wait_ge(stn.h, 32 * qt)
                            scalar.copy(lnh[:, 4 * g:4 * (g + 1), :], pb[4 + g % 2][:]).then_inc(sh.h, 1)
                            sh.inc()

                @block.gpsimd
                def _(gpsimd):
                    for qt in range(NT):
                        gpsimd.wait_ge(vx1.h, qt + 1)
                        gpsimd.dma_start(bass.AP(x1S, qt * 128 * D, [[D, 128], [1, D]]),
                                         x1b[:]).then_inc(stx.h, 16)
                        stx.inc(16)
                        gpsimd.wait_ge(vl.h, 4 * (qt + 1))
                        dsth = bass.AP(ln2Th, qt * D * 128, [[128, 128], [128 * 128, ND], [1, 128]])
                        dstl = bass.AP(ln2Tl, qt * D * 128, [[128, 128], [128 * 128, ND], [1, 128]])
                        gpsimd.dma_start(dsth, lnh[:]).then_inc(stn.h, 16)
                        stn.inc(16)
                        gpsimd.dma_start(dstl, lnl[:]).then_inc(stn.h, 16)
                        stn.inc(16)
                    gpsimd.wait_ge(stx.h, stx.n)
                    gpsimd.wait_ge(stn.h, stn.n)
                barrier()

        if "c" in phases:
            with ExitStack() as ph:
                psb = lambda name, shp, dt: ph.enter_context(nc.sbuf_tensor(name, shp, dt))
                l2h = psb("l2h", [128, ND, 512], BF16)
                l2l = psb("l2l", [128, ND, 512], BF16)
                hTh = psb("hTh", [128, 16, 512], BF16)
                hTl = psb("hTl", [128, 16, 512], BF16)
                yT = psb("yT", [128, ND, 512], F32)
                wgb = [psb(f"wgb{i}", [128, ND, 128], BF16) for i in range(2)]
                wub = [psb(f"wub{i}", [128, ND, 128], BF16) for i in range(2)]
                wpb = [psb(f"wpb{i}", [128, 16, 128], BF16) for i in range(2)]
                sgb = psb("sgb", [128, 512], F32)
                hfb = psb("hfb", [128, 512], F32)
                hfb2 = psb("hfb2", [128, 512], F32)
                h2bb2 = [psb(f"h2bb{i}", [128, 512], BF16) for i in range(2)]
                rr = psb("rr", [1, 1024], F32)
                invc = psb("invc", [128, 4], F32)
                ysb = psb("ysb", [128, D], F32)
                x1t = [psb(f"x1t{i}", [128, D], F32) for i in range(2)]
                osb = [psb(f"osb{i}", [128, D], F32) for i in range(2)]
                tmpc = psb("pct1", [128, D], F32)
                sqc = psb("pcsq", [128, D], F32)
                pcy = [ph.enter_context(nc.psum_tensor(f"pcy{i}", [128, 512], F32)) for i in range(2)]

                for tc in range(4):
                    tg = f"c{tc}_"
                    ldln = sem(tg + "ldln", "Cldln")
                    ldg = [sem(tg + "ldg0", "Cldg0"), sem(tg + "ldg1", "Cldg1")]
                    ldp = [sem(tg + "ldp0", "Cldp0"), sem(tg + "ldp1", "Cldp1")]
                    ldx1 = [sem(tg + "ldx10", "Cldx10"), sem(tg + "ldx11", "Cldx11")]
                    tgu = sem(tg + "tgu", "Ctgu")
                    ssg = sem(tg + "ssg", "Cssg")
                    vh = sem(tg + "vh", "Cvh")
                    sh2 = sem(tg + "sh2", "Csh2")
                    vlo = sem(tg + "vlo", "Cvlo")
                    tp1 = sem(tg + "tp1", "Ctp1")
                    vy = sem(tg + "vy", "Cvy")
                    trm = sem(tg + "trm", "Ctrm")
                    vrm = sem(tg + "vrm", "Cvrm")
                    srm = sem(tg + "srm", "Csrm")
                    vri = sem(tg + "vri", "Cvri")
                    tiv = sem(tg + "tiv", "Ctiv")
                    siv = sem(tg + "siv", "Csiv")
                    tyt = sem(tg + "tyt", "Ctyt")
                    vye = sem(tg + "vye", "Cvye")
                    vo = sem(tg + "vo", "Cvo")
                    sto = sem(tg + "sto", "Csto")

                    @block.sync
                    def _(sync, tc=tc):
                        for s in range(4):
                            srch = bass.AP(ln2Th, (tc * 4 + s) * D * 128,
                                           [[128, 128], [128 * 128, ND], [1, 128]])
                            srcl = bass.AP(ln2Tl, (tc * 4 + s) * D * 128,
                                           [[128, 128], [128 * 128, ND], [1, 128]])
                            sync.dma_start(l2h[:, :, s * 128:(s + 1) * 128], srch).then_inc(ldln.h, 16)
                            ldln.inc(16)
                            sync.dma_start(l2l[:, :, s * 128:(s + 1) * 128], srcl).then_inc(ldln.h, 16)
                            ldln.inc(16)
                        for hg in range(4):
                            for hc in range(16):
                                j = hg * 16 + hc
                                par = j % 2
                                if j >= 2:
                                    sync.wait_ge(tgu.h, tgu.at(j - 1))
                                hcg = hg * 16 + hc
                                srcg = bass.AP(wgT, hcg * 128 * ND * 128,
                                               [[ND * 128, 128], [128, ND], [1, 128]])
                                srcu = bass.AP(wgT, (64 + hcg) * 128 * ND * 128,
                                               [[ND * 128, 128], [128, ND], [1, 128]])
                                sync.dma_start(wgb[par][:], srcg).then_inc(ldg[par].h, 16)
                                sync.dma_start(wub[par][:], srcu).then_inc(ldg[par].h, 16)
                                ldg[par].inc(32)
                            for dd in range(16):
                                k = hg * 16 + dd
                                par = k % 2
                                if k >= 2:
                                    sync.wait_ge(tp1.h, tp1.at(k - 1))
                                srcp = bass.AP(wpT, dd * 128 * 64 * 128 + hg * 16 * 128,
                                               [[64 * 128, 128], [128, 16], [1, 128]])
                                sync.dma_start(wpb[par][:], srcp).then_inc(ldp[par].h, 16)
                                ldp[par].inc(16)
                        for tt in range(4):
                            par = tt % 2
                            if tt >= 2:
                                sync.wait_ge(vo.h, vo.at(tt - 1))
                            t_glob = tc * 4 + tt
                            sync.dma_start(x1t[par][:],
                                           bass.AP(x1S, t_glob * 128 * D, [[D, 128], [1, D]])).then_inc(ldx1[par].h, 16)
                            ldx1[par].inc(16)

                    def vye_expected(gi_done):
                        tt_full, g_in = divmod(gi_done, 4)
                        return tt_full * 5 + g_in

                    @block.tensor
                    def _(tensor, tc=tc):
                        tensor.wait_ge(ldln.h, ldln.at(128))
                        for hg in range(4):
                            for hc in range(16):
                                j = hg * 16 + hc
                                par = j % 2
                                gb = pb[0] if par == 0 else pcy[0]
                                ub = pb[1] if par == 0 else pcy[1]
                                tensor.wait_ge(ldg[par].h, ldg[par].at(32 * (j // 2 + 1)))
                                if j >= 2:
                                    tensor.wait_ge(vh.h, vh.at(j - 1))
                                if j >= 2:
                                    tensor.wait_ge(sh2.h, sh2.at(j - 1))
                                    tensor.matmul(pb[2][0:1, :], onesb[:], h2bb2[(j - 2) % 2][:],
                                                  start=(j == 2), stop=False, skip_group_check=True)
                                for dc in range(ND):
                                    tensor.matmul(gb[:], wgb[par][:, dc, :], l2h[:, dc, :],
                                                  start=(dc == 0), stop=False, skip_group_check=True)
                                    tensor.matmul(gb[:], wgb[par][:, dc, :], l2l[:, dc, :],
                                                  start=False, stop=(dc == ND - 1), skip_group_check=True)
                                for dc in range(ND):
                                    tensor.matmul(ub[:], wub[par][:, dc, :], l2h[:, dc, :],
                                                  start=(dc == 0), stop=False, skip_group_check=True)
                                    mm = tensor.matmul(ub[:], wub[par][:, dc, :], l2l[:, dc, :],
                                                       start=False, stop=(dc == ND - 1), skip_group_check=True)
                                mm.then_inc(tgu.h, 1)
                                tgu.inc()
                            for dd in range(ND):
                                k = hg * 16 + dd
                                par = k % 2
                                tensor.wait_ge(ldp[par].h, ldp[par].at(16 * (k // 2 + 1)))
                                tensor.wait_ge(vlo.h, vlo.at(hg * 16 + 16))
                                if k >= 2:
                                    tensor.wait_ge(vy.h, vy.at(k - 1))
                                for hc in range(16):
                                    tensor.matmul(pb[4 + k % 2][:], wpb[par][:, hc, :], hTh[:, hc, :],
                                                  start=(hc == 0), stop=False, skip_group_check=True)
                                    mm = tensor.matmul(pb[4 + k % 2][:], wpb[par][:, hc, :], hTl[:, hc, :],
                                                       start=False, stop=(hc == 15), skip_group_check=True)
                                mm.then_inc(tp1.h, 1)
                                tp1.inc()
                        tensor.wait_ge(sh2.h, sh2.at(63))
                        tensor.matmul(pb[2][0:1, :], onesb[:], h2bb2[0][:],
                                      start=False, stop=False, skip_group_check=True)
                        tensor.wait_ge(sh2.h, sh2.at(64))
                        tensor.matmul(pb[2][0:1, :], onesb[:], h2bb2[1][:],
                                      start=False, stop=True, skip_group_check=True).then_inc(trm.h, 1)
                        trm.inc()
                        tensor.wait_ge(vri.h, vri.at(1))
                        for ttj in range(4):
                            mm = tensor.transpose(pb[3][:, ttj:ttj + 1], rr[0:1, ttj * 128:(ttj + 1) * 128],
                                                  idf[0:1, 0:1])
                        mm.then_inc(tiv.h, 1)
                        tiv.inc()
                        tensor.wait_ge(vy.h, vy.at(64))
                        for tt in range(4):
                            for g in range(4):
                                gi = 4 * tt + g
                                if gi >= 2:
                                    tensor.wait_ge(vye.h, vye.at(vye_expected(gi - 1)))
                                for j in range(4):
                                    dd = 4 * g + j
                                    mm = tensor.transpose(pcy[g % 2][:, j * 128:(j + 1) * 128],
                                                          yT[:, dd, tt * 128:(tt + 1) * 128], idf[:])
                                mm.then_inc(tyt.h, 1)
                                tyt.inc()

                    @block.scalar
                    def _(scalar, tc=tc):
                        for j in range(64):
                            hg, hc = divmod(j, 16)
                            scalar.wait_ge(tgu.h, tgu.at(j + 1))
                            gbs = pb[0] if j % 2 == 0 else pcy[0]
                            scalar.activation(sgb[:], gbs[:], AF.Sigmoid).then_inc(ssg.h, 1)
                            ssg.inc()
                            scalar.wait_ge(vh.h, vh.at(j + 1))
                            if hg >= 1:
                                scalar.wait_ge(tp1.h, tp1.at(hg * 16))
                            scalar.copy(hTh[:, hc, :], hfb2[:])
                            scalar.activation(h2bb2[j % 2][:], hfb2[:], AF.Square).then_inc(sh2.h, 1)
                            sh2.inc()
                        scalar.wait_ge(vrm.h, vrm.at(1))
                        scalar.activation(rr[0:1, 512:1024], rr[0:1, 0:512], AF.Sqrt).then_inc(srm.h, 1)
                        srm.inc()
                        scalar.wait_ge(tiv.h, tiv.at(1))
                        scalar.copy(invc[:, 0:4], pb[3][:, 0:4]).then_inc(siv.h, 1)
                        siv.inc()
                        for tt in range(4):
                            scalar.wait_ge(vo.h, vo.at(tt))
                            scalar.wait_ge(vye.h, vye.at(5 * tt + 5))
                            scalar.activation(stat[:, 3:4], stat[:, 2:3], AF.Sqrt,
                                              bias=stat[:, 8:9], scale=1.0 / D).then_inc(srm.h, 1)
                            srm.inc()

                    @block.vector
                    def _(vector, tc=tc):
                        for hg in range(4):
                            for hc in range(16):
                                j = hg * 16 + hc
                                vector.wait_ge(ssg.h, ssg.at(j + 1))
                                gbv = pb[0] if j % 2 == 0 else pcy[0]
                                ubv = pb[1] if j % 2 == 0 else pcy[1]
                                vector.tensor_tensor(hfb[:], sgb[:], gbv[:], op=ALU.mult)
                                vector.tensor_tensor(hfb2[:], hfb[:], ubv[:], op=ALU.mult).then_inc(vh.h, 1)
                                vh.inc()
                                vector.wait_ge(sh2.h, sh2.at(j + 1))
                                if hg >= 1:
                                    vector.wait_ge(tp1.h, tp1.at(hg * 16))
                                vector.scalar_tensor_tensor(hTl[:, hc, :], hfb2[:], 1.0, hTh[:, hc, :],
                                                            op0=ALU.mult, op1=ALU.subtract).then_inc(vlo.h, 1)
                                vlo.inc()
                            for dd in range(ND):
                                k = hg * 16 + dd
                                vector.wait_ge(tp1.h, tp1.at(k + 1))
                                if hg == 0:
                                    vector.tensor_copy(yT[:, dd, :], pb[4 + k % 2][:]).then_inc(vy.h, 1)
                                else:
                                    vector.tensor_tensor(yT[:, dd, :], yT[:, dd, :],
                                                         pb[4 + k % 2][:], op=ALU.add).then_inc(vy.h, 1)
                                vy.inc()
                        vector.wait_ge(trm.h, trm.at(1))
                        vector.tensor_scalar(rr[0:1, 0:512], pb[2][0:1, :], 1.0 / HFF, EPS_RMS,
                                             op0=ALU.mult, op1=ALU.add).then_inc(vrm.h, 1)
                        vrm.inc()
                        vector.wait_ge(srm.h, srm.at(1))
                        vector.reciprocal(rr[0:1, 0:512], rr[0:1, 512:1024])
                        vector.drain()
                        vector.drain().then_inc(vri.h, 1)
                        vri.inc()
                        vector.wait_ge(siv.h, siv.at(1))
                        for tt in range(4):
                            par = tt % 2
                            for g in range(4):
                                gi = 4 * tt + g
                                vector.wait_ge(tyt.h, tyt.at(gi + 1))
                                vector.tensor_scalar(ysb[:, g * 512:(g + 1) * 512], pcy[g % 2][:],
                                                     invc[:, tt:tt + 1], None,
                                                     op0=ALU.mult).then_inc(vye.h, 1)
                                vye.inc()
                            vector.tensor_reduce(stat[:, 0:1], ysb[:], axis=AX.X, op=ALU.add)
                            vector.drain()
                            vector.tensor_scalar(stat[:, 1:2], stat[:, 0:1], 1.0 / D, None, op0=ALU.mult)
                            vector.drain()
                            vector.tensor_scalar(tmpc[:], ysb[:], stat[:, 1:2], None, op0=ALU.subtract)
                            vector.tensor_tensor(sqc[:], tmpc[:], tmpc[:], op=ALU.mult)
                            vector.tensor_reduce(stat[:, 2:3], sqc[:], axis=AX.X, op=ALU.add)
                            vector.drain()
                            vector.drain().then_inc(vye.h, 1)
                            vye.inc()
                            vector.wait_ge(srm.h, srm.at(tt + 2))
                            vector.reciprocal(stat[:, 4:5], stat[:, 3:4])
                            vector.drain()
                            vector.tensor_scalar(tmpc[:], tmpc[:], stat[:, 4:5], None, op0=ALU.mult)
                            vector.wait_ge(ldx1[par].h, ldx1[par].at(16 * (tt // 2 + 1)))
                            if tt >= 2:
                                vector.wait_ge(sto.h, sto.at(16 * (tt - 1)))
                            vector.tensor_tensor(osb[par][:], x1t[par][:], tmpc[:],
                                                 op=ALU.add).then_inc(vo.h, 1)
                            vo.inc()

                    @block.gpsimd
                    def _(gpsimd, tc=tc):
                        for tt in range(4):
                            par = tt % 2
                            gpsimd.wait_ge(vo.h, vo.at(tt + 1))
                            t_glob = tc * 4 + tt
                            gpsimd.dma_start(out[t_glob * 128:(t_glob + 1) * 128, :],
                                             osb[par][:]).then_inc(sto.h, 16)
                            sto.inc(16)
                        gpsimd.wait_ge(sto.h, sto.n)
                    barrier()

    return nc


_ROPE_THETA = 10000.0


def _host_layouts(inputs):
    x = np.asarray(inputs["L0_attn_normalized_resid_pre"], dtype=np.float32)
    WQ = np.asarray(inputs["L0_attn_W_Q"], dtype=np.float32)
    WK = np.asarray(inputs["L0_attn_W_K"], dtype=np.float32)
    WV = np.asarray(inputs["L0_attn_W_V"], dtype=np.float32)
    WO = np.asarray(inputs["L0_attn_W_O"], dtype=np.float32)
    bQ = np.asarray(inputs["L0_attn_b_Q"], dtype=np.float32)
    bK = np.asarray(inputs["L0_attn_b_K"], dtype=np.float32)
    bV = np.asarray(inputs["L0_attn_b_V"], dtype=np.float32)
    bO = np.asarray(inputs["L0_attn_b_O"], dtype=np.float32)
    wgu = np.ascontiguousarray(np.asarray(inputs["L0_ffn_w_gate_up"], dtype=np.float32))
    wpr = np.ascontiguousarray(np.asarray(inputs["L0_ffn_w_proj"], dtype=np.float32))

    perm = np.concatenate([np.arange(0, DH, 2), np.arange(1, DH, 2)])
    wqp = np.ascontiguousarray(WQ[:, :, perm].transpose(1, 0, 2).reshape(D, D))
    wkp = np.ascontiguousarray(WK[:, :, perm].transpose(1, 0, 2).reshape(D, D))
    wvp = np.ascontiguousarray(WV.transpose(1, 0, 2).reshape(D, D))
    wo = np.ascontiguousarray(WO.reshape(D, D))
    bq = np.ascontiguousarray(np.broadcast_to(bQ[:, perm].reshape(1, D), (128, D)))
    bk = np.ascontiguousarray(np.broadcast_to(bK[:, perm].reshape(1, D), (128, D)))
    bvt = np.ascontiguousarray(np.broadcast_to(bV.reshape(1, D), (128, D)))
    bot = np.ascontiguousarray(np.broadcast_to(bO.reshape(1, D), (128, D)))

    inv_freq = (1.0 / _ROPE_THETA ** (np.arange(0, DH, 2, dtype=np.float32) / DH)).astype(np.float32)
    m = np.arange(S, dtype=np.float32)
    freqs = np.outer(m, inv_freq).astype(np.float32)
    cosf = np.cos(freqs).astype(np.float32)
    sinf = np.sin(freqs).astype(np.float32)
    tab1 = np.ascontiguousarray(cosf.reshape(NT, 128, 64).transpose(1, 0, 2).reshape(128, NT * 64))
    tab2 = np.ascontiguousarray(sinf.reshape(NT, 128, 64).transpose(1, 0, 2).reshape(128, NT * 64))

    identf = np.eye(128, dtype=np.float32)
    identb = np.eye(128, dtype=ml_dtypes.bfloat16)
    dmask = np.zeros((128, 640), dtype=np.float32)
    dmask[:, 512:640] = np.triu(np.full((128, 128), -1.0e30, dtype=np.float32), 1)
    onesb = np.ones((128, 1), dtype=ml_dtypes.bfloat16)

    const = dict(wqp=wqp, wkp=wkp, wvp=wvp, wo=wo, wgu=wgu, wpr=wpr,
                 bq=bq, bk=bk, bvt=bvt, bot=bot, tab1=tab1, tab2=tab2,
                 identf=identf, identb=identb, dmask=dmask, onesb=onesb)
    return x, const


def kernel(**inputs):
    x, const = _host_layouts(inputs)
    nc = build_program()
    in_maps = [dict(const, x=np.ascontiguousarray(x[b])) for b in range(8)]
    res = run_bass_kernel_spmd(nc, in_maps, core_ids=list(range(8)))
    return np.stack([np.asarray(r["out"]) for r in res.results], axis=0)


# version 17
# speedup vs baseline: 1.1037x; 1.0810x over previous
import math
from contextlib import ExitStack

import numpy as np
import ml_dtypes
import concourse.bass as bass
import concourse.mybir as mybir
from concourse.bass_utils import run_bass_kernel_spmd

F32 = mybir.dt.float32
BF16 = mybir.dt.bfloat16
AF = mybir.ActivationFunctionType
ALU = mybir.AluOpType
AX = mybir.AxisListType

S = 2048
D = 2048
H = 16
DH = 128
NT = S // 128
ND = D // 128
HFF = 8192
NHC = 2 * HFF // 128
SCALE = 1.0 / math.sqrt(DH)
EPS_LN = 1e-5
EPS_RMS = float(np.finfo(np.float32).eps)


class _Pool:
    def __init__(self, h):
        self.h = h
        self.n = 0


class Sem:

    def __init__(self, pool):
        self.p = pool
        self.base = pool.n

    @property
    def h(self):
        return self.p.h

    @property
    def n(self):
        return self.p.n

    def inc(self, k=1):
        self.p.n += k
        return self.p.n

    def at(self, v):
        return self.base + v


def build_program(phases="0qabc"):
    nc = bass.Bass("TRN2", target_bir_lowering=False)

    x_in = nc.dram_tensor("x", [S, D], F32, kind="ExternalInput").ap()
    wqp = nc.dram_tensor("wqp", [D, D], F32, kind="ExternalInput").ap()
    wkp = nc.dram_tensor("wkp", [D, D], F32, kind="ExternalInput").ap()
    wvp = nc.dram_tensor("wvp", [D, D], F32, kind="ExternalInput").ap()
    wo_in = nc.dram_tensor("wo", [D, D], F32, kind="ExternalInput").ap()
    wgu_in = nc.dram_tensor("wgu", [2 * HFF, D], F32, kind="ExternalInput").ap()
    wpr_in = nc.dram_tensor("wpr", [D, HFF], F32, kind="ExternalInput").ap()
    bq_in = nc.dram_tensor("bq", [128, D], F32, kind="ExternalInput").ap()
    bk_in = nc.dram_tensor("bk", [128, D], F32, kind="ExternalInput").ap()
    bvt_in = nc.dram_tensor("bvt", [128, D], F32, kind="ExternalInput").ap()
    bot_in = nc.dram_tensor("bot", [128, D], F32, kind="ExternalInput").ap()
    tab1_in = nc.dram_tensor("tab1", [128, NT * 64], F32, kind="ExternalInput").ap()
    tab2_in = nc.dram_tensor("tab2", [128, NT * 64], F32, kind="ExternalInput").ap()
    idf_in = nc.dram_tensor("identf", [128, 128], F32, kind="ExternalInput").ap()
    idb_in = nc.dram_tensor("identb", [128, 128], BF16, kind="ExternalInput").ap()
    dmask_in = nc.dram_tensor("dmask", [128, 640], F32, kind="ExternalInput").ap()
    ones_in = nc.dram_tensor("onesb", [128, 1], BF16, kind="ExternalInput").ap()

    out = nc.dram_tensor("out", [S, D], F32, kind="ExternalOutput").ap()

    ln1T = nc.dram_tensor("ln1T", [NT, D, 128], F32, kind="Internal")
    zT = nc.dram_tensor("zT", [D, S], F32, kind="Internal")
    x1S = nc.dram_tensor("x1S", [S, D], F32, kind="Internal")
    ln2Th = nc.dram_tensor("ln2Th", [NT, D, 128], BF16, kind="Internal")
    ln2Tl = nc.dram_tensor("ln2Tl", [NT, D, 128], BF16, kind="Internal")
    wgT = nc.dram_tensor("wgT", [NHC, 128, ND, 128], BF16, kind="Internal")
    wpT = nc.dram_tensor("wpT", [ND, 128, 64, 128], BF16, kind="Internal")

    with ExitStack() as top:
        tsb = lambda name, shp, dt: top.enter_context(nc.sbuf_tensor(name, shp, dt))

        idf = tsb("idf", [128, 128], F32)
        idb = tsb("idb", [128, 128], BF16)
        onesb = tsb("onesc", [128, 1], BF16)
        stat = tsb("stat", [128, 24], F32)

        pb = [top.enter_context(nc.psum_tensor(f"pb{i}", [128, 512], F32)) for i in range(6)]

        pools = {}

        def sem(name, pool=None):
            key = pool or name
            if key not in pools:
                pools[key] = _Pool(top.enter_context(nc.semaphore(key)))
            return Sem(pools[key])

        BAR = sem("bar")
        block = top.enter_context(nc.Block())

        def barrier():
            tgt = BAR.n + 5
            for name in ("sync", "vector", "scalar", "tensor", "gpsimd"):
                @getattr(block, name)
                def _(eng, name=name):
                    if name in ("vector", "scalar", "tensor"):
                        eng.drain().then_inc(BAR.h, 1)
                    else:
                        eng.nop().then_inc(BAR.h, 1)
                    eng.wait_ge(BAR.h, tgt)
            BAR.inc(5)

        ldc = sem("ldc")

        @block.sync
        def _(sync):
            for dst, src in ((idf, idf_in), (idb, idb_in), (onesb, ones_in)):
                sync.dma_start(dst[:], src).then_inc(ldc.h, 16)
                ldc.inc(16)
            sync.wait_ge(ldc.h, ldc.n)

        @block.vector
        def _(vector):
            vector.memset(stat[:, 8:9], EPS_LN)
        barrier()

        with ExitStack() as ph:
            psb = lambda name, shp, dt: ph.enter_context(nc.sbuf_tensor(name, shp, dt))
            xt = [psb(f"p0x{i}", [128, D], F32) for i in range(2)]
            lnst = [psb(f"p0st{i}", [128, ND, 128], F32) for i in range(2)]
            tmp1 = psb("p0t1", [128, D], F32)
            tmp2 = psb("p0t2", [128, D], F32)
            sqb = psb("p0sq", [128, D], F32)

            ldx = [sem("p0ldx0"), sem("p0ldx1")]
            v1 = sem("p0v1")
            s1 = sem("p0s1")
            v0m = sem("p0v0m")
            s0q = sem("p0s0q")
            v2 = sem("p0v2")
            t1 = sem("p0t1s")
            se = sem("p0se")
            stl = [sem("p0stl0"), sem("p0stl1")]

            @block.sync
            def _(sync):
                for tt in range(NT):
                    par = tt % 2
                    if tt >= 2:
                        sync.wait_ge(v1.h, tt - 1)
                    sync.dma_start(xt[par][:], x_in[tt * 128:(tt + 1) * 128, :]).then_inc(ldx[par].h, 16)
                    ldx[par].inc(16)

            @block.vector
            def _(vector):
                for tt in range(NT):
                    par = tt % 2
                    vector.wait_ge(ldx[par].h, 16 * (tt // 2 + 1))
                    vector.tensor_reduce(stat[:, 0:1], xt[par][:], axis=AX.X, op=ALU.add)
                    vector.drain()
                    vector.tensor_scalar(stat[:, 1:2], stat[:, 0:1], -1.0 / D, None, op0=ALU.mult)
                    vector.drain()
                    vector.drain().then_inc(v0m.h, 1)
                    v0m.inc()
                    vector.tensor_scalar(tmp1[:], xt[par][:], stat[:, 1:2], None, op0=ALU.add)
                    vector.wait_ge(s0q.h, tt + 1)
                    vector.tensor_tensor(stat[:, 2:3], stat[:, 2:3], stat[:, 10:11], op=ALU.add)
                    vector.drain()
                    vector.drain().then_inc(v1.h, 1)
                    v1.inc()
                    vector.wait_ge(s1.h, tt + 1)
                    vector.reciprocal(stat[:, 4:5], stat[:, 3:4])
                    vector.drain()
                    if tt >= 1:
                        vector.wait_ge(t1.h, 4 * tt)
                    vector.tensor_scalar(tmp2[:], tmp1[:], stat[:, 4:5], None, op0=ALU.mult).then_inc(v2.h, 1)
                    v2.inc()

            @block.scalar
            def _(scalar):
                for tt in range(NT):
                    par = tt % 2
                    scalar.wait_ge(v0m.h, tt + 1)
                    scalar.activation(sqb[:, 0:1024], xt[par][:, 0:1024], AF.Square,
                                      bias=stat[:, 1:2], accum_out=stat[:, 2:3])
                    scalar.activation(sqb[:, 0:1024], xt[par][:, 1024:2048], AF.Square,
                                      bias=stat[:, 1:2],
                                      accum_out=stat[:, 10:11]).then_inc(s0q.h, 1)
                    s0q.inc()
                    scalar.wait_ge(v1.h, tt + 1)
                    scalar.activation(stat[:, 3:4], stat[:, 2:3], AF.Sqrt,
                                      bias=stat[:, 8:9], scale=1.0 / D).then_inc(s1.h, 1)
                    s1.inc()
                    par = tt % 2
                    for g in range(4):
                        scalar.wait_ge(t1.h, 4 * tt + g + 1)
                        if g == 0:
                            scalar.wait_ge(stl[par].h, 16 * (tt // 2))
                        scalar.copy(lnst[par][:, 4 * g:4 * (g + 1), :], pb[g % 2][:]).then_inc(se.h, 1)
                        se.inc()

            @block.tensor
            def _(tensor):
                for tt in range(NT):
                    tensor.wait_ge(v2.h, tt + 1)
                    for g in range(4):
                        if 4 * tt + g >= 2:
                            tensor.wait_ge(se.h, 4 * tt + g - 1)
                        for j in range(4):
                            dc = 4 * g + j
                            mm = tensor.transpose(pb[g % 2][:, j * 128:(j + 1) * 128],
                                                  tmp2[:, dc * 128:(dc + 1) * 128], idf[:])
                        mm.then_inc(t1.h, 1)
                        t1.inc()

            @block.gpsimd
            def _(gpsimd):
                for tt in range(NT):
                    par = tt % 2
                    gpsimd.wait_ge(se.h, 4 * (tt + 1))
                    dst = bass.AP(ln1T, tt * D * 128, [[128, 128], [128 * 128, ND], [1, 128]])
                    gpsimd.dma_start(dst, lnst[par][:]).then_inc(stl[par].h, 16)
                    stl[par].inc(16)
                gpsimd.wait_ge(stl[0].h, stl[0].n)
                gpsimd.wait_ge(stl[1].h, stl[1].n)
            barrier()

        with ExitStack() as pa:
            asb = lambda name, shp, dt: pa.enter_context(nc.sbuf_tensor(name, shp, dt))
            Ksb = asb("Ksb", [128, 4, S], F32)
            Qsb = asb("Qsb", [128, 4, S], F32)
            Vsb = asb("Vsb", [128, NT, 512], F32)
            costab = asb("costab", [128, NT, 64], F32)
            sintab = asb("sintab", [128, NT, 64], F32)
            dmaskp = asb("dmaskp", [128, 640], F32)

            qwb4 = [asb(f"qwb{i}", [128, D], BF16) for i in range(3)]
            qt_ = asb("qt", [128, D], BF16)
            qg1_2 = [asb(f"qg1{i}", [128, D], BF16) for i in range(4)]
            qg2 = asb("qg2", [128, D], BF16)
            qsc = asb("qsc", [128, 32], BF16)
            qrec = asb("qrec", [128, 32], F32)
            qstg = asb("qstg", [128, ND, 128], BF16)
            QL = [sem(f"ql{i}", f"QL{i}") for i in range(3)]
            Qv = sem("qv", "Qv")
            Qt = sem("qt_s", "Qt")
            Qs = sem("qs", "Qs")
            Qst = sem("qst", "Qst")

            def unit_aps(u):
                if u < NHC:
                    srca = bass.AP(wgu_in.tensor, u * 128 * D, [[D, 128], [1, D]])
                    dsta = bass.AP(wgT, u * 128 * ND * 128, [[ND * 128, 128], [128, ND], [1, 128]])
                else:
                    v = u - NHC
                    dd, h4 = divmod(v, 4)
                    srca = bass.AP(wpr_in.tensor, dd * 128 * HFF + h4 * 2048, [[HFF, 128], [1, 2048]])
                    dsta = bass.AP(wpT, dd * 128 * 64 * 128 + h4 * 16 * 128,
                                   [[64 * 128, 128], [128, 16], [1, 128]])
                return srca, dsta

            def q_load(gpsimd, v):
                if v >= 192:
                    return
                if v >= 3:
                    gpsimd.wait_ge(Qv.h, Qv.at(v - 2))
                srca, _ = unit_aps(v)
                gpsimd.dma_start(qwb4[v % 3][:], srca).then_inc(QL[v % 3].h, 16)
                QL[v % 3].inc(16)

            def q_vector(vector, u):
                par = u % 3
                qwb = qwb4[par]
                qg1 = qg1_2[u % 4]
                vector.wait_ge(QL[par].h, QL[par].at(16 * (u // 3 + 1)))
                if u >= 4:
                    vector.wait_ge(Qt.h, Qt.at(2 * (u - 3)))
                wb3 = qwb[:].rearrange("p (g k) -> p g k", k=64)
                vector.tensor_reduce(qrec[:, 0:32], wb3, axis=AX.X, op=ALU.add,
                                     apply_absolute_value=True)
                vector.drain()
                vector.tensor_scalar(qsc[:, 0:32], qrec[:, 0:32], 1.0 / 64, 1e-8,
                                     op0=ALU.mult, op1=ALU.max)
                vector.drain()
                vector.reciprocal(qrec[:, 0:32], qsc[:, 0:32])
                vector.drain()
                rec3 = bass.AP(qrec, 0, [[qrec[:].ap[0][0], 128], [1, 32], [0, 64]])
                sc3 = bass.AP(qsc, 0, [[qsc[:].ap[0][0], 128], [1, 32], [0, 64]])
                t3 = qt_[:].rearrange("p (g k) -> p g k", k=64)
                vector.tensor_tensor(t3, wb3, rec3, op=ALU.mult)
                vector.tensor_scalar(qg2[:], qt_[:], -0.5, None, op0=ALU.is_lt)
                vector.scalar_tensor_tensor(qg1[:], qt_[:], 0.5, qg2[:],
                                            op0=ALU.is_gt, op1=ALU.subtract)
                g13 = qg1[:].rearrange("p (g k) -> p g k", k=64)
                qt3 = qt_[:].rearrange("p (g k) -> p g k", k=64)
                vector.tensor_tensor(qt3, g13, sc3, op=ALU.mult)
                vector.tensor_tensor(qg2[:], qt_[:], qwb[:], op=ALU.subtract)
                vector.tensor_tensor(qg1[:], qwb[:], qg2[:], op=ALU.add).then_inc(Qv.h, 1)
                Qv.inc()

            def q_tensor(tensor, u, qpb):
                qg1 = qg1_2[u % 4]
                tensor.wait_ge(Qv.h, Qv.at(u + 1))
                for g in range(2):
                    if u >= 1:
                        tensor.wait_ge(Qs.h, Qs.at(2 * (u - 1) + g + 1))
                    for j in range(8):
                        blk = 8 * g + j
                        mm = tensor.transpose(qpb[g][:, j * 128:(j + 1) * 128],
                                              qg1[:, blk * 128:(blk + 1) * 128], idb[:])
                    mm.then_inc(Qt.h, 1)
                    Qt.inc()

            def q_vecevac(vector, u, qpb):
                for g in range(2):
                    vector.wait_ge(Qt.h, Qt.at(2 * u + g + 1))
                    if g == 0:
                        vector.wait_ge(Qst.h, Qst.at(16 * u))
                    vector.tensor_copy(qstg[:, 8 * g:8 * (g + 1), :],
                                       qpb[g][:]).then_inc(Qs.h, 1)
                    Qs.inc()

            def q_store(gpsimd, u):
                gpsimd.wait_ge(Qs.h, Qs.at(2 * (u + 1)))
                _, dsta = unit_aps(u)
                gpsimd.dma_start(dsta, qstg[:]).then_inc(Qst.h, 16)
                Qst.inc(16)

            qctr = {"vslot": 0, "t": 0, "s": 0, "ld": 3, "st": 0}

            QLAG = 6

            def q_vector_slot(vector, qpb):
                u = qctr["vslot"] - QLAG
                qctr["vslot"] += 1
                if 0 <= u < 192:
                    q_vector(vector, u)
                q_vecevac_upto(vector, qctr["vslot"] - 7, qpb)

            def q_tensor_upto(tensor, target, qpb):
                while qctr["t"] < min(target, 192):
                    q_tensor(tensor, qctr["t"], qpb)
                    qctr["t"] += 1

            def q_vecevac_upto(vector, target, qpb):
                while qctr["s"] < min(target, 192):
                    q_vecevac(vector, qctr["s"], qpb)
                    qctr["s"] += 1

            def q_gp_pass(gpsimd, ld_target, st_target):
                ld_target = min(ld_target, 192)
                st_target = min(st_target, 192)
                while qctr["ld"] < ld_target or qctr["st"] < st_target:
                    if qctr["ld"] < ld_target:
                        q_load(gpsimd, qctr["ld"])
                        qctr["ld"] += 1
                    if qctr["st"] < st_target:
                        q_store(gpsimd, qctr["st"])
                        qctr["st"] += 1

            lda1 = sem("palda1")

            @block.sync
            def _(sync):
                for dst, srcx in ((costab, tab1_in), (sintab, tab2_in), (dmaskp, dmask_in)):
                    sync.dma_start(dst[:], srcx).then_inc(lda1.h, 16)
                    lda1.inc(16)
                sync.wait_ge(lda1.h, lda1.n)

            @block.gpsimd
            def _(gpsimd):
                for v in range(3):
                    q_load(gpsimd, v)
            barrier()

            aparts = 3 if "a" in phases else (2 if "Y" in phases else (1 if "X" in phases else 0))
            for q4 in (range(4) if aparts else []):
                tagq = f"a{q4}"
                def emit_rope_proj(tag, w_src, b_src, dst_sb, with_scale, pi):
                    with ExitStack() as pp:
                        psb = lambda name, shp, dt: pp.enter_context(nc.sbuf_tensor(name, shp, dt))
                        qpb = [pp.enter_context(nc.psum_tensor(f"{tagq}{tag}qpb{i}", [128, 1024], BF16))
                               for i in range(2)]
                        wa = psb(tagq + tag + "wa", [128, ND, 512], F32)
                        bt = psb(tagq + tag + "bt", [128, 512], F32)
                        ttg = psb(tagq + tag + "ttg", [128, 8, 512], F32)
                        rA = Vsb[:, 0, 0:256]
                        rB = Vsb[:, 0, 256:512]
                        rC = Vsb[:, 1, 0:256]
                        rD = Vsb[:, 1, 256:512]
                        vpitch = Vsb[:].ap[0][0]
                        lhsb2 = [bass.AP(Vsb, (2 + 4 * i) * 512, [[vpitch, 128], [128, ND], [1, 128]])
                                 for i in range(2)]


                        ldw = sem(tagq + tag + "ldw", "Pldw")
                        ldr = [sem(tagq + tag + "ldr0", "Pldr0"), sem(tagq + tag + "ldr1", "Pldr1")]
                        ta = sem(tagq + tag + "ta", "Pta")
                        veA = sem(tagq + tag + "veA", "PveA")
                        tf = sem(tagq + tag + "tf", "Ptf")
                        se = sem(tagq + tag + "se", "Pse")

                        @block.sync
                        def _(sync, q4=q4):
                            src_w = bass.AP(w_src.tensor, q4 * 512, [[D, 128], [128 * D, ND], [1, 512]])
                            sync.dma_start(wa[:], src_w).then_inc(ldw.h, 16)
                            ldw.inc(16)
                            sync.dma_start(bt[:], b_src[:, q4 * 512:(q4 + 1) * 512]).then_inc(ldw.h, 16)
                            ldw.inc(16)
                            for tt in range(NT):
                                par = tt % 2
                                if tt >= 2:
                                    sync.wait_ge(ta.h, ta.at(tt - 1))
                                srcl = bass.AP(ln1T, tt * D * 128, [[128, 128], [128 * 128, ND], [1, 128]])
                                sync.dma_start(lhsb2[par], srcl).then_inc(ldr[par].h, 16)
                                ldr[par].inc(16)

                        @block.vector
                        def _(vector, q4=q4):
                            vector.wait_ge(ldw.h, ldw.at(32))
                            if with_scale:
                                vector.tensor_scalar(bt[:], bt[:], SCALE, None, op0=ALU.mult)
                            for tt in range(NT):
                                g, ti = divmod(tt, 4)
                                slot = (g % 2) * 4
                                vector.wait_ge(ta.h, ta.at(tt + 1))
                                if tt % 4 == 0 and tt >= 8:
                                    vector.wait_ge(tf.h, tf.at(4 * (g - 1)))
                                if with_scale:
                                    vector.scalar_tensor_tensor(ttg[:, slot + ti, :], pb[tt % 2][:], SCALE,
                                                                bt[:], op0=ALU.mult, op1=ALU.add)
                                else:
                                    vector.tensor_tensor(ttg[:, slot + ti, :], pb[tt % 2][:], bt[:], op=ALU.add)
                                if ti == 3:
                                    for ti2 in range(4):
                                        tt2 = 4 * g + ti2
                                        base = ttg[:, slot + ti2, :]
                                        pitch = ttg[:].ap[0][0]
                                        off = base.offset
                                        x1v = bass.AP(ttg, off, [[pitch, 128], [128, 4], [1, 64]])
                                        x2v = bass.AP(ttg, off + 64, [[pitch, 128], [128, 4], [1, 64]])
                                        cpitch = costab[:].ap[0][0]
                                        c3 = bass.AP(costab, tt2 * 64, [[cpitch, 128], [0, 4], [1, 64]])
                                        s3 = bass.AP(sintab, tt2 * 64, [[cpitch, 128], [0, 4], [1, 64]])
                                        a4 = rA.rearrange("p (h k) -> p h k", k=64)
                                        b4 = rB.rearrange("p (h k) -> p h k", k=64)
                                        c4 = rC.rearrange("p (h k) -> p h k", k=64)
                                        d4 = rD.rearrange("p (h k) -> p h k", k=64)
                                        vector.tensor_tensor(a4, x1v, c3, op=ALU.mult)
                                        vector.tensor_tensor(b4, x2v, s3, op=ALU.mult)
                                        vector.tensor_tensor(c4, x1v, s3, op=ALU.mult)
                                        vector.tensor_tensor(d4, x2v, c3, op=ALU.mult)
                                        vector.tensor_tensor(x1v, a4, b4, op=ALU.subtract)
                                        vector.tensor_tensor(x2v, c4, d4, op=ALU.add)
                                vector.drain().then_inc(veA.h, 1)
                                veA.inc()
                                q_vector_slot(vector, qpb)

                        @block.tensor
                        def _(tensor, q4=q4):
                            tensor.wait_ge(ldw.h, ldw.at(32))

                            def emit_proj_g(tensor, g):
                                for ti in range(4):
                                    tt = 4 * g + ti
                                    par = tt % 2
                                    tensor.wait_ge(ldr[par].h, ldr[par].at(16 * (tt // 2 + 1)))
                                    if tt >= 2:
                                        tensor.wait_ge(veA.h, veA.at(tt - 1))
                                    lb = lhsb2[par]
                                    for dc in range(ND):
                                        mm = tensor.matmul(pb[tt % 2][:],
                                                           bass.AP(Vsb, lb.offset + dc * 128, [[vpitch, 128], [1, 128]]),
                                                           wa[:, dc, :],
                                                           start=(dc == 0), stop=(dc == ND - 1))
                                    mm.then_inc(ta.h, 1)
                                    ta.inc()
                                    q_tensor_upto(tensor, 16 * pi + tt - 6, qpb)

                            def emit_tr_g(tensor, g):
                                slot = (g % 2) * 4
                                tensor.wait_ge(veA.h, veA.at(4 * (g + 1)))
                                for hh in range(4):
                                    fill = 4 * g + hh
                                    if fill >= 2:
                                        tensor.wait_ge(se.h, se.at(fill - 1))
                                    for ti in range(4):
                                        mm = tensor.transpose(pb[4 + hh % 2][:, ti * 128:(ti + 1) * 128],
                                                              ttg[:, slot + ti, hh * 128:(hh + 1) * 128], idf[:])
                                    mm.then_inc(tf.h, 1)
                                    tf.inc()

                            emit_proj_g(tensor, 0)
                            for g in range(4):
                                if g + 1 < 4:
                                    emit_proj_g(tensor, g + 1)
                                emit_tr_g(tensor, g)
                            q_tensor_upto(tensor, 16 * pi + 9, qpb)

                        @block.scalar
                        def _(scalar, q4=q4):
                            for g in range(4):
                                for hh in range(4):
                                    fill = 4 * g + hh
                                    scalar.wait_ge(tf.h, tf.at(fill + 1))
                                    scalar.copy(dst_sb[:, hh, 4 * g * 128:4 * g * 128 + 512],
                                                pb[4 + hh % 2][:]).then_inc(se.h, 1)
                                    se.inc()

                        @block.gpsimd
                        def _(gpsimd, q4=q4):
                            q_gp_pass(gpsimd, 16 * pi + 12, 16 * pi + 9)
                        barrier()

                emit_rope_proj("K", wkp, bk_in, Ksb, False, 3 * q4)
                emit_rope_proj("Q", wqp, bq_in, Qsb, True, 3 * q4 + 1)
                if aparts < 2:
                    continue

                with ExitStack() as pp:
                    psb = lambda name, shp, dt: pp.enter_context(nc.sbuf_tensor(name, shp, dt))
                    vpi = 3 * q4 + 2
                    vlast = (q4 == 3)
                    qpbv = [pp.enter_context(nc.psum_tensor(f"{tagq}vqpb{i}", [128, 1024], BF16))
                            for i in range(2)]
                    wva = psb(tagq + "wva", [128, ND, 512], F32)
                    bvq = psb(tagq + "bvq", [128, 512], F32)
                    lhs = [psb(f"{tagq}lhs{i}", [128, ND, 128], F32) for i in range(2)]

                    ldw2 = sem(tagq + "ldw2", "Aldw2")
                    ldl = [sem(tagq + "ldl0", "Aldl0"), sem(tagq + "ldl1", "Aldl1")]
                    t2 = sem(tagq + "t2", "At2")
                    ve2 = sem(tagq + "ve2", "Ave2")

                    @block.sync
                    def _(sync, q4=q4):
                        src_v = bass.AP(wvp.tensor, q4 * 512, [[D, 128], [128 * D, ND], [1, 512]])
                        sync.dma_start(wva[:], src_v).then_inc(ldw2.h, 16)
                        ldw2.inc(16)
                        sync.dma_start(bvq[:], bvt_in[:, q4 * 512:(q4 + 1) * 512]).then_inc(ldw2.h, 16)
                        ldw2.inc(16)
                        for tt in range(NT):
                            par = tt % 2
                            if tt >= 2:
                                sync.wait_ge(t2.h, t2.at(tt - 1))
                            src = bass.AP(ln1T, tt * D * 128, [[128, 128], [128 * 128, ND], [1, 128]])
                            sync.dma_start(lhs[par][:], src).then_inc(ldl[par].h, 16)
                            ldl[par].inc(16)

                    @block.tensor
                    def _(tensor, q4=q4):
                        tensor.wait_ge(ldw2.h, ldw2.at(32))
                        for tt in range(NT):
                            par = tt % 2
                            tensor.wait_ge(ldl[par].h, ldl[par].at(16 * (tt // 2 + 1)))
                            if tt >= 4:
                                tensor.wait_ge(ve2.h, ve2.at(tt - 3))
                            for dc in range(ND):
                                mm = tensor.matmul(pb[tt % 4][:], lhs[par][:, dc, :], wva[:, dc, :],
                                                   start=(dc == 0), stop=(dc == ND - 1))
                            mm.then_inc(t2.h, 1)
                            t2.inc()
                            q_tensor_upto(tensor, 16 * vpi + tt - 6, qpbv)

                    @block.vector
                    def _(vector, q4=q4):
                        for tt in range(NT):
                            vector.wait_ge(t2.h, t2.at(tt + 1))
                            vector.tensor_tensor(Vsb[:, tt, :], pb[tt % 4][:], bvq[:],
                                                 op=ALU.add).then_inc(ve2.h, 1)
                            ve2.inc()
                            q_vector_slot(vector, qpbv)
                        if vlast:
                            for _ in range(QLAG):
                                q_vector_slot(vector, qpbv)
                            q_vecevac_upto(vector, 192, qpbv)

                    @block.tensor
                    def _(tensor, q4=q4):
                        q_tensor_upto(tensor, 192 if vlast else 16 * vpi + 9, qpbv)

                    @block.gpsimd
                    def _(gpsimd, q4=q4):
                        q_gp_pass(gpsimd, 192 if vlast else 16 * vpi + 12,
                                  192 if vlast else 16 * vpi + 9)
                    barrier()

                with ExitStack() as pp:
                    psb = lambda name, shp, dt: pp.enter_context(nc.sbuf_tensor(name, shp, dt))
                    scb2 = [psb(tagq + f"scb{i}", [128, S], F32) for i in range(2)]
                    prb2 = [psb(tagq + f"prb{i}", [128, S], F32) for i in range(2)]
                    pTb2 = [psb(tagq + f"pTb{i}", [128, NT, 128], F32) for i in range(2)]
                    zqb = psb(tagq + "zqb", [128, NT, 128], F32)
                    px = [pp.enter_context(nc.psum_tensor(f"{tagq}px{i}", [128, 512], F32)) for i in range(2)]

                    ts1 = sem(tagq + "ts1", "Ats1")
                    se1 = sem(tagq + "se1", "Ase1")
                    vm = sem(tagq + "vm", "Avm")
                    sx = sem(tagq + "sx", "Asx")
                    tt1 = sem(tagq + "tt1", "Att1")
                    sp = sem(tagq + "sp", "Asp")
                    tz = sem(tagq + "tz", "Atz")
                    sz = sem(tagq + "sz", "Asz")
                    stz = sem(tagq + "stz", "Astz")

                    def nk(qt):
                        return (qt * 128 + 128 + 511) // 512

                    def ngr(qt):
                        return (qt + 1 + 7) // 8

                    def cum(hh, qt, per):
                        full = sum(per(q) for q in range(NT))
                        return hh * full + sum(per(q) for q in range(qt))

                    def prev_cs(hh, qt):
                        cq = hh * NT + qt
                        pq = cq - 1
                        return cum(pq // NT, pq % NT, nk)

                    def prev_cg(hh, qt):
                        cq = hh * NT + qt
                        pq = cq - 1
                        return cum(pq // NT, pq % NT, ngr)

                    SC = [(4, 5, 6, 7), (10, 11, 12, 13)]

                    @block.tensor
                    def _(tensor, q4=q4):
                        def emit_scores(tensor, hh, qt):
                            kv = (qt + 1) * 128
                            cs = cum(hh, qt, nk)
                            cq = hh * NT + qt
                            p = cq % 2
                            if cq >= 2:
                                tensor.wait_ge(se1.h, se1.at(prev_cs(hh, qt)))
                                pq2 = cq - 2
                                tensor.wait_ge(sp.h, sp.at(cum(pq2 // NT, pq2 % NT, ngr) + ngr(pq2 % NT)))
                            for kc in range(nk(qt)):
                                n = min(512, kv - kc * 512)
                                if kc >= 2:
                                    tensor.wait_ge(se1.h, se1.at(cs + kc - 1))
                                tensor.matmul(pb[2 * p + kc % 2][:, 0:n],
                                              Qsb[:, hh, qt * 128:(qt + 1) * 128],
                                              Ksb[:, hh, kc * 512:kc * 512 + n],
                                              start=True, stop=True).then_inc(ts1.h, 1)
                                ts1.inc()

                        def emit_tp_pv(tensor, hh, qt):
                            cg = cum(hh, qt, ngr)
                            cq = hh * NT + qt
                            p = cq % 2
                            scb = scb2[p]
                            pTb = pTb2[p]
                            tensor.wait_ge(vm.h, vm.at(3 * cq + 3))
                            if cq >= 2:
                                tensor.wait_ge(sp.h, sp.at(prev_cg(hh, qt)))
                            done, g = 0, 0
                            while done < qt + 1:
                                take = min(8, qt + 1 - done)
                                if g >= 1:
                                    tensor.wait_ge(sp.h, sp.at(cg + g))
                                for j in range(take):
                                    kb = done + j
                                    bank = px[p] if j < 4 else pb[2 * p + 1]
                                    mm = tensor.transpose(bank[:, (j % 4) * 128:(j % 4 + 1) * 128],
                                                          scb[:, kb * 128:(kb + 1) * 128], idf[:])
                                mm.then_inc(tt1.h, 1)
                                tt1.inc()
                                done += take
                                g += 1
                            tensor.wait_ge(sp.h, sp.at(cg + ngr(qt)))
                            if cq >= 2:
                                tensor.wait_ge(sz.h, sz.at(cq - 1))
                            for kb in range(qt + 1):
                                mm = tensor.matmul(pb[4 + p][:, 0:128],
                                                   Vsb[:, kb, hh * 128:(hh + 1) * 128],
                                                   pTb[:, kb, :], start=(kb == 0), stop=(kb == qt))
                            mm.then_inc(tz.h, 1)
                            tz.inc()

                        iters = [(hh, qt) for hh in range(4) for qt in range(NT)]
                        emit_scores(tensor, *iters[0])
                        for idx, (hh, qt) in enumerate(iters):
                            if idx + 1 < len(iters):
                                emit_scores(tensor, *iters[idx + 1])
                            emit_tp_pv(tensor, hh, qt)

                    @block.scalar
                    def _(scalar, q4=q4):
                        def emit_evacs(scalar, hh, qt):
                            kv = (qt + 1) * 128
                            cs = cum(hh, qt, nk)
                            cq = hh * NT + qt
                            p = cq % 2
                            scb = scb2[p]
                            if cq >= 2:
                                scalar.wait_ge(vm.h, vm.at(3 * (cq - 1)))
                            if cq >= 2:
                                pq2 = cq - 2
                                scalar.wait_ge(tt1.h, tt1.at(cum(pq2 // NT, pq2 % NT, ngr) + ngr(pq2 % NT)))
                            for kc in range(nk(qt) - 1):
                                n = 512
                                scalar.wait_ge(ts1.h, ts1.at(cs + kc + 1))
                                scalar.copy(scb[:, kc * 512:kc * 512 + n],
                                            pb[2 * p + kc % 2][:, 0:n]).then_inc(se1.h, 1)
                                se1.inc()

                        def emit_rest(scalar, hh, qt):
                            kv = (qt + 1) * 128
                            cg = cum(hh, qt, ngr)
                            cq = hh * NT + qt
                            p = cq % 2
                            scb = scb2[p]
                            prb = prb2[p]
                            pTb = pTb2[p]
                            mx, nmx, sm, rc = SC[p]
                            scalar.wait_ge(vm.h, vm.at(3 * cq + 1))
                            scalar.activation(prb[:, 0:kv], scb[:, 0:kv], AF.Exp,
                                              bias=stat[:, nmx:nmx + 1],
                                              accum_out=stat[:, sm:sm + 1]).then_inc(sx.h, 1)
                            sx.inc()
                            done, g = 0, 0
                            while done < qt + 1:
                                take = min(8, qt + 1 - done)
                                scalar.wait_ge(tt1.h, tt1.at(cg + g + 1))
                                if done == 0 and cq >= 2:
                                    scalar.wait_ge(tz.h, tz.at(cq - 1))
                                t1_ = min(take, 4)
                                scalar.copy(pTb[:, done:done + t1_, :],
                                            px[p][:, 0:t1_ * 128])
                                if take > 4:
                                    scalar.copy(pTb[:, done + 4:done + take, :],
                                                pb[2 * p + 1][:, 0:(take - 4) * 128])
                                scalar.drain().then_inc(sp.h, 1)
                                sp.inc()
                                done += take
                                g += 1
                            scalar.wait_ge(tz.h, tz.at(cq + 1))
                            if qt == 0:
                                scalar.wait_ge(stz.h, stz.at(16 * hh))
                            scalar.copy(zqb[:, qt, :], pb[4 + p][:, 0:128]).then_inc(sz.h, 1)
                            sz.inc()

                        iters = [(hh, qt) for hh in range(4) for qt in range(NT)]
                        emit_evacs(scalar, *iters[0])
                        for idx, (hh, qt) in enumerate(iters):
                            if idx + 1 < len(iters):
                                emit_evacs(scalar, *iters[idx + 1])
                            emit_rest(scalar, hh, qt)

                    @block.vector
                    def _(vector, q4=q4):
                        for hh in range(4):
                            for qt in range(NT):
                                kv = (qt + 1) * 128
                                cs = cum(hh, qt, nk)
                                cq = hh * NT + qt
                                p = cq % 2
                                scb = scb2[p]
                                prb = prb2[p]
                                mx, nmx, sm, rc = SC[p]
                                nkq = nk(qt)
                                lastn = kv - (nkq - 1) * 512
                                mxb = 16 + 4 * p
                                for kc in range(nkq - 1):
                                    vector.wait_ge(se1.h, se1.at(cs + kc + 1))
                                    vector.tensor_reduce(stat[:, mxb + kc:mxb + kc + 1],
                                                         scb[:, kc * 512:(kc + 1) * 512],
                                                         axis=AX.X, op=ALU.max)
                                vector.wait_ge(ts1.h, ts1.at(cs + nkq))
                                if cq >= 2:
                                    pq2 = cq - 2
                                    vector.wait_ge(tt1.h, tt1.at(cum(pq2 // NT, pq2 % NT, ngr) + ngr(pq2 % NT)))
                                vector.tensor_tensor(scb[:, (nkq - 1) * 512:kv],
                                                     pb[2 * p + (nkq - 1) % 2][:, 0:lastn],
                                                     dmaskp[:, 640 - lastn:640], op=ALU.add).then_inc(se1.h, 1)
                                se1.inc()
                                if nkq == 1:
                                    vector.tensor_reduce(stat[:, nmx:nmx + 1], scb[:, 0:kv], axis=AX.X,
                                                         op=ALU.max, negate=True)
                                else:
                                    vector.tensor_reduce(stat[:, mxb + nkq - 1:mxb + nkq],
                                                         scb[:, (nkq - 1) * 512:kv],
                                                         axis=AX.X, op=ALU.max)
                                    vector.drain()
                                    vector.tensor_reduce(stat[:, nmx:nmx + 1], stat[:, mxb:mxb + nkq],
                                                         axis=AX.X, op=ALU.max, negate=True)
                                vector.drain()
                                vector.drain().then_inc(vm.h, 1)
                                vm.inc()
                                vector.wait_ge(sx.h, sx.at(cq + 1))
                                vector.reciprocal(stat[:, rc:rc + 1], stat[:, sm:sm + 1])
                                vector.drain()
                                vector.drain().then_inc(vm.h, 1)
                                vm.inc()
                                vector.tensor_scalar(scb[:, 0:kv], prb[:, 0:kv], stat[:, rc:rc + 1],
                                                     None, op0=ALU.mult).then_inc(vm.h, 1)
                                vm.inc()

                    @block.gpsimd
                    def _(gpsimd, q4=q4):
                        for hh in range(4):
                            gpsimd.wait_ge(sz.h, sz.at(NT * (hh + 1)))
                            h = q4 * 4 + hh
                            dst = bass.AP(zT, h * 128 * S, [[S, 128], [1, S]])
                            gpsimd.dma_start(dst, zqb[:].rearrange("p a b -> p (a b)")).then_inc(stz.h, 16)
                            stz.inc(16)
                        gpsimd.wait_ge(stz.h, stz.n)
                    barrier()

        if "b" in phases:
            with ExitStack() as ph:
                psb = lambda name, shp, dt: ph.enter_context(nc.sbuf_tensor(name, shp, dt))
                wos = psb("wos", [128, ND, D], F32)
                botc = psb("botc", [128, D], F32)
                zc = [psb(f"zc{i}", [128, ND, 128], F32) for i in range(2)]
                xtb = psb("pbx0", [128, D], F32)
                att = psb("att", [128, D], F32)
                x1b = psb("x1b", [128, D], F32)
                tmpb = psb("pbt1", [128, D], F32)
                sqb2 = psb("pbsq", [128, 1024], F32)
                lnh = psb("lnh", [128, ND, 128], BF16)
                lnl = psb("lnl", [128, ND, 128], BF16)

                ldo = sem("pbldo")
                ldz = [sem("pbldz0"), sem("pbldz1")]
                ldx2 = sem("pbldx0")
                tw = sem("pbtw")
                vev = sem("pbvev")
                vs1 = sem("pbvs1")
                ss1 = sem("pbss1")
                vmn = sem("pbvmn")
                s2q = sem("pbs2q")
                vx1 = sem("pbvx1")
                vln = sem("pbvln")
                tt2 = sem("pbtt2")
                sh = sem("pbsh")
                vl = sem("pbvl")
                stx = sem("pbstx")
                stn = sem("pbstn")

                @block.sync
                def _(sync):
                    src_o = bass.AP(wo_in.tensor, 0, [[D, 128], [128 * D, ND], [1, D]])
                    sync.dma_start(wos[:], src_o).then_inc(ldo.h, 16)
                    ldo.inc(16)
                    sync.dma_start(botc[:], bot_in).then_inc(ldo.h, 16)
                    ldo.inc(16)
                    for qt in range(NT):
                        par = qt % 2
                        if qt >= 2:
                            sync.wait_ge(tw.h, 4 * (qt - 1))
                        if qt >= 1:
                            sync.wait_ge(vx1.h, qt)
                        src = bass.AP(zT, qt * 128, [[S, 128], [128 * S, ND], [1, 128]])
                        sync.dma_start(zc[par][:], src).then_inc(ldz[par].h, 16)
                        ldz[par].inc(16)
                        sync.dma_start(xtb[:], x_in[qt * 128:(qt + 1) * 128, :]).then_inc(ldx2.h, 16)
                        ldx2.inc(16)

                @block.tensor
                def _(tensor):
                    tensor.wait_ge(ldo.h, 32)

                    def emit_wo(tensor, qt):
                        par = qt % 2
                        tensor.wait_ge(ldz[par].h, 16 * (qt // 2 + 1))
                        for dcol in range(4):
                            if qt >= 1:
                                tensor.wait_ge(vev.h, 4 * (qt - 1) + dcol + 1)
                            for dc in range(ND):
                                mm = tensor.matmul(pb[dcol][:], zc[par][:, dc, :],
                                                   wos[:, dc, dcol * 512:(dcol + 1) * 512],
                                                   start=(dc == 0), stop=(dc == ND - 1))
                            mm.then_inc(tw.h, 1)
                            tw.inc()

                    def emit_ln2t(tensor, qt):
                        tensor.wait_ge(vln.h, qt + 1)
                        for g in range(4):
                            if 4 * qt + g >= 2:
                                tensor.wait_ge(vl.h, 4 * qt + g - 1)
                            for j in range(4):
                                dc = 4 * g + j
                                mm = tensor.transpose(pb[4 + g % 2][:, j * 128:(j + 1) * 128],
                                                      tmpb[:, dc * 128:(dc + 1) * 128], idf[:])
                            mm.then_inc(tt2.h, 1)
                            tt2.inc()

                    emit_wo(tensor, 0)
                    for qt in range(NT):
                        if qt + 1 < NT:
                            emit_wo(tensor, qt + 1)
                        emit_ln2t(tensor, qt)

                @block.vector
                def _(vector):
                    for qt in range(NT):
                        par = qt % 2
                        for dcol in range(4):
                            vector.wait_ge(tw.h, 4 * qt + dcol + 1)
                            vector.tensor_tensor(att[:, dcol * 512:(dcol + 1) * 512], pb[dcol][:],
                                                 botc[:, dcol * 512:(dcol + 1) * 512],
                                                 op=ALU.add).then_inc(vev.h, 1)
                            vev.inc()
                        if qt >= 1:
                            vector.wait_ge(tt2.h, 4 * qt)
                        vector.tensor_reduce(stat[:, 0:1], att[:], axis=AX.X, op=ALU.add)
                        vector.drain()
                        vector.tensor_scalar(stat[:, 1:2], stat[:, 0:1], -1.0 / D, None, op0=ALU.mult)
                        vector.drain()
                        vector.drain().then_inc(vmn.h, 1)
                        vmn.inc()
                        vector.tensor_scalar(tmpb[:], att[:], stat[:, 1:2], None, op0=ALU.add)
                        vector.wait_ge(s2q.h, 2 * qt + 1)
                        vector.tensor_tensor(stat[:, 2:3], stat[:, 2:3], stat[:, 10:11], op=ALU.add)
                        vector.drain()
                        vector.drain().then_inc(vs1.h, 1)
                        vs1.inc()
                        vector.wait_ge(ss1.h, 2 * qt + 1)
                        vector.reciprocal(stat[:, 4:5], stat[:, 3:4])
                        vector.drain()
                        vector.tensor_scalar(att[:], tmpb[:], stat[:, 4:5], None, op0=ALU.mult)
                        vector.wait_ge(ldx2.h, 16 * (qt + 1))
                        if qt >= 1:
                            vector.wait_ge(stx.h, 16 * qt)
                        vector.tensor_tensor(x1b[:], xtb[:], att[:], op=ALU.add).then_inc(vx1.h, 1)
                        vx1.inc()
                        vector.tensor_reduce(stat[:, 0:1], x1b[:], axis=AX.X, op=ALU.add)
                        vector.drain()
                        vector.tensor_scalar(stat[:, 1:2], stat[:, 0:1], -1.0 / D, None, op0=ALU.mult)
                        vector.drain()
                        vector.drain().then_inc(vmn.h, 1)
                        vmn.inc()
                        vector.tensor_scalar(tmpb[:], x1b[:], stat[:, 1:2], None, op0=ALU.add)
                        vector.wait_ge(s2q.h, 2 * qt + 2)
                        vector.tensor_tensor(stat[:, 2:3], stat[:, 2:3], stat[:, 10:11], op=ALU.add)
                        vector.drain()
                        vector.drain().then_inc(vs1.h, 1)
                        vs1.inc()
                        vector.wait_ge(ss1.h, 2 * qt + 2)
                        vector.reciprocal(stat[:, 4:5], stat[:, 3:4])
                        vector.drain()
                        vector.tensor_scalar(tmpb[:], tmpb[:], stat[:, 4:5], None,
                                             op0=ALU.mult).then_inc(vln.h, 1)
                        vln.inc()
                        for g in range(4):
                            vector.wait_ge(sh.h, 4 * qt + g + 1)
                            hi = lnh[:, 4 * g:4 * (g + 1), :].rearrange("p a b -> p (a b)")
                            lo = lnl[:, 4 * g:4 * (g + 1), :].rearrange("p a b -> p (a b)")
                            vector.scalar_tensor_tensor(lo, pb[4 + g % 2][:], 1.0, hi,
                                                        op0=ALU.mult, op1=ALU.subtract).then_inc(vl.h, 1)
                            vl.inc()

                @block.scalar
                def _(scalar):
                    for qt in range(NT):
                        for k in range(2):
                            buf = att if k == 0 else x1b
                            scalar.wait_ge(vmn.h, 2 * qt + k + 1)
                            scalar.activation(sqb2[:], buf[:, 0:1024], AF.Square,
                                              bias=stat[:, 1:2], accum_out=stat[:, 2:3])
                            scalar.activation(sqb2[:], buf[:, 1024:2048], AF.Square,
                                              bias=stat[:, 1:2],
                                              accum_out=stat[:, 10:11]).then_inc(s2q.h, 1)
                            s2q.inc()
                            scalar.wait_ge(vs1.h, 2 * qt + k + 1)
                            scalar.activation(stat[:, 3:4], stat[:, 2:3], AF.Sqrt,
                                              bias=stat[:, 8:9], scale=1.0 / D).then_inc(ss1.h, 1)
                            ss1.inc()
                        for g in range(4):
                            scalar.wait_ge(tt2.h, 4 * qt + g + 1)
                            if g == 0:
                                scalar.wait_ge(stn.h, 32 * qt)
                            scalar.copy(lnh[:, 4 * g:4 * (g + 1), :], pb[4 + g % 2][:]).then_inc(sh.h, 1)
                            sh.inc()

                @block.gpsimd
                def _(gpsimd):
                    for qt in range(NT):
                        gpsimd.wait_ge(vx1.h, qt + 1)
                        gpsimd.dma_start(bass.AP(x1S, qt * 128 * D, [[D, 128], [1, D]]),
                                         x1b[:]).then_inc(stx.h, 16)
                        stx.inc(16)
                        gpsimd.wait_ge(vl.h, 4 * (qt + 1))
                        dsth = bass.AP(ln2Th, qt * D * 128, [[128, 128], [128 * 128, ND], [1, 128]])
                        dstl = bass.AP(ln2Tl, qt * D * 128, [[128, 128], [128 * 128, ND], [1, 128]])
                        gpsimd.dma_start(dsth, lnh[:]).then_inc(stn.h, 16)
                        stn.inc(16)
                        gpsimd.dma_start(dstl, lnl[:]).then_inc(stn.h, 16)
                        stn.inc(16)
                    gpsimd.wait_ge(stx.h, stx.n)
                    gpsimd.wait_ge(stn.h, stn.n)
                barrier()

        if "c" in phases:
            with ExitStack() as ph:
                psb = lambda name, shp, dt: ph.enter_context(nc.sbuf_tensor(name, shp, dt))
                l2h = psb("l2h", [128, ND, 512], BF16)
                l2l = psb("l2l", [128, ND, 512], BF16)
                hTh = psb("hTh", [128, 16, 512], BF16)
                hTl = psb("hTl", [128, 16, 512], BF16)
                yT = psb("yT", [128, ND, 512], F32)
                wgb = [psb(f"wgb{i}", [128, ND, 128], BF16) for i in range(2)]
                wub = [psb(f"wub{i}", [128, ND, 128], BF16) for i in range(2)]
                wpb = [psb(f"wpb{i}", [128, 16, 128], BF16) for i in range(2)]
                sgb = psb("sgb", [128, 512], F32)
                hfb = psb("hfb", [128, 512], F32)
                hfb2 = psb("hfb2", [128, 512], F32)
                h2bb2 = [psb(f"h2bb{i}", [128, 512], BF16) for i in range(2)]
                rr = psb("rr", [1, 1024], F32)
                invc = psb("invc", [128, 4], F32)
                ysb = psb("ysb", [128, D], F32)
                x1t = [psb(f"x1t{i}", [128, D], F32) for i in range(2)]
                osb = [psb(f"osb{i}", [128, D], F32) for i in range(2)]
                tmpc = psb("pct1", [128, D], F32)
                sqc = psb("pcsq", [128, D], F32)
                pcy = [ph.enter_context(nc.psum_tensor(f"pcy{i}", [128, 512], F32)) for i in range(2)]

                for tc in range(4):
                    tg = f"c{tc}_"
                    ldln = sem(tg + "ldln", "Cldln")
                    ldg = [sem(tg + "ldg0", "Cldg0"), sem(tg + "ldg1", "Cldg1")]
                    ldp = [sem(tg + "ldp0", "Cldp0"), sem(tg + "ldp1", "Cldp1")]
                    ldx1 = [sem(tg + "ldx10", "Cldx10"), sem(tg + "ldx11", "Cldx11")]
                    tgu = sem(tg + "tgu", "Ctgu")
                    ssg = sem(tg + "ssg", "Cssg")
                    vh = sem(tg + "vh", "Cvh")
                    sh2 = sem(tg + "sh2", "Csh2")
                    vlo = sem(tg + "vlo", "Cvlo")
                    tp1 = sem(tg + "tp1", "Ctp1")
                    vy = sem(tg + "vy", "Cvy")
                    trm = sem(tg + "trm", "Ctrm")
                    vrm = sem(tg + "vrm", "Cvrm")
                    srm = sem(tg + "srm", "Csrm")
                    vmnC = sem(tg + "vmnC", "CvmnC")
                    s2qC = sem(tg + "s2qC", "Cs2qC")
                    vri = sem(tg + "vri", "Cvri")
                    tiv = sem(tg + "tiv", "Ctiv")
                    siv = sem(tg + "siv", "Csiv")
                    tyt = sem(tg + "tyt", "Ctyt")
                    vye = sem(tg + "vye", "Cvye")
                    vo = sem(tg + "vo", "Cvo")
                    sto = sem(tg + "sto", "Csto")

                    @block.sync
                    def _(sync, tc=tc):
                        for s in range(4):
                            srch = bass.AP(ln2Th, (tc * 4 + s) * D * 128,
                                           [[128, 128], [128 * 128, ND], [1, 128]])
                            srcl = bass.AP(ln2Tl, (tc * 4 + s) * D * 128,
                                           [[128, 128], [128 * 128, ND], [1, 128]])
                            sync.dma_start(l2h[:, :, s * 128:(s + 1) * 128], srch).then_inc(ldln.h, 16)
                            ldln.inc(16)
                            sync.dma_start(l2l[:, :, s * 128:(s + 1) * 128], srcl).then_inc(ldln.h, 16)
                            ldln.inc(16)
                        for hg in range(4):
                            for hc in range(16):
                                j = hg * 16 + hc
                                par = j % 2
                                if j >= 2:
                                    sync.wait_ge(tgu.h, tgu.at(j - 1))
                                hcg = hg * 16 + hc
                                srcg = bass.AP(wgT, hcg * 128 * ND * 128,
                                               [[ND * 128, 128], [128, ND], [1, 128]])
                                srcu = bass.AP(wgT, (64 + hcg) * 128 * ND * 128,
                                               [[ND * 128, 128], [128, ND], [1, 128]])
                                sync.dma_start(wgb[par][:], srcg).then_inc(ldg[par].h, 16)
                                sync.dma_start(wub[par][:], srcu).then_inc(ldg[par].h, 16)
                                ldg[par].inc(32)
                            for dd in range(16):
                                k = hg * 16 + dd
                                par = k % 2
                                if k >= 2:
                                    sync.wait_ge(tp1.h, tp1.at(k - 1))
                                srcp = bass.AP(wpT, dd * 128 * 64 * 128 + hg * 16 * 128,
                                               [[64 * 128, 128], [128, 16], [1, 128]])
                                sync.dma_start(wpb[par][:], srcp).then_inc(ldp[par].h, 16)
                                ldp[par].inc(16)
                        for tt in range(4):
                            par = tt % 2
                            if tt >= 2:
                                sync.wait_ge(vo.h, vo.at(tt - 1))
                            t_glob = tc * 4 + tt
                            sync.dma_start(x1t[par][:],
                                           bass.AP(x1S, t_glob * 128 * D, [[D, 128], [1, D]])).then_inc(ldx1[par].h, 16)
                            ldx1[par].inc(16)

                    def vye_expected(gi_done):
                        tt_full, g_in = divmod(gi_done, 4)
                        return tt_full * 5 + g_in

                    @block.tensor
                    def _(tensor, tc=tc):
                        tensor.wait_ge(ldln.h, ldln.at(128))
                        for hg in range(4):
                            for hc in range(16):
                                j = hg * 16 + hc
                                par = j % 2
                                gb = pb[0] if par == 0 else pcy[0]
                                ub = pb[1] if par == 0 else pcy[1]
                                tensor.wait_ge(ldg[par].h, ldg[par].at(32 * (j // 2 + 1)))
                                if j >= 2:
                                    tensor.wait_ge(vh.h, vh.at(j - 1))
                                if j >= 2:
                                    tensor.wait_ge(sh2.h, sh2.at(j - 1))
                                    tensor.matmul(pb[2][0:1, :], onesb[:], h2bb2[(j - 2) % 2][:],
                                                  start=(j == 2), stop=False, skip_group_check=True)
                                for dc in range(ND):
                                    tensor.matmul(gb[:], wgb[par][:, dc, :], l2h[:, dc, :],
                                                  start=(dc == 0), stop=False, skip_group_check=True)
                                    tensor.matmul(gb[:], wgb[par][:, dc, :], l2l[:, dc, :],
                                                  start=False, stop=(dc == ND - 1), skip_group_check=True)
                                for dc in range(ND):
                                    tensor.matmul(ub[:], wub[par][:, dc, :], l2h[:, dc, :],
                                                  start=(dc == 0), stop=False, skip_group_check=True)
                                    mm = tensor.matmul(ub[:], wub[par][:, dc, :], l2l[:, dc, :],
                                                       start=False, stop=(dc == ND - 1), skip_group_check=True)
                                mm.then_inc(tgu.h, 1)
                                tgu.inc()
                            for dd in range(ND):
                                k = hg * 16 + dd
                                par = k % 2
                                tensor.wait_ge(ldp[par].h, ldp[par].at(16 * (k // 2 + 1)))
                                tensor.wait_ge(vlo.h, vlo.at(hg * 16 + 16))
                                if k >= 2:
                                    tensor.wait_ge(vy.h, vy.at(k - 1))
                                for hc in range(16):
                                    tensor.matmul(pb[4 + k % 2][:], wpb[par][:, hc, :], hTh[:, hc, :],
                                                  start=(hc == 0), stop=False, skip_group_check=True)
                                    mm = tensor.matmul(pb[4 + k % 2][:], wpb[par][:, hc, :], hTl[:, hc, :],
                                                       start=False, stop=(hc == 15), skip_group_check=True)
                                mm.then_inc(tp1.h, 1)
                                tp1.inc()
                        tensor.wait_ge(sh2.h, sh2.at(63))
                        tensor.matmul(pb[2][0:1, :], onesb[:], h2bb2[0][:],
                                      start=False, stop=False, skip_group_check=True)
                        tensor.wait_ge(sh2.h, sh2.at(64))
                        tensor.matmul(pb[2][0:1, :], onesb[:], h2bb2[1][:],
                                      start=False, stop=True, skip_group_check=True).then_inc(trm.h, 1)
                        trm.inc()
                        tensor.wait_ge(vri.h, vri.at(1))
                        for ttj in range(4):
                            mm = tensor.transpose(pb[3][:, ttj:ttj + 1], rr[0:1, ttj * 128:(ttj + 1) * 128],
                                                  idf[0:1, 0:1])
                        mm.then_inc(tiv.h, 1)
                        tiv.inc()
                        tensor.wait_ge(vy.h, vy.at(64))
                        for tt in range(4):
                            for g in range(4):
                                gi = 4 * tt + g
                                if gi >= 2:
                                    tensor.wait_ge(vye.h, vye.at(vye_expected(gi - 1)))
                                for j in range(4):
                                    dd = 4 * g + j
                                    mm = tensor.transpose(pcy[g % 2][:, j * 128:(j + 1) * 128],
                                                          yT[:, dd, tt * 128:(tt + 1) * 128], idf[:])
                                mm.then_inc(tyt.h, 1)
                                tyt.inc()

                    @block.scalar
                    def _(scalar, tc=tc):
                        for j in range(64):
                            hg, hc = divmod(j, 16)
                            scalar.wait_ge(tgu.h, tgu.at(j + 1))
                            gbs = pb[0] if j % 2 == 0 else pcy[0]
                            scalar.activation(sgb[:], gbs[:], AF.Sigmoid).then_inc(ssg.h, 1)
                            ssg.inc()
                            scalar.wait_ge(vh.h, vh.at(j + 1))
                            if hg >= 1:
                                scalar.wait_ge(tp1.h, tp1.at(hg * 16))
                            scalar.copy(hTh[:, hc, :], hfb2[:])
                            scalar.activation(h2bb2[j % 2][:], hfb2[:], AF.Square).then_inc(sh2.h, 1)
                            sh2.inc()
                        scalar.wait_ge(vrm.h, vrm.at(1))
                        scalar.activation(rr[0:1, 512:1024], rr[0:1, 0:512], AF.Sqrt).then_inc(srm.h, 1)
                        srm.inc()
                        scalar.wait_ge(tiv.h, tiv.at(1))
                        scalar.copy(invc[:, 0:4], pb[3][:, 0:4]).then_inc(siv.h, 1)
                        siv.inc()
                        for tt in range(4):
                            scalar.wait_ge(vmnC.h, vmnC.at(tt + 1))
                            scalar.activation(sqc[:, 0:1024], ysb[:, 0:1024], AF.Square,
                                              bias=stat[:, 1:2], accum_out=stat[:, 2:3])
                            scalar.activation(sqc[:, 0:1024], ysb[:, 1024:2048], AF.Square,
                                              bias=stat[:, 1:2],
                                              accum_out=stat[:, 10:11]).then_inc(s2qC.h, 1)
                            s2qC.inc()
                            scalar.wait_ge(vo.h, vo.at(tt))
                            scalar.wait_ge(vye.h, vye.at(5 * tt + 5))
                            scalar.activation(stat[:, 3:4], stat[:, 2:3], AF.Sqrt,
                                              bias=stat[:, 8:9], scale=1.0 / D).then_inc(srm.h, 1)
                            srm.inc()

                    @block.vector
                    def _(vector, tc=tc):
                        for hg in range(4):
                            for hc in range(16):
                                j = hg * 16 + hc
                                vector.wait_ge(ssg.h, ssg.at(j + 1))
                                gbv = pb[0] if j % 2 == 0 else pcy[0]
                                ubv = pb[1] if j % 2 == 0 else pcy[1]
                                vector.tensor_tensor(hfb[:], sgb[:], gbv[:], op=ALU.mult)
                                vector.tensor_tensor(hfb2[:], hfb[:], ubv[:], op=ALU.mult).then_inc(vh.h, 1)
                                vh.inc()
                                vector.wait_ge(sh2.h, sh2.at(j + 1))
                                if hg >= 1:
                                    vector.wait_ge(tp1.h, tp1.at(hg * 16))
                                vector.scalar_tensor_tensor(hTl[:, hc, :], hfb2[:], 1.0, hTh[:, hc, :],
                                                            op0=ALU.mult, op1=ALU.subtract).then_inc(vlo.h, 1)
                                vlo.inc()
                            for dd in range(ND):
                                k = hg * 16 + dd
                                vector.wait_ge(tp1.h, tp1.at(k + 1))
                                if hg == 0:
                                    vector.tensor_copy(yT[:, dd, :], pb[4 + k % 2][:]).then_inc(vy.h, 1)
                                else:
                                    vector.tensor_tensor(yT[:, dd, :], yT[:, dd, :],
                                                         pb[4 + k % 2][:], op=ALU.add).then_inc(vy.h, 1)
                                vy.inc()
                        vector.wait_ge(trm.h, trm.at(1))
                        vector.tensor_scalar(rr[0:1, 0:512], pb[2][0:1, :], 1.0 / HFF, EPS_RMS,
                                             op0=ALU.mult, op1=ALU.add).then_inc(vrm.h, 1)
                        vrm.inc()
                        vector.wait_ge(srm.h, srm.at(1))
                        vector.reciprocal(rr[0:1, 0:512], rr[0:1, 512:1024])
                        vector.drain()
                        vector.drain().then_inc(vri.h, 1)
                        vri.inc()
                        vector.wait_ge(siv.h, siv.at(1))
                        for tt in range(4):
                            par = tt % 2
                            for g in range(4):
                                gi = 4 * tt + g
                                vector.wait_ge(tyt.h, tyt.at(gi + 1))
                                vector.tensor_scalar(ysb[:, g * 512:(g + 1) * 512], pcy[g % 2][:],
                                                     invc[:, tt:tt + 1], None,
                                                     op0=ALU.mult).then_inc(vye.h, 1)
                                vye.inc()
                            vector.tensor_reduce(stat[:, 0:1], ysb[:], axis=AX.X, op=ALU.add)
                            vector.drain()
                            vector.tensor_scalar(stat[:, 1:2], stat[:, 0:1], -1.0 / D, None, op0=ALU.mult)
                            vector.drain()
                            vector.drain().then_inc(vmnC.h, 1)
                            vmnC.inc()
                            vector.tensor_scalar(tmpc[:], ysb[:], stat[:, 1:2], None, op0=ALU.add)
                            vector.wait_ge(s2qC.h, s2qC.at(tt + 1))
                            vector.tensor_tensor(stat[:, 2:3], stat[:, 2:3], stat[:, 10:11], op=ALU.add)
                            vector.drain()
                            vector.drain().then_inc(vye.h, 1)
                            vye.inc()
                            vector.wait_ge(srm.h, srm.at(tt + 2))
                            vector.reciprocal(stat[:, 4:5], stat[:, 3:4])
                            vector.drain()
                            vector.tensor_scalar(tmpc[:], tmpc[:], stat[:, 4:5], None, op0=ALU.mult)
                            vector.wait_ge(ldx1[par].h, ldx1[par].at(16 * (tt // 2 + 1)))
                            if tt >= 2:
                                vector.wait_ge(sto.h, sto.at(16 * (tt - 1)))
                            vector.tensor_tensor(osb[par][:], x1t[par][:], tmpc[:],
                                                 op=ALU.add).then_inc(vo.h, 1)
                            vo.inc()

                    @block.gpsimd
                    def _(gpsimd, tc=tc):
                        for tt in range(4):
                            par = tt % 2
                            gpsimd.wait_ge(vo.h, vo.at(tt + 1))
                            t_glob = tc * 4 + tt
                            gpsimd.dma_start(out[t_glob * 128:(t_glob + 1) * 128, :],
                                             osb[par][:]).then_inc(sto.h, 16)
                            sto.inc(16)
                        gpsimd.wait_ge(sto.h, sto.n)
                    barrier()

    return nc


_ROPE_THETA = 10000.0


def _host_layouts(inputs):
    x = np.asarray(inputs["L0_attn_normalized_resid_pre"], dtype=np.float32)
    WQ = np.asarray(inputs["L0_attn_W_Q"], dtype=np.float32)
    WK = np.asarray(inputs["L0_attn_W_K"], dtype=np.float32)
    WV = np.asarray(inputs["L0_attn_W_V"], dtype=np.float32)
    WO = np.asarray(inputs["L0_attn_W_O"], dtype=np.float32)
    bQ = np.asarray(inputs["L0_attn_b_Q"], dtype=np.float32)
    bK = np.asarray(inputs["L0_attn_b_K"], dtype=np.float32)
    bV = np.asarray(inputs["L0_attn_b_V"], dtype=np.float32)
    bO = np.asarray(inputs["L0_attn_b_O"], dtype=np.float32)
    wgu = np.ascontiguousarray(np.asarray(inputs["L0_ffn_w_gate_up"], dtype=np.float32))
    wpr = np.ascontiguousarray(np.asarray(inputs["L0_ffn_w_proj"], dtype=np.float32))

    perm = np.concatenate([np.arange(0, DH, 2), np.arange(1, DH, 2)])
    wqp = np.ascontiguousarray(WQ[:, :, perm].transpose(1, 0, 2).reshape(D, D))
    wkp = np.ascontiguousarray(WK[:, :, perm].transpose(1, 0, 2).reshape(D, D))
    wvp = np.ascontiguousarray(WV.transpose(1, 0, 2).reshape(D, D))
    wo = np.ascontiguousarray(WO.reshape(D, D))
    bq = np.ascontiguousarray(np.broadcast_to(bQ[:, perm].reshape(1, D), (128, D)))
    bk = np.ascontiguousarray(np.broadcast_to(bK[:, perm].reshape(1, D), (128, D)))
    bvt = np.ascontiguousarray(np.broadcast_to(bV.reshape(1, D), (128, D)))
    bot = np.ascontiguousarray(np.broadcast_to(bO.reshape(1, D), (128, D)))

    inv_freq = (1.0 / _ROPE_THETA ** (np.arange(0, DH, 2, dtype=np.float32) / DH)).astype(np.float32)
    m = np.arange(S, dtype=np.float32)
    freqs = np.outer(m, inv_freq).astype(np.float32)
    cosf = np.cos(freqs).astype(np.float32)
    sinf = np.sin(freqs).astype(np.float32)
    tab1 = np.ascontiguousarray(cosf.reshape(NT, 128, 64).transpose(1, 0, 2).reshape(128, NT * 64))
    tab2 = np.ascontiguousarray(sinf.reshape(NT, 128, 64).transpose(1, 0, 2).reshape(128, NT * 64))

    identf = np.eye(128, dtype=np.float32)
    identb = np.eye(128, dtype=ml_dtypes.bfloat16)
    dmask = np.zeros((128, 640), dtype=np.float32)
    dmask[:, 512:640] = np.triu(np.full((128, 128), -1.0e30, dtype=np.float32), 1)
    onesb = np.ones((128, 1), dtype=ml_dtypes.bfloat16)

    const = dict(wqp=wqp, wkp=wkp, wvp=wvp, wo=wo, wgu=wgu, wpr=wpr,
                 bq=bq, bk=bk, bvt=bvt, bot=bot, tab1=tab1, tab2=tab2,
                 identf=identf, identb=identb, dmask=dmask, onesb=onesb)
    return x, const


def kernel(**inputs):
    x, const = _host_layouts(inputs)
    nc = build_program()
    in_maps = [dict(const, x=np.ascontiguousarray(x[b])) for b in range(8)]
    res = run_bass_kernel_spmd(nc, in_maps, core_ids=list(range(8)))
    return np.stack([np.asarray(r["out"]) for r in res.results], axis=0)
